# version 16
# baseline (speedup 1.0000x reference)
import numpy as np
import ml_dtypes
from contextlib import ExitStack

import concourse.bass as bass
import concourse.mybir as mybir
from concourse.bass_utils import run_bass_kernel_spmd

F32 = mybir.dt.float32
BF16 = mybir.dt.bfloat16
I32 = mybir.dt.int32
AF = mybir.ActivationFunctionType
ALU = mybir.AluOpType
AX = mybir.AxisListType
BF = ml_dtypes.bfloat16

NCORES = 8
D = 1024
C2 = 2048
H = 4096
NST = 16
RK = 64
S = 8192
B = 2
NT = B * S
TPC = NT // NCORES
EPS = 1e-5

ENGS = ("sync", "scalar", "vector", "tensor", "gpsimd")


def f32(x):
    return np.ascontiguousarray(np.asarray(x, dtype=np.float32))


def _key(x):
    if hasattr(x, "name"):
        return x.name
    if hasattr(x, "tensor"):
        return x.tensor.name
    return str(x)


class LB:

    def __init__(self):
        self.nc = bass.Bass("TRN2", target_bir_lowering=False)
        self.es = ExitStack()
        self.stacks = [self.es]
        self.ops = {e: [] for e in ENGS}
        self.cnt = {e: 0 for e in ENGS}
        self.last_writer = {}
        self.readers = {}
        self.dma_cnt = {}
        self.stage_idx = 0
        self.active_streams = set()
        self.stream_slots = {}
        self.free_slots = []
        self.max_slot = 0
        self.fence = None
        self._gpos = {}
        self._slot_sem = {}
        self.slot_cnt = {}

    def inp(self, name, shape, dtype=F32):
        return self.nc.dram_tensor(name, list(shape), dtype, kind="ExternalInput").ap()

    def out(self, name, shape, dtype=F32):
        return self.nc.dram_tensor(name, list(shape), dtype, kind="ExternalOutput").ap()

    def dram(self, name, shape, dtype=F32):
        return self.nc.dram_tensor(f"{name}_u{uid()}", list(shape), dtype,
                                   kind="Internal").ap()

    def sb(self, name, shape, dtype=F32):
        return self.stacks[-1].enter_context(
            self.nc.sbuf_tensor(f"{name}_u{uid()}", list(shape), dtype))

    def ps(self, name, shape, dtype=F32):
        return self.stacks[-1].enter_context(
            self.nc.psum_tensor(f"{name}_u{uid()}", list(shape), dtype))

    def gpos(self, E, div=4):
        k = (str(getattr(E, "engine", E)), div)
        if k not in self._gpos:
            self._gpos[k] = E.partition_id() % div
        return self._gpos[k]

    def dynmul(self, E, mul):
        k = ("dm", str(getattr(E, "engine", E)), mul)
        if k not in self._gpos:
            self._gpos[k] = self.gpos(E) * mul
        return self._gpos[k]

    def _stream(self, key):
        if key not in self.dma_cnt:
            self.active_streams.add(key)
            slot = self.free_slots.pop() if self.free_slots else self.max_slot
            self.max_slot = max(self.max_slot, slot + 1)
            self.stream_slots[key] = slot
            self.dma_cnt[key] = self.slot_cnt.get(slot, 0)
        return key

    def op(self, eng, fn, r=(), w=(), dma=False, dma_stream=None, small=False):
        if dma:
            inc_amt = 1 if dma == "coll" else 16
            skey = (f"dma{self.stage_idx}:"
                    + (dma_stream if dma_stream is not None else _key(list(w)[0])))
            self._stream(skey)
            token = (skey, self.dma_cnt[skey] + inc_amt)
        else:
            inc_amt = 1
            token = (eng, self.cnt[eng] + 1)
        waits = {}

        def need(tok, same_ok=False):
            if tok is None:
                return
            s, v = tok
            if same_ok and s == eng:
                return
            if v > 0:
                waits[s] = max(waits.get(s, 0), v)

        def _small(x):
            try:
                sh = x.shape
                return (len(sh) < 2) or (int(np.prod(sh[1:])) < 64)
            except Exception:
                return True

        if self.fence is not None:
            need(self.fence)

        for x in r:
            k = _key(x)
            need(self.last_writer.get(k), same_ok=(not dma) and not (small or _small(x)))
        for x in w:
            k = _key(x)
            lw = self.last_writer.get(k)
            if lw is not None:
                if not (dma and lw[0] == token[0]):
                    need(lw, same_ok=(not dma) and not (small or _small(x)))
            for tok in self.readers.get(k, {}).values():
                need(tok, same_ok=(not dma) and not (small or _small(x)))
        for x in r:
            self.readers.setdefault(_key(x), {})[token[0]] = token
        for x in w:
            k = _key(x)
            self.last_writer[k] = token
            self.readers[k] = {}
        self.ops[eng].append((tuple(waits.items()), fn, token, inc_amt))
        if dma:
            self.dma_cnt[token[0]] += inc_amt
        else:
            self.cnt[eng] += 1

    def dma(self, eng, dst, src, dma_stream=None, **kw):
        self.op(eng, lambda E: E.dma_start(out=dst, in_=src, **kw),
                r=[src], w=[dst], dma=True, dma_stream=dma_stream)

    def cvec(self, val, dtype=F32):
        key = (float(val), dtype)
        if not hasattr(self, "_consts"):
            self._consts = {}
        if key not in self._consts:
            t = self.sb(f"cvec{len(self._consts)}_{uid()}", [128, 1], dtype)
            self.op("vector", lambda E: E.memset(t[:], val), w=[t])
            self._consts[key] = t
        return self._consts[key]

    def barrier_op(self, eng, fn):
        waits = [(e, self.cnt[e]) for e in ENGS if e != eng and self.cnt[e] > 0]
        waits += [(s, self.dma_cnt[s]) for s in self.active_streams
                  if self.dma_cnt[s] > 0]
        self.ops[eng].append((tuple(waits), fn, (eng, self.cnt[eng] + 1), 1))
        self.cnt[eng] += 1

    def stage_open(self):
        self.stacks.append(ExitStack())

    def stage_close(self):
        for e in ENGS:
            self.barrier_op(e, lambda E: E.nop())
        for skey in sorted(self.active_streams):
            slot = self.stream_slots[skey]
            self.slot_cnt[slot] = self.dma_cnt[skey]
            self.free_slots.append(slot)
        self.active_streams.clear()
        self.fence = ("sync", self.cnt["sync"])
        for k in list(self.last_writer):
            self.last_writer[k] = self.fence
        self.readers.clear()
        self._consts = {}
        st = self.stacks.pop()
        st.close()
        self.stage_idx += 1
        self.stage_open()

    def finalize(self):
        nc = self.nc
        sems = {}
        for e in ENGS:
            if self.ops[e] or any(s == e for ops in self.ops.values()
                                  for rec in ops for (s, _v) in rec[0]):
                sems[e] = self.es.enter_context(nc.semaphore(f"sem_{e}"))
        for slot in range(self.max_slot):
            self._slot_sem[slot] = self.es.enter_context(nc.semaphore(f"sd{slot}"))
        for skey, slot in self.stream_slots.items():
            sems[skey] = self._slot_sem[slot]
        crosswaits = {e: set() for e in ENGS}
        for eng in ENGS:
            for (ws, fn, tok, inc_) in self.ops[eng]:
                for (s, v) in ws:
                    if s in ENGS and s != eng:
                        crosswaits[s].add(v)
        block = self.es.enter_context(nc.Block())
        with self.es:
            for eng in ENGS:
                if not self.ops[eng]:
                    continue

                def make(eng=eng, opslist=self.ops[eng]):
                    def body(E):
                        watermark = {}
                        cw = crosswaits[eng]
                        drainable = eng in ("vector", "scalar", "gpsimd")
                        for (ws, fn, tok, inc) in opslist:
                            for (s, v) in ws:
                                if watermark.get(s, -1) >= v:
                                    continue
                                E.wait_ge(sems[s], v)
                                watermark[s] = v
                            sem_s, val = tok
                            inst = fn(E)
                            inst.then_inc(sems[sem_s], inc)
                    return body

                getattr(block, eng)(make())
        return nc




class IO:

    def __init__(self, lb, provided=None):
        self.lb = lb
        self.p = provided
        self._auto = {}

    def inp(self, name, shape, dtype=F32):
        if self.p is None:
            ap = self.lb.inp(name, shape, dtype)
            self._auto[name] = ap
            return ap
        return self.p[name]

    def out(self, name, shape, dtype=F32):
        if self.p is None:
            ap = self.lb.out(name, shape, dtype)
            self._auto[name] = ap
            return ap
        return self.p[name]

    def load(self, name, dst, r0, r1, t0, t1, eng="sync", src=None, **kw):
        lb = self.lb
        if src is None:
            src = self._auto[name] if self.p is None else self.p[name]
        if not isinstance(src, tuple):
            lb.dma(eng, dst, src[r0:r1, t0:t1], **kw)
            return
        kind = src[0]
        if kind == "gath":
            if len(src) == 5:
                _, ap, RR, gmul, CT = src
                roff = 0
            else:
                _, ap, RR, gmul, CT, roff = src
            r0, r1 = r0 + roff, r1 + roff
            r = t0 // CT
            while r * CT < t1:
                a, b = max(t0, r * CT), min(t1, (r + 1) * CT)
                dsl = dst if (a, b) == (t0, t1) else dst[:, (a - t0):(b - t0)]

                def mk(E, ap=ap, dsl=dsl, a=a, b=b, r=r, kw=kw, RR=RR,
                       gmul=gmul, r0=r0, r1=r1, CT=CT):
                    if gmul:
                        base = ap[r * RR + r0:r * RR + r0 + 3 * gmul + (r1 - r0)]
                        s = base[bass.DynSlice(self.lb.dynmul(E, gmul), r1 - r0),
                                 a - r * CT:b - r * CT]
                    else:
                        s = ap[r * RR + r0:r * RR + r1, a - r * CT:b - r * CT]
                    return E.dma_start(out=dsl, in_=s, **kw)
                lb.op(eng, mk, r=[ap], w=[dst], dma=True, dma_stream=_key(dst))
                r += 1
        elif kind == "gcols":
            _, ap, CT = src
            assert t0 == 0 and t1 == CT

            def mk(E, ap=ap, r0=r0, r1=r1, CT=CT, kw=kw):
                return E.dma_start(
                    out=dst,
                    in_=ap[r0:r1, bass.DynSlice(self.lb.dynmul(E, CT), CT)], **kw)
            lb.op(eng, mk, r=[ap], w=[dst], dma=True, dma_stream=_key(dst))
        elif kind == "ycols":
            _, ap, SEQ, OWN, NBLK, BC = src
            for blk in range(NBLK):
                def mk(E, ap=ap, blk=blk, r0=r0, r1=r1, kw=kw,
                       SEQ=SEQ, OWN=OWN, BC=BC):
                    base = ap[blk * SEQ + r0:blk * SEQ + r0 + 3 * OWN + (r1 - r0)]
                    s = base[bass.DynSlice(self.lb.dynmul(E, OWN), r1 - r0), :]
                    return E.dma_start(out=dst[:, blk * BC:(blk + 1) * BC],
                                       in_=s, **kw)
                lb.op(eng, mk, r=[ap], w=[dst], dma=True, dma_stream=_key(dst))
        else:
            raise KeyError(kind)


def chunks(total, size):
    out = []
    o = 0
    while o < total:
        c = min(size, total - o)
        out.append((o, c))
        o += c
    return out



_UID = [0]


def uid():
    _UID[0] += 1
    return _UID[0]


def emit_ln_tm(lb, xt, ot, sq, nf=D):
    u = uid()
    m = lb.sb(f"lnm{u}", [128, 1], F32)
    v = lb.sb(f"lnv{u}", [128, 1], F32)
    rr = lb.sb(f"lnr{u}", [128, 1], F32)
    lb.op("vector", lambda E: E.tensor_reduce(m[:], xt[:], AX.X, ALU.add), r=[xt], w=[m])
    lb.op("vector", lambda E: E.tensor_scalar(m[:], m[:], 1.0 / nf, None, ALU.mult), r=[m], w=[m])
    lb.op("scalar", lambda E: E.activation(sq[:], xt[:], AF.Square), r=[xt], w=[sq])
    lb.op("vector", lambda E: E.tensor_reduce(v[:], sq[:], AX.X, ALU.add), r=[sq], w=[v])
    lb.op("vector", lambda E: E.tensor_scalar(v[:], v[:], 1.0 / nf, None, ALU.mult), r=[v], w=[v])
    lb.op("scalar", lambda E: E.activation(rr[:], m[:], AF.Square), r=[m], w=[rr])
    lb.op("vector", lambda E: E.tensor_sub(v[:], v[:], rr[:]), r=[v, rr], w=[v])
    epsv = lb.cvec(EPS)
    lb.op("scalar", lambda E: E.activation(rr[:], v[:], AF.Sqrt, bias=epsv[:]), r=[v, epsv], w=[rr])
    lb.op("vector", lambda E: E.reciprocal(rr[:], rr[:]), r=[rr], w=[rr])
    lb.op("vector", lambda E: E.tensor_scalar(ot[:], xt[:], m[:], None, ALU.subtract),
          r=[xt, m], w=[ot])
    lb.op("vector", lambda E: E.tensor_scalar(ot[:], ot[:], rr[:], None, ALU.mult),
          r=[ot, rr], w=[ot])
    return m, rr



def emit_ln1(lb, io, window=False):
    ntt = 17 if window else 16
    NR = ntt * 128
    x = io.inp("x_rows", [NR, D], F32)
    idf = io.inp("ident_f32", [128, 128], F32)
    idb = io.inp("ident_bf16", [128, 128], BF16)
    o_xattn = io.out("xattn_fm", [D, NR if window else TPC], F32)
    o_xnq = io.out("xnq_fm", [D, NR], BF16)
    o_qinv = io.out("qinv", [1, NR], F32)
    sid_f = lb.sb("sid_f", [128, 128], F32)
    sid_b = lb.sb("sid_b", [128, 128], BF16)
    lb.dma("sync", sid_f[:], idf)
    lb.dma("sync", sid_b[:], idb)

    xt = [lb.sb(f"xt{i}", [128, D], F32) for i in range(ntt)]
    xq = [lb.sb(f"xq{i}", [128, D], BF16) for i in range(ntt)]
    sq = [lb.sb(f"sq{i}", [128, D], F32) for i in range(2)]
    tmp_i = [lb.sb(f"tmpi{i}", [128, D], I32) for i in range(2)]
    qinv_t = lb.sb("qinv_t", [128, ntt], F32)

    for i in range(ntt):
        lb.dma("sync", xt[i][:], x[i * 128:(i + 1) * 128, :])

    for i in range(ntt):
        s = sq[i % 2]
        emit_ln_tm(lb, xt[i], xt[i], s)
        am = lb.sb(f"am{i}", [128, 1], F32)
        sc = lb.sb(f"sc{i}", [128, 1], F32)
        ti = tmp_i[i % 2]
        lb.op("vector", lambda E, i=i, am=am: E.tensor_reduce(am[:], xt[i][:], AX.X, ALU.max,
                                                              apply_absolute_value=True),
              r=[xt[i]], w=[am])
        lb.op("vector", lambda E, am=am: E.tensor_scalar(am[:], am[:], 1e-5, None, ALU.max),
              r=[am], w=[am])
        lb.op("vector", lambda E, i=i, am=am: E.tensor_scalar(qinv_t[:, i:i + 1], am[:],
                                                              1.0 / 127.0, None, ALU.mult),
              r=[am], w=[qinv_t])
        lb.op("vector", lambda E, am=am, sc=sc: E.reciprocal(sc[:], am[:]), r=[am], w=[sc])
        lb.op("vector", lambda E, sc=sc: E.tensor_scalar(sc[:], sc[:], 127.0, None, ALU.mult),
              r=[sc], w=[sc])
        lb.op("vector", lambda E, i=i, sc=sc, s=s: E.tensor_scalar(s[:], xt[i][:], sc[:], None, ALU.mult),
              r=[xt[i], sc], w=[s])
        lb.op("vector", lambda E, s=s, ti=ti: E.tensor_copy(ti[:], s[:]), r=[s], w=[ti])
        lb.op("vector", lambda E, i=i, ti=ti: E.tensor_copy(xq[i][:], ti[:]), r=[ti], w=[xq[i]])

    ps_f = [lb.ps(f"ps_f{k}", [128, 128], F32) for k in range(2)]
    ps_b = [lb.ps(f"ps_b{k}", [128, 128], BF16) for k in range(2)]
    st_f = [lb.sb(f"st_f{k}", [128, 128], F32) for k in range(4)]
    st_b = [lb.sb(f"st_b{k}", [128, 128], BF16) for k in range(4)]
    k = 0
    for i in range(ntt):
        for j in range(D // 128):
            pf, sf = ps_f[k % 2], st_f[k % 4]
            pb, sb_ = ps_b[k % 2], st_b[k % 4]
            lb.op("tensor", lambda E, i=i, j=j, pf=pf: E.transpose(pf[:], xt[i][:, j * 128:(j + 1) * 128], sid_f[:]),
                  r=[xt[i], sid_f], w=[pf])
            lb.op("vector", lambda E, pf=pf, sf=sf: E.tensor_copy(sf[:], pf[:]), r=[pf], w=[sf])
            lb.dma("gpsimd", o_xattn[j * 128:(j + 1) * 128, i * 128:(i + 1) * 128], sf[:])
            lb.op("tensor", lambda E, i=i, j=j, pb=pb: E.transpose(pb[:], xq[i][:, j * 128:(j + 1) * 128], sid_b[:]),
                  r=[xq[i], sid_b], w=[pb])
            lb.op("scalar", lambda E, pb=pb, sb_=sb_: E.copy(sb_[:], pb[:]), r=[pb], w=[sb_])
            lb.dma("gpsimd", o_xnq[j * 128:(j + 1) * 128, i * 128:(i + 1) * 128], sb_[:])
            k += 1

    lb.dma("gpsimd", o_qinv.rearrange("o (i p) -> o p i", p=128).squeeze(0), qinv_t[:],
           allow_slow_non_contiguous=True)


def build_ln1():
    lb = LB()
    io = IO(lb)
    emit_ln1(lb, io, window=False)
    lb.barrier_op("sync", lambda E: E.nop())
    return lb.finalize()



def emit_fm_ln(lb, xat, out_tiles, out_dtype, nw=None, nb=None, tlen=TPC, gb_tile=None):
    u = uid()
    nc = lb.nc
    nft = len(xat)
    nf = nft * 128
    ones_b = lb.cvec(1.0, BF16)
    ps_s = lb.ps(f"lnps_s{u}", [1, 512], F32)
    ps_q = lb.ps(f"lnps_q{u}", [1, 512], F32)
    xb = lb.sb(f"lnxb{u}", [128, 512], BF16)
    sqb = lb.sb(f"lnsq{u}", [128, 512], BF16)
    st1 = lb.sb(f"lnst1{u}", [1, 512], F32)
    st2 = lb.sb(f"lnst2{u}", [1, 512], F32)
    scr = lb.dram(f"lnscr{u}", [2, tlen], F32)
    mbc = lb.sb(f"lnmbc{u}", [128, 512], F32)
    rbc = lb.sb(f"lnrbc{u}", [128, 512], F32)
    tmps = [lb.sb(f"lnap{u}_{k}", [128, 512], F32) for k in range(2)]
    epsv1 = lb.cvec(EPS)
    for (o, w) in chunks(tlen, 512):
        sl = slice(o, o + w)
        for j in range(nft):
            lb.op("vector", lambda E, j=j, sl=sl, w=w: E.tensor_copy(xb[:, :w], xat[j][:, sl]),
                  r=[xat[j]], w=[xb])
            lb.op("scalar", lambda E, j=j, sl=sl, w=w: E.activation(sqb[:, :w], xat[j][:, sl], AF.Square),
                  r=[xat[j]], w=[sqb])
            lb.op("tensor", lambda E, j=j, sl=sl, w=w: E.matmul(ps_s[:, :w], ones_b[:], xb[:, :w],
                                                    start=(j == 0), stop=(j == nft - 1)),
                  r=[ones_b, xb] + ([ps_s] if j else []), w=[ps_s])
            lb.op("tensor", lambda E, j=j, sl=sl, w=w: E.matmul(ps_q[:, :w], ones_b[:], sqb[:, :w],
                                                    start=(j == 0), stop=(j == nft - 1)),
                  r=[ones_b, sqb] + ([ps_q] if j else []), w=[ps_q])
        lb.op("vector", lambda E, sl=sl, w=w: E.tensor_scalar(st1[:, :w], ps_s[:, :w], 1.0 / nf, None, ALU.mult),
              r=[ps_s], w=[st1])
        lb.op("vector", lambda E, sl=sl, w=w: E.tensor_scalar(st2[:, :w], ps_q[:, :w], 1.0 / nf, None, ALU.mult),
              r=[ps_q], w=[st2])
        lb.op("scalar", lambda E, sl=sl, w=w: E.activation(sqb[:1, :w], st1[:, :w], AF.Square),
              r=[st1], w=[sqb])
        lb.op("vector", lambda E, sl=sl, w=w: E.tensor_sub(st2[:, :w], st2[:, :w], sqb[:1, :w]),
              r=[st2, sqb], w=[st2])
        lb.op("scalar", lambda E, sl=sl, w=w: E.activation(st2[:, :w], st2[:, :w], AF.Sqrt, bias=epsv1[:1]),
              r=[st2, epsv1], w=[st2])
        lb.op("vector", lambda E, sl=sl, w=w: E.reciprocal(st2[:, :w], st2[:, :w]), r=[st2], w=[st2])
        lb.dma("scalar", scr[0:1, sl], st1[:, :w])
        lb.dma("scalar", scr[1:2, sl], st2[:, :w])
        lb.dma("scalar", mbc[:, :w], scr[0:1, sl].broadcast_to([128, w]))
        lb.dma("scalar", rbc[:, :w], scr[1:2, sl].broadcast_to([128, w]))
        for j in range(nft):
            tmpa = tmps[j % 2]
            lb.op("vector", lambda E, j=j, tmpa=tmpa, sl=sl, w=w: E.tensor_tensor(tmpa[:, :w], xat[j][:, sl], mbc[:, :w], ALU.subtract),
                  r=[xat[j], mbc], w=[tmpa])
            lb.op("vector", lambda E, j=j, tmpa=tmpa, sl=sl, w=w: E.tensor_tensor(tmpa[:, :w], tmpa[:, :w], rbc[:, :w], ALU.mult),
                  r=[tmpa, rbc], w=[tmpa])
            if gb_tile is not None:
                lb.op("vector", lambda E, j=j, tmpa=tmpa, sl=sl, w=w:
                      E.tensor_scalar(out_tiles[j][:, sl], tmpa[:, :w],
                                      gb_tile[:, j:j + 1], gb_tile[:, nft + j:nft + j + 1],
                                      ALU.mult, ALU.add),
                      r=[tmpa, gb_tile], w=[out_tiles[j]])
            else:
                lb.op("vector", lambda E, j=j, tmpa=tmpa, sl=sl, w=w: E.tensor_copy(out_tiles[j][:, sl], tmpa[:, :w]),
                      r=[tmpa], w=[out_tiles[j]])


def load_vec_to_cols(lb, dst_tile, src_vec, n, ncols):
    lb.dma("sync", dst_tile[:, 0:ncols],
           src_vec.rearrange("(j p) -> p j", p=128), allow_slow_non_contiguous=True)



def emit_proj(lb, io, has_outproj, has_inproj=True, final_ln=False):
    xattn_in = io.inp("xattn", [D, TPC], F32)
    nft = D // 128
    xat = [lb.sb(f"xat{j}", [128, TPC], F32) for j in range(nft)]
    for j in range(nft):
        lb.dma("sync", xat[j][:], xattn_in[j * 128:(j + 1) * 128, :])

    if has_outproj:
        io.inp("y_prev", [C2, TPC], BF16)
        io.inp("sz_prev", [C2, TPC], BF16)
        ow_in = io.inp("ow", [C2, D], F32)
        o_xattn = io.out("xattn_new", [D, TPC], F32)
        nkt = C2 // 128
        mt = [lb.sb(f"mt{k}", [128, TPC], BF16) for k in range(nkt)]
        szt = lb.sb("szt", [128, TPC], BF16)
        for k in range(nkt):
            io.load("y_prev", mt[k][:], k * 128, (k + 1) * 128, 0, TPC)
            io.load("sz_prev", szt[:], k * 128, (k + 1) * 128, 0, TPC)
            lb.op("vector", lambda E, k=k: E.tensor_mul(mt[k][:], mt[k][:], szt[:]),
                  r=[mt[k], szt], w=[mt[k]])
        ow_f = lb.sb("ow_f", [128, nkt * 128], F32)
        ow_b = lb.sb("ow_b", [128, nkt * 128], BF16)
        psB = [lb.ps(f"psB{t}", [128, 512], F32) for t in range(4)]
        for j in range(nft):
            lb.dma("sync", ow_f[:], ow_in[:, j * 128:(j + 1) * 128].rearrange("(k p) c -> p k c", p=128))
            lb.op("vector", lambda E: E.tensor_copy(ow_b[:], ow_f[:]), r=[ow_f], w=[ow_b])
            for t in range(4):
                for k in range(nkt):
                    lb.op("tensor", lambda E, j=j, t=t, k=k:
                          E.matmul(psB[t][:], ow_b[:, k * 128:(k + 1) * 128],
                                   mt[k][:, t * 512:(t + 1) * 512],
                                   start=(k == 0), stop=(k == nkt - 1)),
                          r=[ow_b, mt[k]] + ([psB[t]] if k else []), w=[psB[t]])
            for t in range(4):
                lb.op("vector", lambda E, j=j, t=t:
                      E.tensor_tensor(xat[j][:, t * 512:(t + 1) * 512],
                                      xat[j][:, t * 512:(t + 1) * 512], psB[t][:], ALU.add),
                      r=[xat[j], psB[t]], w=[xat[j]])
            lb.dma("gpsimd", o_xattn[j * 128:(j + 1) * 128, :], xat[j][:])

    nw_in = io.inp("norm_w", [D], F32)
    nb_in = io.inp("norm_b", [D], F32)
    gb = lb.sb("gb", [128, 2 * nft], F32)
    load_vec_to_cols(lb, gb, nw_in, D, nft)
    lb.dma("sync", gb[:, nft:2 * nft], nb_in.rearrange("(j p) -> p j", p=128),
           allow_slow_non_contiguous=True)

    if final_ln:
        idf = io.inp("ident_f32", [128, 128], F32)
        sid_f = lb.sb("sid_f", [128, 128], F32)
        lb.dma("sync", sid_f[:], idf)
        o_attn = io.out("attn_tm", [TPC, D], F32)
        at = xat
        emit_fm_ln(lb, xat, at, F32, gb_tile=gb)
        ps_t = [lb.ps(f"pstr{k}", [128, 128], F32) for k in range(2)]
        st_t = [lb.sb(f"sttr{k}", [128, 128], F32) for k in range(4)]
        kk = 0
        for j in range(nft):
            for i in range(TPC // 128):
                p, s = ps_t[kk % 2], st_t[kk % 4]
                lb.op("tensor", lambda E, j=j, i=i, p=p:
                      E.transpose(p[:], at[j][:, i * 128:(i + 1) * 128], sid_f[:]),
                      r=[at[j], sid_f], w=[p])
                lb.op("vector", lambda E, p=p, s=s: E.tensor_copy(s[:], p[:]), r=[p], w=[s])
                lb.dma("gpsimd", o_attn[i * 128:(i + 1) * 128, j * 128:(j + 1) * 128], s[:])
                kk += 1
        return

    xn_blk = [lb.sb(f"xnb{j}", [128, TPC], BF16) for j in range(nft)]
    emit_fm_ln(lb, xat, xn_blk, BF16, gb_tile=gb)

    if has_inproj:
        iw_in = io.inp("in_w", [D, 2 * C2], F32)
        o_u = io.out("u_cols", [C2, TPC], BF16)
        o_z = io.out("siluz_cols", [C2, TPC], BF16)
        w_f = [lb.sb(f"ipwf{k}", [128, nft * 128], F32) for k in range(2)]
        w_b = [lb.sb(f"ipwb{k}", [128, nft * 128], BF16) for k in range(2)]
        psA = psB if has_outproj else [lb.ps(f"psA{t}", [128, 512], F32) for t in range(4)]
        stg = [lb.sb(f"ipst{t}", [128, 512], BF16) for t in range(4)]
        nct = 2 * C2 // 128
        for c in range(nct):
            wf, wb = w_f[c % 2], w_b[c % 2]
            lb.dma("sync", wf[:], iw_in[:, c * 128:(c + 1) * 128].rearrange("(j p) c -> p j c", p=128))
            lb.op("vector", lambda E, wf=wf, wb=wb: E.tensor_copy(wb[:], wf[:]), r=[wf], w=[wb])
            for k in range(nft):
                for t in range(4):
                    lb.op("tensor", lambda E, c=c, k=k, t=t, wb=wb:
                          E.matmul(psA[t][:], wb[:, k * 128:(k + 1) * 128],
                                   xn_blk[k][:, t * 512:(t + 1) * 512],
                                   start=(k == 0), stop=(k == nft - 1)),
                          r=[wb, xn_blk[k]] + ([psA[t]] if k else []), w=[psA[t]])
            for t in range(4):
                s = stg[t]
                if c < nct // 2:
                    lb.op("vector", lambda E, t=t, s=s: E.tensor_copy(s[:], psA[t][:]),
                          r=[psA[t]], w=[s])
                    lb.dma("gpsimd", o_u[c * 128:(c + 1) * 128, t * 512:(t + 1) * 512], s[:])
                else:
                    lb.op("scalar", lambda E, t=t, s=s: E.activation(s[:], psA[t][:], AF.Silu),
                          r=[psA[t]], w=[s])
                    lb.dma("gpsimd", o_z[(c - nct // 2) * 128:(c - nct // 2 + 1) * 128,
                                         t * 512:(t + 1) * 512], s[:])



def build_proj(has_outproj, has_inproj=True, final_ln=False):
    lb = LB()
    io = IO(lb)
    emit_proj(lb, io, has_outproj, has_inproj, final_ln)
    lb.barrier_op("sync", lambda E: E.nop())
    return lb.finalize()



def emit_convx(lb, io):
    nch = C2 // 4
    nct = nch // 128
    io.inp("u_sl", [nch, S], BF16)
    cw_in = io.inp("cw_sl", [nch, 4], F32)
    cb_in = io.inp("cb_cols", [128, nct], F32)
    xw_in = io.inp("xw_sl", [nch, 96], F32)
    o_uc = io.out("uc_sl", [nch, S], BF16)
    o_dblp = io.out("dblp", [96, S], F32)

    ut = [lb.sb(f"ut{j}", [128, 3 + S], BF16) for j in range(nct)]
    uct = [lb.sb(f"uct{j}", [128, S], BF16) for j in range(nct)]
    cw = lb.sb("cw", [128, 4 * nct], F32)
    cb = lb.sb("cb", [128, nct], F32)
    xw_f = lb.sb("xw_f", [128, 96 * nct], F32)
    xw_b = lb.sb("xw_b", [128, 96 * nct], BF16)
    cvt = [lb.sb(f"cvt{k}", [128, 2048], F32) for k in range(2)]
    ps_x = [lb.ps(f"ps_x{k}", [96, 512], F32) for k in range(2)]
    stx = [lb.sb(f"stx{k}", [96, 512], F32) for k in range(2)]

    lb.dma("sync", cb[:], cb_in)
    for j in range(nct):
        lb.op("vector", lambda E, j=j: E.memset(ut[j][:, 0:3], 0.0), w=[ut[j]])
        io.load("u_sl", ut[j][:, 3:], j * 128, (j + 1) * 128, 0, S)
        lb.dma("sync", cw[:, j * 4:(j + 1) * 4], cw_in[j * 128:(j + 1) * 128, :])
        lb.dma("sync", xw_f[:, j * 96:(j + 1) * 96], xw_in[j * 128:(j + 1) * 128, :])
    lb.op("vector", lambda E: E.tensor_copy(xw_b[:], xw_f[:]), r=[xw_f], w=[xw_b])

    for j in range(nct):
        for (o, w) in chunks(S, 2048):
            cv = cvt[(j * 4 + o // 2048) % 2]
            lb.op("vector", lambda E, j=j, o=o, w=w, cv=cv:
                  E.tensor_scalar(cv[:, :w], ut[j][:, o + 3:o + 3 + w],
                                  cw[:, j * 4 + 3:j * 4 + 4], cb[:, j:j + 1],
                                  ALU.mult, ALU.add),
                  r=[ut[j], cw, cb], w=[cv])
            for k in (2, 1, 0):
                lb.op("vector", lambda E, j=j, o=o, w=w, k=k, cv=cv:
                      E.scalar_tensor_tensor(cv[:, :w], ut[j][:, o + k:o + k + w],
                                             cw[:, j * 4 + k:j * 4 + k + 1], cv[:, :w],
                                             ALU.mult, ALU.add),
                      r=[ut[j], cw, cv], w=[cv])
            lb.op("scalar", lambda E, j=j, o=o, w=w, cv=cv:
                  E.activation(uct[j][:, o:o + w], cv[:, :w], AF.Silu),
                  r=[cv], w=[uct[j]])
        lb.dma("gpsimd", o_uc[j * 128:(j + 1) * 128, :], uct[j][:])

    for ci, (o, w) in enumerate(chunks(S, 512)):
        ps = ps_x[ci % 2]
        st = stx[ci % 2]
        for j in range(nct):
            lb.op("tensor", lambda E, j=j, o=o, w=w, ps=ps:
                  E.matmul(ps[:, :w], xw_b[:, j * 96:(j + 1) * 96], uct[j][:, o:o + w],
                           start=(j == 0), stop=(j == nct - 1)),
                  r=[xw_b, uct[j]] + ([ps] if j else []), w=[ps])
        lb.op("vector", lambda E, ps=ps, st=st, w=w: E.tensor_copy(st[:, :w], ps[:, :w]),
              r=[ps], w=[st])
        lb.dma("gpsimd", o_dblp[:, o:o + w], st[:, :w])



def build_convx():
    lb = LB()
    io = IO(lb)
    emit_convx(lb, io)
    lb.barrier_op("sync", lambda E: E.nop())
    return lb.finalize()



TCH = 1024


def emit_scan_core(lb, io, nch, dt_name, u_name, bc_scr, s_A, post,
                   dt_src=None, dt_key=None):
    nct = nch // 128
    u = uid()
    Bbc_n = [lb.sb(f"Bbc{u}_{n}", [128, TCH], BF16) for n in range(16)]
    Cbc_n = [lb.sb(f"Cbc{u}_{n}", [128, TCH], BF16) for n in range(16)]
    hC = lb.sb(f"hC{u}", [128, 16 * TCH], BF16)
    dAt_r = [lb.sb(f"dAt{u}_{k}", [128, TCH], F32) for k in range(3)]
    dBu_r = [lb.sb(f"dBu{u}_{k}", [128, TCH], BF16) for k in range(3)]
    ht_r = [lb.sb(f"ht{u}_{k}", [128, TCH], BF16) for k in range(3)]
    st_state = lb.sb(f"stst{u}", [128, nct * 16], F32)
    dt_j = [lb.sb(f"dtj{u}_{k}", [128, TCH], BF16) for k in range(2)]
    u_j = [lb.sb(f"uj{u}_{k}", [128, TCH], BF16) for k in range(2)]
    w_j = lb.sb(f"wj{u}", [128, TCH], BF16)
    y_j = lb.sb(f"yj{u}", [128, TCH], F32)
    lb.op("vector", lambda E: E.memset(st_state[:], 0.0), w=[st_state])

    cl = chunks(S, TCH)

    def bload(n, ci2):
        o2, w2 = cl[ci2]
        lb.dma("sync", Bbc_n[n][:, :w2],
               bc_scr[n:n + 1, o2:o2 + w2].broadcast_to([128, w2]))

    def cload(n, ci2):
        o2, w2 = cl[ci2]
        lb.dma("sync", Cbc_n[n][:, :w2],
               bc_scr[16 + n:17 + n, o2:o2 + w2].broadcast_to([128, w2]))

    for ci, (o, w) in enumerate(cl):
        if ci == 0:
            for n in range(16):
                bload(n, 0)
                cload(n, 0)
        for j in range(nct):
            dtj, uj = dt_j[j % 2], u_j[j % 2]
            if dt_key is not None:
                lb.op("sync", lambda E, dtj=dtj, j=j, o=o, w=w:
                      E.dma_start(out=dtj[:, :w],
                                  in_=dt_src[j * 128:(j + 1) * 128, o:o + w]),
                      r=[dt_key(j, ci)], w=[dtj], dma=True,
                      dma_stream=f"dtld{u}_{j % 2}")
            else:
                io.load(dt_name, dtj[:, :w], j * 128, (j + 1) * 128, o, o + w,
                        src=dt_src)
            io.load(u_name, uj[:, :w], j * 128, (j + 1) * 128, o, o + w)
            lb.op("vector", lambda E, dtj=dtj, uj=uj, w=w:
                  E.tensor_mul(w_j[:, :w], dtj[:, :w], uj[:, :w]),
                  r=[dtj, uj], w=[w_j])
            for n in range(16):
                col = j * 16 + n
                dAt, dBu, ht = dAt_r[n % 3], dBu_r[n % 3], ht_r[n % 3]
                lb.op("scalar", lambda E, dtj=dtj, w=w, col=col, dAt=dAt:
                      E.activation(dAt[:, :w], dtj[:, :w], AF.Exp,
                                   scale=s_A[:, col:col + 1]),
                      r=[dtj, s_A], w=[dAt])
                lb.op("vector", lambda E, n=n, w=w, dBu=dBu:
                      E.tensor_mul(dBu[:, :w], w_j[:, :w], Bbc_n[n][:, :w]),
                      r=[w_j, Bbc_n[n]], w=[dBu])
                if j == nct - 1 and ci + 1 < len(cl):
                    bload(n, ci + 1)
                lb.op("vector", lambda E, w=w, col=col, dAt=dAt, dBu=dBu, ht=ht:
                      E.tensor_tensor_scan(ht[:, :w], dAt[:, :w], dBu[:, :w],
                                           st_state[:, col:col + 1], ALU.mult, ALU.add),
                      r=[dAt, dBu, st_state], w=[ht])
                lb.op("vector", lambda E, w=w, col=col, ht=ht:
                      E.tensor_copy(st_state[:, col:col + 1], ht[:, w - 1:w]),
                      r=[ht], w=[st_state])
                lb.op("gpsimd", lambda E, n=n, w=w, ht=ht:
                      E.tensor_mul(hC[:, n * TCH:n * TCH + w], ht[:, :w],
                                   Cbc_n[n][:, :w]),
                      r=[ht, Cbc_n[n]], w=[hC])
                if j == nct - 1 and ci + 1 < len(cl):
                    cload(n, ci + 1)
            lb.op("vector", lambda E:
                  E.tensor_add(hC[:, :8 * TCH], hC[:, :8 * TCH], hC[:, 8 * TCH:]),
                  r=[hC], w=[hC])
            lb.op("vector", lambda E:
                  E.tensor_add(hC[:, :4 * TCH], hC[:, :4 * TCH],
                               hC[:, 4 * TCH:8 * TCH]),
                  r=[hC], w=[hC])
            lb.op("vector", lambda E, w=w:
                  E.tensor_reduce(y_j[:, :w],
                                  hC.rearrange("p (n t) -> p t n", n=16)[:, :w, 0:4],
                                  AX.X, ALU.add),
                  r=[hC], w=[y_j])
            post(j, ci, o, w, y_j, uj)
    return st_state



def emit_scan_l0(lb, io):
    nch = C2 // 4
    nct = nch // 128
    dblp_in = [io.inp(f"dblp{q}", [96, S], F32) for q in range(4)]
    io.inp("uc_sl", [nch, S], BF16)
    dtw_in = io.inp("dtw_sl", [RK, nch], F32)
    dtb_in = io.inp("dtb_cols", [128, nct], F32)
    alog_in = io.inp("alog_sl", [nch, NST], F32)
    dp_in = io.inp("dp_cols", [128, nct], F32)
    o_m = io.out("y_sl", [nch, S], BF16)

    acc = lb.sb("acc", [96, S], F32)
    ldb = lb.sb("ldb", [96, 2048], F32)
    lb.dma("sync", acc[:], dblp_in[0])
    for q in range(1, 4):
        for (o, w) in chunks(S, 2048):
            lb.dma("sync", ldb[:, :w], dblp_in[q][:, o:o + w])
            lb.op("vector", lambda E, o=o, w=w: E.tensor_add(acc[:, o:o + w], acc[:, o:o + w], ldb[:, :w]),
                  r=[acc, ldb], w=[acc])
    dbl_b = lb.sb("dbl_b", [96, S], BF16)
    lb.op("vector", lambda E: E.tensor_copy(dbl_b[:], acc[:]), r=[acc], w=[dbl_b])
    bc_scr = lb.dram("bc_scr", [32, S], BF16)
    lb.dma("gpsimd", bc_scr, dbl_b[64:96, :])

    dt_scr = lb.dram("dt_scr", [nch, S], BF16)
    dtw_f = lb.sb("dtw_f", [RK, nch], F32)
    dtw_b = lb.sb("dtw_b", [RK, nch], BF16)
    lb.dma("sync", dtw_f[:], dtw_in)
    lb.op("vector", lambda E: E.tensor_copy(dtw_b[:], dtw_f[:]), r=[dtw_f], w=[dtw_b])
    dtb = lb.sb("dtb", [128, nct], F32)
    lb.dma("sync", dtb[:], dtb_in)
    ps_dt = [lb.ps(f"ps_dt{k}", [128, 512], F32) for k in range(2)]
    sp_e = [lb.sb(f"sp_e{k}", [128, 512], F32) for k in range(2)]
    dt_st = [lb.sb(f"dt_st{k}", [128, 512], BF16) for k in range(2)]
    onev = lb.cvec(1.0)
    su = uid()
    it = 0
    for cg in range(S // TCH):
        for j in range(nct):
            for sub in range(2):
                o = (cg * 2 + sub) * 512
                w = 512
                ps, se, dst = ps_dt[it % 2], sp_e[it % 2], dt_st[it % 2]
                it += 1
                lb.op("tensor", lambda E, j=j, o=o, w=w, ps=ps:
                      E.matmul(ps[:, :w], dtw_b[:, j * 128:(j + 1) * 128],
                               dbl_b[0:RK, o:o + w], start=True, stop=True),
                      r=[dtw_b, dbl_b], w=[ps])
                lb.op("scalar", lambda E, j=j, w=w, ps=ps, se=se:
                      E.activation(se[:, :w], ps[:, :w], AF.Exp,
                                   bias=dtb[:, j:j + 1]),
                      r=[ps, dtb], w=[se])
                lb.op("scalar", lambda E, w=w, se=se, dst=dst:
                      E.activation(dst[:, :w], se[:, :w], AF.Ln, bias=onev[:]),
                      r=[se, onev], w=[dst])
                lb.op("gpsimd", lambda E, j=j, o=o, w=w, dst=dst:
                      E.dma_start(out=dt_scr[j * 128:(j + 1) * 128, o:o + w],
                                  in_=dst[:, :w]),
                      r=[dst], w=[f"dtk{su}_{j}_{cg}"], dma=True,
                      dma_stream=f"dtw{su}_{j}")

    s_A = lb.sb("s_A", [128, nct * 16], F32)
    for j in range(nct):
        lb.dma("sync", s_A[:, j * 16:(j + 1) * 16], alog_in[j * 128:(j + 1) * 128, :])
    lb.op("scalar", lambda E: E.activation(s_A[:], s_A[:], AF.Exp), r=[s_A], w=[s_A])
    lb.op("vector", lambda E: E.tensor_scalar(s_A[:], s_A[:], -1.0, None, ALU.mult),
          r=[s_A], w=[s_A])
    dp = lb.sb("dp", [128, nct], F32)
    lb.dma("sync", dp[:], dp_in)

    m_st = lb.sb("m_st", [128, TCH], BF16)

    def post(j, ci, o, w, y_j, uj):
        lb.op("vector", lambda E, j=j, w=w, uj=uj:
              E.scalar_tensor_tensor(m_st[:, :w], uj[:, :w], dp[:, j:j + 1], y_j[:, :w],
                                     ALU.mult, ALU.add),
              r=[uj, dp, y_j], w=[m_st])
        lb.dma("gpsimd", o_m[j * 128:(j + 1) * 128, o:o + w], m_st[:, :w])

    emit_scan_core(lb, io, nch, None, "uc_sl", bc_scr, s_A, post, dt_src=dt_scr,
                   dt_key=lambda j, ci: f"dtk{su}_{j}_{ci}")


def build_scan_l0():
    lb = LB()
    io = IO(lb)
    emit_scan_l0(lb, io)
    lb.barrier_op("sync", lambda E: E.nop())
    return lb.finalize()



def emit_scan_l1(lb, io):
    nch = D // 4
    nct = nch // 128
    io.inp("dt_sl", [nch, S], BF16)
    io.inp("x_sl", [nch, S], BF16)
    io.inp("bc2", [32, S], F32)
    alog_in = io.inp("alog_sl", [nch, NST], F32)
    dp_in = io.inp("dp_cols", [128, nct], F32)
    idf = io.inp("ident_f32", [128, 128], F32)
    o_y = io.out("y_tm", [S, nch], F32)

    bcf = lb.sb("bcf", [32, S], F32)
    bcb = lb.sb("bcb", [32, S], BF16)
    io.load("bc2", bcf[:], 0, 32, 0, S)
    lb.op("vector", lambda E: E.tensor_copy(bcb[:], bcf[:]), r=[bcf], w=[bcb])
    bc_scr = lb.dram("bc_scr1", [32, S], BF16)
    lb.dma("gpsimd", bc_scr, bcb[:])

    s_A = lb.sb("s_A", [128, nct * 16], F32)
    for j in range(nct):
        lb.dma("sync", s_A[:, j * 16:(j + 1) * 16], alog_in[j * 128:(j + 1) * 128, :])
    lb.op("scalar", lambda E: E.activation(s_A[:], s_A[:], AF.Exp), r=[s_A], w=[s_A])
    lb.op("vector", lambda E: E.tensor_scalar(s_A[:], s_A[:], -1.0, None, ALU.mult),
          r=[s_A], w=[s_A])
    dp = lb.sb("dp", [128, nct], F32)
    lb.dma("sync", dp[:], dp_in)
    sid_f = lb.sb("sid_f", [128, 128], F32)
    lb.dma("sync", sid_f[:], idf)

    ps_t = [lb.ps(f"pstr{k}", [128, 128], F32) for k in range(2)]
    st_t = [lb.sb(f"sttr{k}", [128, 128], F32) for k in range(4)]
    kcnt = [0]

    def post(j, ci, o, w, y_j, uj):
        lb.op("vector", lambda E, j=j, w=w, uj=uj:
              E.scalar_tensor_tensor(y_j[:, :w], uj[:, :w], dp[:, j:j + 1], y_j[:, :w],
                                     ALU.mult, ALU.add),
              r=[uj, dp, y_j], w=[y_j])
        for i in range(w // 128):
            p, s = ps_t[kcnt[0] % 2], st_t[kcnt[0] % 4]
            lb.op("tensor", lambda E, i=i, w=w, p=p:
                  E.transpose(p[:], y_j[:, i * 128:(i + 1) * 128], sid_f[:]),
                  r=[y_j, sid_f], w=[p])
            lb.op("vector", lambda E, p=p, s=s: E.tensor_copy(s[:], p[:]), r=[p], w=[s])
            lb.dma("gpsimd", o_y[o + i * 128:o + (i + 1) * 128, j * 128:(j + 1) * 128], s[:])
            kcnt[0] += 1

    emit_scan_core(lb, io, nch, "dt_sl", "x_sl", bc_scr, s_A, post)


def build_scan_l1():
    lb = LB()
    io = IO(lb)
    emit_scan_l1(lb, io)
    lb.barrier_op("sync", lambda E: E.nop())
    return lb.finalize()



def emit_wmean(lb, wt_in, ntiles, free, slab, psm):
    u = uid()
    acc = lb.sb(f"wm_acc{u}", [128, 1], F32)
    prt = lb.sb(f"wm_p{u}", [128, 1], F32)
    lb.op("vector", lambda E: E.memset(acc[:], 0.0), w=[acc])
    for i in range(ntiles):
        for (o, w) in chunks(free, 1024):
            lb.dma("sync", slab[:, :w], wt_in[i * 128:(i + 1) * 128, o:o + w])
            lb.op("vector", lambda E, w=w: E.tensor_reduce(prt[:], slab[:, :w], AX.X, ALU.add,
                                                           apply_absolute_value=True),
                  r=[slab], w=[prt])
            lb.op("vector", lambda E: E.tensor_add(acc[:], acc[:], prt[:]), r=[acc, prt], w=[acc])
    onesf = lb.cvec(1.0, F32)
    mn = lb.sb(f"wm_mn{u}", [1, 2], F32)
    lb.op("tensor", lambda E: E.matmul(psm[:], acc[:], onesf[:], start=True, stop=True),
          r=[acc, onesf], w=[psm])
    lb.op("vector", lambda E: E.tensor_scalar(mn[:, 0:1], psm[:], 1.0 / (ntiles * 128 * free),
                                              1e-5, ALU.mult, ALU.max), r=[psm], w=[mn])
    lb.op("vector", lambda E: E.reciprocal(mn[:, 1:2], mn[:, 0:1]), r=[mn], w=[mn])
    scr = lb.dram(f"wm_scr{u}", [1, 2], F32)
    out = lb.sb(f"wm_bc{u}", [128, 2], F32)
    lb.dma("scalar", scr, mn[:])
    lb.dma("scalar", out[:], scr.broadcast_to([128, 2]))
    return out


def emit_ffn0(lb, io):
    TW = TPC + 2
    nft = D // 128
    nht = H // 128
    xnq_in = io.inp("xnq_cols", [D, TW], BF16)
    qinv_in = io.inp("qinv_cols", [1, TW], F32)
    w1t_in = io.inp("w1t", [D, H], F32)
    w2t_in = io.inp("w2t", [H, D], F32)
    cw_in = io.inp("cw_cols", [128, 3 * nht], F32)
    cb_in = io.inp("cb_cols", [128, nht], F32)
    idf = io.inp("ident_f32", [128, 128], F32)
    o_ffn = io.out("ffn_tm", [TPC, D], F32)

    xnq = [lb.sb(f"xnq{k}", [128, TW], BF16) for k in range(nft)]
    for k in range(nft):
        lb.dma("sync", xnq[k][:], xnq_in[k * 128:(k + 1) * 128, :])
    qbc = lb.sb("qbc", [128, TW], F32)
    lb.dma("sync", qbc[:], qinv_in.broadcast_to([128, TW]))
    cw = lb.sb("cw", [128, 3 * nht], F32)
    cb = lb.sb("cb", [128, nht], F32)
    lb.dma("sync", cw[:], cw_in)
    lb.dma("sync", cb[:], cb_in)
    sid_f = lb.sb("sid_f", [128, 128], F32)
    lb.dma("sync", sid_f[:], idf)

    slab_f = lb.sb("slab_f", [128, D], F32)
    wm_ps = lb.ps("wm_ps", [1, 1], F32)
    mn1 = emit_wmean(lb, w1t_in, nft, H, slab_f, wm_ps)
    slab2_f = slab_f
    mn2 = emit_wmean(lb, w2t_in, nht, D, slab2_f, wm_ps)

    w2q = [lb.sb(f"w2q{g}", [128, D], BF16) for g in range(nht)]
    tmp_i = lb.sb("tmp_i", [128, 1024], I32)
    tmp_f = lb.sb("tmp_f", [128, 1024], F32)
    for g in range(nht):
        lb.dma("sync", slab2_f[:], w2t_in[g * 128:(g + 1) * 128, :])
        lb.op("vector", lambda E: E.tensor_scalar(tmp_f[:, :D], slab2_f[:], mn2[:, 1:2], None, ALU.mult),
              r=[slab2_f, mn2], w=[tmp_f])
        lb.op("vector", lambda E: E.tensor_copy(tmp_i[:, :D], tmp_f[:, :D]), r=[tmp_f], w=[tmp_i])
        lb.op("vector", lambda E, g=g: E.tensor_copy(w2q[g][:], tmp_i[:, :D]), r=[tmp_i], w=[w2q[g]])
        lb.op("vector", lambda E, g=g: E.tensor_scalar(w2q[g][:], w2q[g][:], 1.0, -1.0, ALU.min, ALU.max),
              r=[w2q[g]], w=[w2q[g]])

    hs_scr = lb.dram("hs_scr", [H, TPC], F32)
    w1slab_b = lb.sb("w1slab_b", [128, nft * 128], BF16)
    h1 = lb.sb("h1", [128, TW], F32)
    cv = lb.sb("cvt", [128, TPC], F32)
    amax = lb.sb("amax", [128, TPC], F32)
    lb.op("vector", lambda E: E.memset(amax[:], 0.0), w=[amax])
    psA = [lb.ps(f"psA{t}", [128, 512], F32) for t in range(2)]
    for g in range(nht):
        lb.dma("sync", slab_f[:, :D], w1t_in[:, g * 128:(g + 1) * 128].rearrange("(j p) c -> p j c", p=128))
        lb.op("vector", lambda E: E.tensor_scalar(tmp_f[:, :D], slab_f[:, :D], mn1[:, 1:2], None, ALU.mult),
              r=[slab_f, mn1], w=[tmp_f])
        lb.op("vector", lambda E: E.tensor_copy(tmp_i[:, :D], tmp_f[:, :D]), r=[tmp_f], w=[tmp_i])
        lb.op("vector", lambda E: E.tensor_copy(w1slab_b[:], tmp_i[:, :D]), r=[tmp_i], w=[w1slab_b])
        lb.op("vector", lambda E: E.tensor_scalar(w1slab_b[:], w1slab_b[:], 1.0, -1.0, ALU.min, ALU.max),
              r=[w1slab_b], w=[w1slab_b])
        for ci, (o, w) in enumerate(chunks(TW, 410)):
            ps = psA[ci % 2]
            for k in range(nft):
                lb.op("tensor", lambda E, k=k, o=o, w=w, ps=ps:
                      E.matmul(ps[:, :w], w1slab_b[:, k * 128:(k + 1) * 128], xnq[k][:, o:o + w],
                               start=(k == 0), stop=(k == nft - 1)),
                      r=[w1slab_b, xnq[k]] + ([ps] if k else []), w=[ps])
            lb.op("vector", lambda E, o=o, w=w, ps=ps:
                  E.scalar_tensor_tensor(h1[:, o:o + w], ps[:, :w], mn1[:, 0:1], qbc[:, o:o + w],
                                         ALU.mult, ALU.mult),
                  r=[ps, mn1, qbc], w=[h1])
        lb.op("vector", lambda E, g=g:
              E.tensor_scalar(cv[:], h1[:, 2:2 + TPC], cw[:, g * 3 + 2:g * 3 + 3],
                              cb[:, g:g + 1], ALU.mult, ALU.add),
              r=[h1, cw, cb], w=[cv])
        for k in (1, 0):
            lb.op("vector", lambda E, g=g, k=k:
                  E.scalar_tensor_tensor(cv[:], h1[:, k:k + TPC], cw[:, g * 3 + k:g * 3 + k + 1],
                                         cv[:], ALU.mult, ALU.add),
                  r=[h1, cw, cv], w=[cv])
        lb.op("scalar", lambda E: E.activation(cv[:], cv[:], AF.Silu), r=[cv], w=[cv])
        lb.dma("gpsimd", hs_scr[g * 128:(g + 1) * 128, :], cv[:])
        lb.op("vector", lambda E: E.scalar_tensor_tensor(h1[:, 0:TPC], cv[:], -1.0, cv[:], ALU.mult, ALU.max),
              r=[cv], w=[h1])
        lb.op("vector", lambda E: E.tensor_max(amax[:], amax[:], h1[:, 0:TPC]), r=[amax, h1], w=[amax])

    ps_t = [psA[0][:, 0:128], psA[1][:, 0:128]]
    am_t = lb.sb("am_t", [128, 128], F32)
    am_tm = lb.sb("am_tm", [128, TPC // 128], F32)
    for i in range(TPC // 128):
        p = ps_t[i % 2]
        lb.op("tensor", lambda E, i=i, p=p: E.transpose(p[:], amax[:, i * 128:(i + 1) * 128], sid_f[:]),
              r=[amax, sid_f], w=[p])
        lb.op("vector", lambda E, p=p: E.tensor_copy(am_t[:], p[:]), r=[p], w=[am_t])
        lb.op("vector", lambda E, i=i: E.tensor_reduce(am_tm[:, i:i + 1], am_t[:], AX.X, ALU.max),
              r=[am_t], w=[am_tm])
    sc_tm = lb.sb("sc_tm", [128, TPC // 128], F32)
    rc_tm = lb.sb("rc_tm", [128, TPC // 128], F32)
    lb.op("vector", lambda E: E.tensor_scalar(am_tm[:], am_tm[:], 1e-5, None, ALU.max),
          r=[am_tm], w=[am_tm])
    lb.op("vector", lambda E: E.reciprocal(sc_tm[:], am_tm[:]), r=[am_tm], w=[sc_tm])
    lb.op("vector", lambda E: E.tensor_scalar(sc_tm[:], sc_tm[:], 127.0, None, ALU.mult),
          r=[sc_tm], w=[sc_tm])
    lb.op("vector", lambda E: E.tensor_scalar(rc_tm[:], am_tm[:], 1.0 / 127.0, None, ALU.mult),
          r=[am_tm], w=[rc_tm])
    scr_s = lb.dram("scr_s", [2, TPC], F32)
    lb.dma("scalar", scr_s[0:1, :].rearrange("o (i p) -> o p i", p=128).squeeze(0), sc_tm[:],
           allow_slow_non_contiguous=True)
    lb.dma("scalar", scr_s[1:2, :].rearrange("o (i p) -> o p i", p=128).squeeze(0), rc_tm[:],
           allow_slow_non_contiguous=True)
    sc_bc = lb.sb("sc_bc", [128, TPC], F32)
    rc_bc = lb.sb("rc_bc", [128, TPC], F32)
    lb.dma("scalar", sc_bc[:], scr_s[0:1, :].broadcast_to([128, TPC]))
    lb.dma("scalar", rc_bc[:], scr_s[1:2, :].broadcast_to([128, TPC]))

    ffn_scr = lb.dram("ffn_scr", [D, TPC], F32)
    qg = [lb.sb(f"qg{g}", [128, 512], BF16) for g in range(nht)]
    hs_ld = lb.sb("hs_ld", [128, 512], F32)
    psF = [lb.ps(f"psF{f}", [128, 512], F32) for f in range(4)]
    stF = [lb.sb(f"stF{f}", [128, 512], F32) for f in range(4)]
    for (o, w) in chunks(TPC, 512):
        for g in range(nht):
            lb.dma("sync", hs_ld[:, :w], hs_scr[g * 128:(g + 1) * 128, o:o + w])
            lb.op("vector", lambda E, o=o, w=w: E.tensor_mul(tmp_f[:, :w], hs_ld[:, :w], sc_bc[:, o:o + w]),
                  r=[hs_ld, sc_bc], w=[tmp_f])
            lb.op("vector", lambda E, w=w: E.tensor_copy(tmp_i[:, :w], tmp_f[:, :w]),
                  r=[tmp_f], w=[tmp_i])
            lb.op("vector", lambda E, g=g, w=w: E.tensor_copy(qg[g][:, :w], tmp_i[:, :w]),
                  r=[tmp_i], w=[qg[g]])
        for half in range(2):
            for f in range(4):
                ff = half * 4 + f
                for g in range(nht):
                    lb.op("tensor", lambda E, ff=ff, f=f, g=g, w=w:
                          E.matmul(psF[f][:, :w], w2q[g][:, ff * 128:(ff + 1) * 128], qg[g][:, :w],
                                   start=(g == 0), stop=(g == nht - 1)),
                          r=[w2q[g], qg[g]] + ([psF[f]] if g else []), w=[psF[f]])
            for f in range(4):
                ff = half * 4 + f
                lb.op("vector", lambda E, ff=ff, f=f, o=o, w=w:
                      E.scalar_tensor_tensor(stF[f][:, :w], psF[f][:, :w], mn2[:, 0:1],
                                             rc_bc[:, o:o + w], ALU.mult, ALU.mult),
                      r=[psF[f], mn2, rc_bc], w=[stF[f]])
                lb.dma("gpsimd", ffn_scr[ff * 128:(ff + 1) * 128, o:o + w], stF[f][:, :w])

    ld_f = cv
    st_o = [stF[k][:, 0:128] for k in range(4)]
    kk = 0
    for j in range(nft):
        lb.dma("sync", ld_f[:], ffn_scr[j * 128:(j + 1) * 128, :])
        for i in range(TPC // 128):
            p, s = ps_t[kk % 2], st_o[kk % 4]
            lb.op("tensor", lambda E, i=i, p=p: E.transpose(p, ld_f[:, i * 128:(i + 1) * 128], sid_f[:]),
                  r=[ld_f, sid_f], w=[p])
            lb.op("vector", lambda E, p=p, s=s: E.tensor_copy(s, p), r=[p], w=[s])
            lb.dma("gpsimd", o_ffn[i * 128:(i + 1) * 128, j * 128:(j + 1) * 128], s)
            kk += 1



def build_ffn0():
    lb = LB()
    io = IO(lb)
    emit_ffn0(lb, io)
    lb.barrier_op("sync", lambda E: E.nop())
    return lb.finalize()



def emit_mid(lb, io):
    x_in = io.inp("x_rows", [TPC, D], F32)
    a_in = io.inp("attn_rows", [TPC, D], F32)
    f_in = io.inp("ffn_rows", [TPC, D], F32)
    wd_in = io.inp("wdT", [D, 96], F32)
    wdt_in = io.inp("wdtT", [RK, D], F32)
    bdt_in = io.inp("bdt_cols", [128, D // 128], F32)
    idb = io.inp("ident_bf16", [128, 128], BF16)
    o_x1 = io.out("x1_rows", [TPC, D], F32)
    o_xn2 = io.out("xn2_fm", [D, TPC], BF16)
    o_dbl2 = io.out("dbl2", [96, TPC], F32)
    o_dt2 = io.out("dt2", [D, TPC], BF16)

    nft = D // 128
    ntt = TPC // 128
    sid_b = lb.sb("sid_b", [128, 128], BF16)
    lb.dma("sync", sid_b[:], idb)
    xt = [lb.sb(f"xt{k}", [128, D], F32) for k in range(2)]
    at = [lb.sb(f"at{k}", [128, D], F32) for k in range(2)]
    ft = [lb.sb(f"ft{k}", [128, D], F32) for k in range(2)]
    sq = lb.sb("sq", [128, D], F32)
    xnb = [lb.sb(f"xnb{k}", [128, D], BF16) for k in range(2)]
    xn2 = [lb.sb(f"xn2_{j}", [128, TPC], BF16) for j in range(nft)]
    ps_b = [lb.ps(f"ps_b{k}", [128, 128], BF16) for k in range(2)]

    for i in range(ntt):
        xi, ai, fi, xb = xt[i % 2], at[i % 2], ft[i % 2], xnb[i % 2]
        lb.dma("sync", xi[:], x_in[i * 128:(i + 1) * 128, :])
        lb.dma("sync", ai[:], a_in[i * 128:(i + 1) * 128, :])
        lb.dma("sync", fi[:], f_in[i * 128:(i + 1) * 128, :])
        lb.op("vector", lambda E, xi=xi, ai=ai: E.tensor_add(xi[:], xi[:], ai[:]),
              r=[xi, ai], w=[xi])
        lb.op("vector", lambda E, xi=xi, fi=fi: E.tensor_add(xi[:], xi[:], fi[:]),
              r=[xi, fi], w=[xi])
        lb.dma("gpsimd", o_x1[i * 128:(i + 1) * 128, :], xi[:])
        emit_ln_tm(lb, xi, ai, sq)
        lb.op("vector", lambda E, ai=ai, xb=xb: E.tensor_copy(xb[:], ai[:]), r=[ai], w=[xb])
        for j in range(nft):
            p = ps_b[(i * nft + j) % 2]
            lb.op("tensor", lambda E, i=i, j=j, p=p, xb=xb:
                  E.transpose(p[:], xb[:, j * 128:(j + 1) * 128], sid_b[:]),
                  r=[xb, sid_b], w=[p])
            lb.op("scalar", lambda E, i=i, j=j, p=p:
                  E.copy(xn2[j][:, i * 128:(i + 1) * 128], p[:]), r=[p], w=[xn2[j]])
    for j in range(nft):
        lb.dma("gpsimd", o_xn2[j * 128:(j + 1) * 128, :], xn2[j][:])

    wd_f = lb.sb("wd_f", [128, nft * 96], F32)
    wd_b = lb.sb("wd_b", [128, nft * 96], BF16)
    lb.dma("sync", wd_f[:], wd_in.rearrange("(j p) c -> p j c", p=128))
    lb.op("vector", lambda E: E.tensor_copy(wd_b[:], wd_f[:]), r=[wd_f], w=[wd_b])
    ps_d = [lb.ps(f"ps_d{k}", [128, 512], F32) for k in range(2)]
    dbl2 = lb.sb("dbl2_sb", [96, TPC], F32)
    for ci, (o, w) in enumerate(chunks(TPC, 512)):
        ps = ps_d[ci % 2]
        for j in range(nft):
            lb.op("tensor", lambda E, j=j, o=o, w=w, ps=ps:
                  E.matmul(ps[0:96, :w], wd_b[:, j * 96:(j + 1) * 96], xn2[j][:, o:o + w],
                           start=(j == 0), stop=(j == nft - 1)),
                  r=[wd_b, xn2[j]] + ([ps] if j else []), w=[ps])
        lb.op("vector", lambda E, o=o, w=w, ps=ps: E.tensor_copy(dbl2[:, o:o + w], ps[0:96, :w]),
              r=[ps], w=[dbl2])
    lb.dma("gpsimd", o_dbl2, dbl2[:])
    dbl2_b = lb.sb("dbl2_b", [RK, TPC], BF16)
    lb.op("vector", lambda E: E.tensor_copy(dbl2_b[:], dbl2[0:RK, :]), r=[dbl2], w=[dbl2_b])

    wdt_f = lb.sb("wdt_f", [RK, D], F32)
    wdt_b = lb.sb("wdt_b", [RK, D], BF16)
    lb.dma("sync", wdt_f[:], wdt_in)
    lb.op("vector", lambda E: E.tensor_copy(wdt_b[:], wdt_f[:]), r=[wdt_f], w=[wdt_b])
    bdt = lb.sb("bdt", [128, nft], F32)
    lb.dma("sync", bdt[:], bdt_in)
    sp_t1 = lb.sb("sp_t1", [128, 512], F32)
    sp_e = lb.sb("sp_e", [128, 512], F32)
    sp_l = lb.sb("sp_l", [128, 512], F32)
    dt_st = [lb.sb(f"dt_st{k}", [128, 512], BF16) for k in range(2)]
    onev = lb.cvec(1.0)
    for j in range(nft):
        for ci, (o, w) in enumerate(chunks(TPC, 512)):
            ps = ps_d[ci % 2]
            lb.op("tensor", lambda E, j=j, o=o, w=w, ps=ps:
                  E.matmul(ps[0:128, :w] if False else ps[:, :w], wdt_b[:, j * 128:(j + 1) * 128],
                           dbl2_b[:, o:o + w], start=True, stop=True),
                  r=[wdt_b, dbl2_b], w=[ps])
            lb.op("vector", lambda E, j=j, w=w, ps=ps:
                  E.tensor_scalar(sp_t1[:, :w], ps[:, :w], bdt[:, j:j + 1], 30.0, ALU.add, ALU.min),
                  r=[ps, bdt], w=[sp_t1])
            lb.op("scalar", lambda E, w=w: E.activation(sp_e[:, :w], sp_t1[:, :w], AF.Exp),
                  r=[sp_t1], w=[sp_e])
            lb.op("scalar", lambda E, w=w: E.activation(sp_l[:, :w], sp_e[:, :w], AF.Ln, bias=onev[:]),
                  r=[sp_e, onev], w=[sp_l])
            dst = dt_st[ci % 2]
            lb.op("vector", lambda E, j=j, w=w, ps=ps, dst=dst:
                  E.scalar_tensor_tensor(dst[:, :w], ps[:, :w], bdt[:, j:j + 1], sp_l[:, :w],
                                         ALU.add, ALU.max),
                  r=[ps, bdt, sp_l], w=[dst])
            lb.dma("gpsimd", o_dt2[j * 128:(j + 1) * 128, o:o + w], dst[:, :w])



def build_mid():
    lb = LB()
    io = IO(lb)
    emit_mid(lb, io)
    lb.barrier_op("sync", lambda E: E.nop())
    return lb.finalize()



def emit_l1ffn(lb, io):
    nft = D // 128
    ntt = TPC // 128
    xn2_in = io.inp("xn2_cols", [D, TPC], BF16)
    x1_in = io.inp("x1_rows", [TPC, D], F32)
    io.inp("y2_rows", [TPC, D], F32)
    wv_in = io.inp("WvT", [D, D], F32)
    bv_in = io.inp("bv_cols", [128, nft], F32)
    wg_in = io.inp("WgT", [D, 8], F32)
    bg_in = io.inp("bg8", [8, 1], F32)
    wp_in = io.inp("WpT", [D, 64], F32)
    w1_in = io.inp("W1T", [D, 256], F32)
    b1_in = io.inp("b1_cols", [128, 2], F32)
    w2_in = io.inp("W2n", [256, D], F32)
    b2_in = io.inp("b2stk", [8, D], F32)
    gr_in = io.inp("g_r8", [8, 1], F32)
    br_in = io.inp("b_r8", [8, 1], F32)
    gc_in = io.inp("g_c8", [8, 1], F32)
    bc_in = io.inp("b_c8", [8, 1], F32)
    al_in = io.inp("alpha1", [1, 1], F32)
    idf = io.inp("ident_f32", [128, 128], F32)
    o_out = io.out("out_rows", [TPC, D], F32)

    sid_f = lb.sb("sid_f", [128, 128], F32)
    lb.dma("sync", sid_f[:], idf)
    xn2 = [lb.sb(f"xn2_{j}", [128, TPC], BF16) for j in range(nft)]
    for j in range(nft):
        lb.dma("sync", xn2[j][:], xn2_in[j * 128:(j + 1) * 128, :])

    psA = [lb.ps(f"psA{k}", [128, 512], F32) for k in range(2)]
    ps8 = lb.ps("ps8", [128, 512], F32)
    ps1 = lb.ps("ps1", [1, 512], F32)

    Vf = [lb.sb(f"Vf{j}", [128, TPC], F32) for j in range(nft)]
    Vb = [lb.sb(f"Vb{j}", [128, TPC], BF16) for j in range(nft)]
    bv = lb.sb("bv", [128, nft], F32)
    lb.dma("sync", bv[:], bv_in)
    wslab_f = lb.sb("wslab_f", [128, nft * 128], F32)
    wslab_b = lb.sb("wslab_b", [128, nft * 128], BF16)
    for f in range(nft):
        lb.dma("sync", wslab_f[:], wv_in[:, f * 128:(f + 1) * 128].rearrange("(j p) c -> p j c", p=128))
        lb.op("vector", lambda E: E.tensor_copy(wslab_b[:], wslab_f[:]), r=[wslab_f], w=[wslab_b])
        for ci, (o, w) in enumerate(chunks(TPC, 512)):
            ps = psA[ci % 2]
            for k in range(nft):
                lb.op("tensor", lambda E, k=k, o=o, w=w, ps=ps:
                      E.matmul(ps[:, :w], wslab_b[:, k * 128:(k + 1) * 128], xn2[k][:, o:o + w],
                               start=(k == 0), stop=(k == nft - 1)),
                      r=[wslab_b, xn2[k]] + ([ps] if k else []), w=[ps])
            lb.op("vector", lambda E, f=f, o=o, w=w, ps=ps:
                  E.tensor_scalar(Vf[f][:, o:o + w], ps[:, :w], bv[:, f:f + 1], None, ALU.add),
                  r=[ps, bv], w=[Vf[f]])
            lb.op("vector", lambda E, f=f, o=o, w=w: E.tensor_copy(Vb[f][:, o:o + w], Vf[f][:, o:o + w]),
                  r=[Vf[f]], w=[Vb[f]])

    wg_f = lb.sb("wg_f", [128, nft * 8], F32)
    wg_b = lb.sb("wg_b", [128, nft * 8], BF16)
    lb.dma("sync", wg_f[:], wg_in.rearrange("(j p) c -> p j c", p=128))
    lb.op("vector", lambda E: E.tensor_copy(wg_b[:], wg_f[:]), r=[wg_f], w=[wg_b])
    wp_f = lb.sb("wp_f", [128, nft * 64], F32)
    wp_b = lb.sb("wp_b", [128, nft * 64], BF16)
    lb.dma("sync", wp_f[:], wp_in.rearrange("(j p) c -> p j c", p=128))
    lb.op("vector", lambda E: E.tensor_copy(wp_b[:], wp_f[:]), r=[wp_f], w=[wp_b])
    rt = lb.sb("rt", [8, TPC], F32)
    ct = lb.sb("ct", [8, TPC], F32)
    pv = lb.sb("pv", [64, TPC], F32)
    sqv = lb.sb("sqv", [64, 512], F32)
    bg = lb.sb("bg", [8, 1], F32)
    lb.dma("sync", bg[:], bg_in)
    blk8_in = io.inp("blk8c", [64, 8], F32)
    blk8 = lb.sb("blk8", [64, 8], F32)
    lb.dma("sync", blk8[:], blk8_in)
    onesf = lb.cvec(1.0, F32)
    eighth = lb.cvec(0.125, F32)
    for ci, (o, w) in enumerate(chunks(TPC, 512)):
        for k in range(nft):
            lb.op("tensor", lambda E, k=k, o=o, w=w:
                  E.matmul(ps8[0:8, :w], wg_b[:, k * 8:(k + 1) * 8], xn2[k][:, o:o + w] if False else Vb[k][:, o:o + w],
                           start=(k == 0), stop=(k == nft - 1)),
                  r=[wg_b, Vb[k]] + ([ps8] if k else []), w=[ps8])
        lb.op("vector", lambda E, o=o, w=w: E.tensor_scalar(rt[:, o:o + w], ps8[0:8, :w], bg[:], None, ALU.add),
              r=[ps8, bg], w=[rt])
        for k in range(nft):
            lb.op("tensor", lambda E, k=k, o=o, w=w:
                  E.matmul(ps8[0:64, :w], wp_b[:, k * 64:(k + 1) * 64], Vb[k][:, o:o + w],
                           start=(k == 0), stop=(k == nft - 1)),
                  r=[wp_b, Vb[k]] + ([ps8] if k else []), w=[ps8])
        lb.op("vector", lambda E, o=o, w=w: E.tensor_copy(pv[:, o:o + w], ps8[0:64, :w]),
              r=[ps8], w=[pv])
        lb.op("vector", lambda E, o=o, w=w: E.tensor_mul(sqv[:, :w], pv[:, o:o + w], pv[:, o:o + w]),
              r=[pv], w=[sqv])
        lb.op("tensor", lambda E, o=o, w=w: E.matmul(ps8[0:8, :w], blk8[:], sqv[:, :w],
                                                     start=True, stop=True),
              r=[blk8, sqv], w=[ps8])
        lb.op("scalar", lambda E, o=o, w=w: E.activation(ct[:, o:o + w], ps8[0:8, :w], AF.Sqrt,
                                                         scale=eighth[0:8, :]),
              r=[ps8, eighth], w=[ct])

    def ln8(src, g_in, b_in, dst):
        u = uid()
        g8 = lb.sb(f"g8_{u}", [8, 1], F32)
        b8 = lb.sb(f"b8_{u}", [8, 1], F32)
        lb.dma("sync", g8[:], g_in)
        lb.dma("sync", b8[:], b_in)
        st1 = lb.sb(f"l8m{u}", [1, 512], F32)
        st2 = lb.sb(f"l8r{u}", [1, 512], F32)
        sqs = lb.sb(f"l8q{u}", [8, 512], F32)
        scr = lb.dram(f"l8scr{u}", [2, TPC], F32)
        mbc = lb.sb(f"l8mb{u}", [8, 512], F32)
        rbc = lb.sb(f"l8rb{u}", [8, 512], F32)
        epsv = lb.cvec(EPS)
        for (o, w) in chunks(TPC, 512):
            lb.op("tensor", lambda E, o=o, w=w: E.matmul(ps1[:, :w], onesf[0:8, :], src[:, o:o + w],
                                                         start=True, stop=True),
                  r=[onesf, src], w=[ps1])
            lb.op("vector", lambda E, o=o, w=w: E.tensor_scalar(st1[:, :w], ps1[:, :w], 0.125, None, ALU.mult),
                  r=[ps1], w=[st1])
            lb.op("scalar", lambda E, o=o, w=w: E.activation(sqs[:, :w], src[:, o:o + w], AF.Square),
                  r=[src], w=[sqs])
            lb.op("tensor", lambda E, o=o, w=w: E.matmul(ps1[:, :w], onesf[0:8, :], sqs[:, :w],
                                                         start=True, stop=True),
                  r=[onesf, sqs], w=[ps1])
            lb.op("vector", lambda E, o=o, w=w: E.tensor_scalar(st2[:, :w], ps1[:, :w], 0.125, None, ALU.mult),
                  r=[ps1], w=[st2])
            lb.op("scalar", lambda E, w=w: E.activation(sqs[0:1, :w], st1[:, :w], AF.Square),
                  r=[st1], w=[sqs])
            lb.op("vector", lambda E, w=w: E.tensor_sub(st2[:, :w], st2[:, :w], sqs[0:1, :w]),
                  r=[st2, sqs], w=[st2])
            lb.op("scalar", lambda E, w=w: E.activation(st2[:, :w], st2[:, :w], AF.Sqrt, bias=epsv[0:1, :]),
                  r=[st2, epsv], w=[st2])
            lb.op("vector", lambda E, w=w: E.reciprocal(st2[:, :w], st2[:, :w]), r=[st2], w=[st2])
            lb.dma("scalar", scr[0:1, o:o + w], st1[:, :w])
            lb.dma("scalar", scr[1:2, o:o + w], st2[:, :w])
            lb.dma("scalar", mbc[:, :w], scr[0:1, o:o + w].broadcast_to([8, w]))
            lb.dma("scalar", rbc[:, :w], scr[1:2, o:o + w].broadcast_to([8, w]))
            lb.op("vector", lambda E, o=o, w=w: E.tensor_sub(dst[:, o:o + w], src[:, o:o + w], mbc[:, :w]),
                  r=[src, mbc], w=[dst])
            lb.op("vector", lambda E, o=o, w=w: E.tensor_tensor(dst[:, o:o + w], dst[:, o:o + w], rbc[:, :w], ALU.mult),
                  r=[dst, rbc], w=[dst])
            lb.op("vector", lambda E, o=o, w=w: E.tensor_scalar(dst[:, o:o + w], dst[:, o:o + w], g8[:], b8[:],
                                                                ALU.mult, ALU.add),
                  r=[dst, g8, b8], w=[dst])

    lnr = rt
    lnc = ct
    ln8(rt, gr_in, br_in, lnr)
    ln8(ct, gc_in, bc_in, lnc)

    alv = lb.sb("alv", [1, 1], F32)
    lb.dma("sync", alv[:], al_in)
    lb.op("scalar", lambda E: E.activation(alv[:], alv[:], AF.Sigmoid), r=[alv], w=[alv])
    scr_a = lb.dram("scr_a", [1, 1], F32)
    sgv = lb.sb("sgv", [8, 1], F32)
    lb.dma("scalar", scr_a, alv[:])
    lb.dma("scalar", sgv[:], scr_a.broadcast_to([8, 1]))
    sfm = lnc
    lb.op("vector", lambda E: E.scalar_tensor_tensor(sfm[:], lnc[:], sgv[:], lnr[:], ALU.mult, ALU.add),
          r=[lnc, sgv, lnr], w=[sfm])

    we_fm = lb.sb("we_fm", [8, TPC], F32)
    we_b = lb.sb("we_b", [8, TPC], BF16)
    s_tm = lb.sb("s_tm", [128, 8], F32)
    we_tm = lb.sb("we_tm", [128, 8], F32)
    msk1 = lb.sb("msk1", [128, 8], F32)
    msk2 = lb.sb("msk2", [128, 8], F32)
    m1 = lb.sb("m1", [128, 1], F32)
    m2 = lb.sb("m2", [128, 1], F32)
    w2w = lb.sb("w2w", [128, 1], F32)
    w1w = lb.sb("w1w", [128, 1], F32)
    for i in range(ntt):
        lb.op("tensor", lambda E, i=i: E.transpose(psA[0][:, 0:8], sfm[:, i * 128:(i + 1) * 128], sid_f[0:8, 0:8]),
              r=[sfm, sid_f], w=[psA[0]])
        lb.op("vector", lambda E: E.tensor_copy(s_tm[:], psA[0][:, 0:8]), r=[psA[0]], w=[s_tm])
        lb.op("vector", lambda E: E.tensor_reduce(m1[:], s_tm[:], AX.X, ALU.max), r=[s_tm], w=[m1])
        lb.op("vector", lambda E: E.tensor_scalar(msk1[:], s_tm[:], m1[:], None, ALU.is_equal),
              r=[s_tm, m1], w=[msk1])
        lb.op("vector", lambda E: E.scalar_tensor_tensor(msk2[:], msk1[:], -1e30, s_tm[:], ALU.mult, ALU.add),
              r=[msk1, s_tm], w=[msk2])
        lb.op("vector", lambda E: E.tensor_reduce(m2[:], msk2[:], AX.X, ALU.max), r=[msk2], w=[m2])
        lb.op("vector", lambda E: E.tensor_scalar(msk2[:], s_tm[:], m2[:], None, ALU.is_equal),
              r=[s_tm, m2], w=[msk2])
        lb.op("vector", lambda E: E.tensor_sub(w2w[:], m2[:], m1[:]), r=[m2, m1], w=[w2w])
        lb.op("scalar", lambda E: E.activation(w2w[:], w2w[:], AF.Sigmoid), r=[w2w], w=[w2w])
        lb.op("vector", lambda E: E.tensor_scalar(w1w[:], w2w[:], -1.0, 1.0, ALU.mult, ALU.add),
              r=[w2w], w=[w1w])
        lb.op("vector", lambda E: E.tensor_scalar(we_tm[:], msk1[:], w1w[:], None, ALU.mult),
              r=[msk1, w1w], w=[we_tm])
        lb.op("vector", lambda E: E.scalar_tensor_tensor(we_tm[:], msk2[:], w2w[:], we_tm[:], ALU.mult, ALU.add),
              r=[msk2, w2w, we_tm], w=[we_tm])
        lb.op("tensor", lambda E, i=i: E.transpose(psA[1][0:8, 0:128], we_tm[:], sid_f[:]),
              r=[we_tm, sid_f], w=[psA[1]])
        lb.op("vector", lambda E, i=i: E.tensor_copy(we_fm[:, i * 128:(i + 1) * 128], psA[1][0:8, 0:128]),
              r=[psA[1]], w=[we_fm])
    lb.op("vector", lambda E: E.tensor_copy(we_b[:], we_fm[:]), r=[we_fm], w=[we_b])

    w1_b = lb.sb("w1_b", [128, nft * 256], BF16)
    w1r = w1_in.rearrange("(j p) c -> p j c", p=128)
    for half in range(2):
        lb.dma("sync", wslab_f[:], w1r[:, half * 4:(half + 1) * 4, :])
        lb.op("vector", lambda E, half=half:
              E.tensor_copy(w1_b[:, half * 1024:(half + 1) * 1024], wslab_f[:]),
              r=[wslab_f], w=[w1_b])
    b1c = lb.sb("b1c", [128, 2], F32)
    lb.dma("sync", b1c[:], b1_in)
    rep32_in = io.inp("rep32c", [8, 256], F32)
    rep32 = lb.sb("rep32", [8, 256], F32)
    lb.dma("sync", rep32[:], rep32_in)
    Hw = [xn2[0], xn2[1]]
    for c in range(2):
        for ci, (o, w) in enumerate(chunks(TPC, 512)):
            ps = psA[ci % 2]
            for k in range(nft):
                lb.op("tensor", lambda E, c=c, k=k, o=o, w=w, ps=ps:
                      E.matmul(ps[:, :w], w1_b[:, k * 256 + c * 128:k * 256 + (c + 1) * 128],
                               Vb[k][:, o:o + w], start=(k == 0), stop=(k == nft - 1)),
                      r=[w1_b, Vb[k]] + ([ps] if k else []), w=[ps])
            lb.op("vector", lambda E, c=c, w=w, ps=ps:
                  E.tensor_scalar(ps[:, :w], ps[:, :w], b1c[:, c:c + 1], None, ALU.add),
                  r=[ps, b1c], w=[ps])
            lb.op("scalar", lambda E, c=c, o=o, w=w, ps=ps:
                  E.activation(Hw[c][:, o:o + w], ps[:, :w], AF.Silu), r=[ps], w=[Hw[c]])
            lb.op("tensor", lambda E, c=c, o=o, w=w:
                  E.matmul(ps8[0:128, :w], rep32[:, c * 128:(c + 1) * 128], we_fm[:, o:o + w],
                           start=True, stop=True),
                  r=[rep32, we_fm], w=[ps8])
            lb.op("vector", lambda E, c=c, o=o, w=w:
                  E.tensor_tensor(Hw[c][:, o:o + w], Hw[c][:, o:o + w], ps8[0:128, :w], ALU.mult),
                  r=[Hw[c], ps8], w=[Hw[c]])

    w2_f = wslab_f
    w2_b = lb.sb("w2_b", [128, 2 * 128], BF16)
    b2_b = lb.sb("b2_b", [8, D], BF16)
    lb.dma("sync", wslab_f[0:8, 0:D], b2_in)
    lb.op("vector", lambda E: E.tensor_copy(b2_b[:], wslab_f[0:8, 0:D]), r=[wslab_f], w=[b2_b])
    for f in range(nft):
        lb.dma("sync", w2_f[:, 0:256], w2_in[:, f * 128:(f + 1) * 128].rearrange("(j p) c -> p j c", p=128))
        lb.op("vector", lambda E: E.tensor_copy(w2_b[:], w2_f[:, 0:256]), r=[w2_f], w=[w2_b])
        for ci, (o, w) in enumerate(chunks(TPC, 512)):
            ps = psA[ci % 2]
            for c in range(2):
                lb.op("tensor", lambda E, f=f, c=c, o=o, w=w, ps=ps:
                      E.matmul(ps[:, :w], w2_b[:, c * 128:(c + 1) * 128], Hw[c][:, o:o + w],
                               start=(c == 0), stop=False),
                      r=[w2_b, Hw[c]] + ([ps] if c else []), w=[ps])
            lb.op("tensor", lambda E, f=f, o=o, w=w, ps=ps:
                  E.matmul(ps[:, :w], b2_b[:, f * 128:(f + 1) * 128], we_b[:, o:o + w],
                           start=False, stop=True),
                  r=[b2_b, we_b, ps], w=[ps])
            lb.op("vector", lambda E, f=f, o=o, w=w, ps=ps:
                  E.tensor_add(Vf[f][:, o:o + w], Vf[f][:, o:o + w], ps[:, :w]),
                  r=[Vf[f], ps], w=[Vf[f]])

    x1t = [Vb[0][:].bitcast(F32), Vb[1][:].bitcast(F32)]
    y2t = [Vb[2][:].bitcast(F32), Vb[3][:].bitcast(F32)]
    o2t = [Vb[4][:].bitcast(F32), Vb[5][:].bitcast(F32)]
    for i in range(ntt):
        xi, yi, oi = x1t[i % 2], y2t[i % 2], o2t[i % 2]
        lb.dma("sync", xi, x1_in[i * 128:(i + 1) * 128, :])
        io.load("y2_rows", yi, i * 128, (i + 1) * 128, 0, D)
        for f in range(nft):
            p = psA[f % 2]
            lb.op("tensor", lambda E, i=i, f=f, p=p:
                  E.transpose(p[:, 0:128], Vf[f][:, i * 128:(i + 1) * 128], sid_f[:]),
                  r=[Vf[f], sid_f], w=[p])
            lb.op("vector", lambda E, f=f, p=p, oi=oi: E.tensor_copy(oi[:, f * 128:(f + 1) * 128], p[:, 0:128]),
                  r=[p], w=[oi])
        lb.op("vector", lambda E, xi=xi, yi=yi: E.tensor_add(xi, xi, yi), r=[xi, yi], w=[xi])
        lb.op("vector", lambda E, xi=xi, oi=oi: E.tensor_add(oi, oi, xi), r=[oi, xi], w=[oi])
        lb.dma("gpsimd", o_out[i * 128:(i + 1) * 128, :], oi)



def build_l1ffn():
    lb = LB()
    io = IO(lb)
    emit_l1ffn(lb, io)
    lb.barrier_op("sync", lambda E: E.nop())
    return lb.finalize()




GRPS = [[0, 1, 2, 3], [4, 5, 6, 7]]


def build_fused():
    lb = LB()
    nc = lb.nc
    NR = 17 * 128
    nch = C2 // 4
    nch1 = D // 4

    E = {}
    E["x_rows"] = lb.inp("x_rows", [NR, D], F32)
    E["idf"] = lb.inp("ident_f32", [128, 128], F32)
    E["idb"] = lb.inp("ident_bf16", [128, 128], BF16)
    for i in range(4):
        E[f"b{i}_norm_w"] = lb.inp(f"b{i}_norm_w", [D], F32)
        E[f"b{i}_norm_b"] = lb.inp(f"b{i}_norm_b", [D], F32)
        E[f"b{i}_in_w"] = lb.inp(f"b{i}_in_w", [D, 2 * C2], F32)
        if i > 0:
            E[f"b{i}_ow"] = lb.inp(f"b{i}_ow", [C2, D], F32)
        E[f"b{i}_cw_sl"] = lb.inp(f"b{i}_cw_sl", [nch, 4], F32)
        E[f"b{i}_cb_cols"] = lb.inp(f"b{i}_cb_cols", [128, 4], F32)
        E[f"b{i}_xw_sl"] = lb.inp(f"b{i}_xw_sl", [nch, 96], F32)
        E[f"b{i}_dtw_sl"] = lb.inp(f"b{i}_dtw_sl", [RK, nch], F32)
        E[f"b{i}_dtb_cols"] = lb.inp(f"b{i}_dtb_cols", [128, 4], F32)
        E[f"b{i}_alog_sl"] = lb.inp(f"b{i}_alog_sl", [nch, NST], F32)
        E[f"b{i}_dp_cols"] = lb.inp(f"b{i}_dp_cols", [128, 4], F32)
    E["fin_ow"] = lb.inp("fin_ow", [C2, D], F32)
    E["fin_w"] = lb.inp("fin_w", [D], F32)
    E["fin_b"] = lb.inp("fin_b", [D], F32)
    E["w1t"] = lb.inp("w1t", [D, H], F32)
    E["w2t"] = lb.inp("w2t", [H, D], F32)
    E["cw_cols"] = lb.inp("cw_cols", [128, 96], F32)
    E["cb_cols32"] = lb.inp("cb_cols32", [128, 32], F32)
    E["wdT"] = lb.inp("wdT", [D, 96], F32)
    E["wdtT"] = lb.inp("wdtT", [RK, D], F32)
    E["bdt_cols"] = lb.inp("bdt_cols", [128, 8], F32)
    E["l1_alog_sl"] = lb.inp("l1_alog_sl", [nch1, NST], F32)
    E["l1_dp_cols"] = lb.inp("l1_dp_cols", [128, 2], F32)
    for nm, shp in [("WvT", [D, D]), ("bv_cols", [128, 8]), ("WgT", [D, 8]),
                    ("bg8", [8, 1]), ("WpT", [D, 64]), ("W1T", [D, 256]),
                    ("b1_cols", [128, 2]), ("W2n", [256, D]), ("b2stk", [8, D]),
                    ("g_r8", [8, 1]), ("b_r8", [8, 1]), ("g_c8", [8, 1]),
                    ("b_c8", [8, 1]), ("alpha1", [1, 1]), ("blk8c", [64, 8]),
                    ("rep32c", [8, 256])]:
        E[nm] = lb.inp(nm, shp, F32)
    out_rows = lb.out("out_rows", [TPC, D], F32)

    d_xattnW = lb.dram("d_xattnW", [D, 17 * 128], F32)
    d_xattn = d_xattnW[:, 1:TPC + 1]
    d_xnq = lb.dram("d_xnq", [D, NR], BF16)
    d_qinv = lb.dram("d_qinv", [1, NR], F32)
    d_u = lb.dram("d_u", [C2, TPC], BF16)
    d_sz = lb.dram("d_sz", [C2, TPC], BF16)
    g_u = lb.dram("g_u", [4 * C2, TPC], BF16)
    d_uc = lb.dram("d_uc", [nch, S], BF16)
    d_dblp = lb.dram("d_dblp", [96, S], F32)
    g_dblp = lb.dram("g_dblp", [4 * 96, S], F32)
    d_y = lb.dram("d_y", [nch, S], BF16)
    g_y = lb.dram("g_y", [C2, S], BF16)
    d_attn = lb.dram("d_attn", [TPC, D], F32)
    d_ffn = lb.dram("d_ffn", [TPC, D], F32)
    d_x1 = lb.dram("d_x1", [TPC, D], F32)
    d_xn2 = lb.dram("d_xn2", [D, TPC], BF16)
    d_dbl2 = lb.dram("d_dbl2", [96, TPC], F32)
    d_dt2 = lb.dram("d_dt2", [D, TPC], BF16)
    g_dt2 = lb.dram("g_dt2", [4 * D, TPC], BF16)
    g_xn2 = lb.dram("g_xn2", [4 * D, TPC], BF16)
    g_dbl2 = lb.dram("g_dbl2", [4 * 96, TPC], F32)
    d_y2 = lb.dram("d_y2", [S, nch1], F32)
    g_y2 = lb.dram("g_y2", [4 * S, nch1], F32)
    l_u = lb.dram("l_u", [nch, S], BF16)
    l_y = lb.dram("l_y", [C2, TPC], BF16)
    l_dt2 = lb.dram("l_dt2", [nch1, S], BF16)
    l_xn2 = lb.dram("l_xn2", [nch1, S], BF16)
    l_bc2 = lb.dram("l_bc2", [32, S], F32)
    l_y2 = lb.dram("l_y2", [TPC, D], F32)

    def gather(ins, outs):
        for i_ap, o_ap in zip(ins, outs):
            def fn(Eg, i_ap=i_ap, o_ap=o_ap):
                return Eg.collective_compute(
                    "AllGather", ALU.bypass, replica_groups=GRPS,
                    ins=[i_ap.opt()], outs=[o_ap.opt()])
            lb.op("gpsimd", fn, r=[i_ap], w=[o_ap], dma="coll", dma_stream="cc")

    def dyncopy_rows(dst, src, RR, gmul, roff=0):
        nrows = dst.shape[0]
        CT = src.shape[1]
        nr = src.shape[0] // RR
        s3 = src.rearrange("(r c) t -> c r t", c=RR)
        d3 = dst.rearrange("c (r t) -> c r t", r=nr)

        def fn(Ed, s3=s3, d3=d3, gmul=gmul, roff=roff, nrows=nrows):
            if gmul:
                sl = s3[bass.DynSlice(lb.dynmul(Ed, gmul) + roff if roff else
                                      lb.dynmul(Ed, gmul), nrows)]
            else:
                sl = s3[roff:roff + nrows]
            return Ed.dma_start(out=d3, in_=sl)
        lb.op("sync", fn, r=[src], w=[dst], dma=True, dma_stream=_key(dst))

    def dyncopy_cols(dst, src):
        def fn(Ed, src=src, dst=dst):
            return Ed.dma_start(
                out=dst, in_=src[:, bass.DynSlice(lb.dynmul(Ed, TPC), TPC)])
        lb.op("sync", fn, r=[src], w=[dst], dma=True, dma_stream=_key(dst))

    def dyncopy_y2(dst, src):
        BC = src.shape[1]
        s3 = src.rearrange("(b s) c -> s b c", s=S)
        d3 = dst.rearrange("t (b c) -> t b c", c=BC)

        def fn(Ed, s3=s3, d3=d3):
            return Ed.dma_start(
                out=d3, in_=s3[bass.DynSlice(lb.dynmul(Ed, TPC), TPC)])
        lb.op("sync", fn, r=[src], w=[dst], dma=True, dma_stream=_key(dst))

    lb.stage_open()
    emit_ln1(lb, IO(lb, dict(x_rows=E["x_rows"], ident_f32=E["idf"],
                             ident_bf16=E["idb"], xattn_fm=d_xattnW,
                             xnq_fm=d_xnq, qinv=d_qinv)), window=True)
    lb.stage_close()

    for i in range(4):
        p = dict(xattn=d_xattn, norm_w=E[f"b{i}_norm_w"], norm_b=E[f"b{i}_norm_b"],
                 in_w=E[f"b{i}_in_w"], u_cols=d_u, siluz_cols=d_sz)
        if i > 0:
            p["y_prev"] = l_y
            p["sz_prev"] = d_sz
            p["ow"] = E[f"b{i}_ow"]
            p["xattn_new"] = d_xattn
        emit_proj(lb, IO(lb, p), has_outproj=(i > 0))
        lb.stage_close()
        gather([d_u], [g_u])
        dyncopy_rows(l_u, g_u, C2, nch)
        emit_convx(lb, IO(lb, dict(
            u_sl=l_u,
            cw_sl=E[f"b{i}_cw_sl"], cb_cols=E[f"b{i}_cb_cols"],
            xw_sl=E[f"b{i}_xw_sl"], uc_sl=d_uc, dblp=d_dblp)))
        lb.stage_close()
        gather([d_dblp], [g_dblp])
        emit_scan_l0(lb, IO(lb, dict(
            dblp0=g_dblp[0:96], dblp1=g_dblp[96:192],
            dblp2=g_dblp[192:288], dblp3=g_dblp[288:384],
            uc_sl=d_uc,
            dtw_sl=E[f"b{i}_dtw_sl"], dtb_cols=E[f"b{i}_dtb_cols"],
            alog_sl=E[f"b{i}_alog_sl"], dp_cols=E[f"b{i}_dp_cols"],
            y_sl=d_y)))
        lb.stage_close()
        gather([d_y], [g_y])
        dyncopy_cols(l_y, g_y)

    emit_proj(lb, IO(lb, dict(
        xattn=d_xattn, y_prev=l_y, sz_prev=d_sz, ow=E["fin_ow"],
        xattn_new=d_xattn, norm_w=E["fin_w"], norm_b=E["fin_b"],
        ident_f32=E["idf"], attn_tm=d_attn)),
        has_outproj=True, has_inproj=False, final_ln=True)
    lb.stage_close()

    emit_ffn0(lb, IO(lb, dict(
        xnq_cols=d_xnq[:, 0:TPC + 2], qinv_cols=d_qinv[:, 0:TPC + 2],
        w1t=E["w1t"], w2t=E["w2t"], cw_cols=E["cw_cols"],
        cb_cols=E["cb_cols32"], ident_f32=E["idf"], ffn_tm=d_ffn)))
    lb.stage_close()

    emit_mid(lb, IO(lb, dict(
        x_rows=E["x_rows"][1:TPC + 1, :], attn_rows=d_attn, ffn_rows=d_ffn,
        wdT=E["wdT"], wdtT=E["wdtT"], bdt_cols=E["bdt_cols"],
        ident_bf16=E["idb"], x1_rows=d_x1, xn2_fm=d_xn2, dbl2=d_dbl2,
        dt2=d_dt2)))
    lb.stage_close()
    gather([d_dt2, d_xn2], [g_dt2, g_xn2])
    gather([d_dbl2], [g_dbl2])
    dyncopy_rows(l_dt2, g_dt2, D, nch1)
    dyncopy_rows(l_xn2, g_xn2, D, nch1)
    dyncopy_rows(l_bc2, g_dbl2, 96, 0, roff=RK)

    emit_scan_l1(lb, IO(lb, dict(
        dt_sl=l_dt2,
        x_sl=l_xn2,
        bc2=l_bc2,
        alog_sl=E["l1_alog_sl"], dp_cols=E["l1_dp_cols"],
        ident_f32=E["idf"], y_tm=d_y2)))
    lb.stage_close()
    gather([d_y2], [g_y2])
    dyncopy_y2(l_y2, g_y2)

    emit_l1ffn(lb, IO(lb, dict(
        xn2_cols=d_xn2, x1_rows=d_x1,
        y2_rows=l_y2,
        WvT=E["WvT"], bv_cols=E["bv_cols"], WgT=E["WgT"], bg8=E["bg8"],
        WpT=E["WpT"], W1T=E["W1T"], b1_cols=E["b1_cols"], W2n=E["W2n"],
        b2stk=E["b2stk"], g_r8=E["g_r8"], b_r8=E["b_r8"], g_c8=E["g_c8"],
        b_c8=E["b_c8"], alpha1=E["alpha1"], blk8c=E["blk8c"],
        rep32c=E["rep32c"], ident_f32=E["idf"], out_rows=out_rows)))

    lb.barrier_op("sync", lambda Eb: Eb.nop())
    return lb.finalize()


def kernel(**inputs):
    inp = {k: np.asarray(v) for k, v in inputs.items()}
    x = f32(inp["L0_attn_x"]).reshape(B, S, D)
    NR = 17 * 128
    nch = C2 // 4
    nch1 = D // 4
    nc = _get("fused", build_fused)
    blk8c = np.zeros((64, 8), np.float32)
    for e in range(8):
        blk8c[e * 8:(e + 1) * 8, e] = 1.0
    rep32c = np.zeros((8, 256), np.float32)
    for e in range(8):
        rep32c[e, e * 32:(e + 1) * 32] = 1.0
    W2n = f32(inp["L1_ffn_W2"]).transpose(0, 2, 1).reshape(256, D)
    cwc = f32(inp["L0_ffn_conv_w"][:, 0, :]).reshape(32, 128, 3).transpose(1, 0, 2).reshape(128, 96)
    im = []
    for c in range(NCORES):
        q, g = divmod(c, 4)
        xw = np.zeros((NR, D), np.float32)
        xw[1:TPC + 1] = x[q, g * TPC:(g + 1) * TPC]
        if g > 0:
            xw[0] = x[q, g * TPC - 1]
        if g < 3:
            xw[TPC + 1] = x[q, (g + 1) * TPC]
        sl = slice(g * nch, (g + 1) * nch)
        sl1 = slice(g * nch1, (g + 1) * nch1)
        d = dict(x_rows=xw, ident_f32=IDF_, ident_bf16=IDB_,
                 fin_ow=f32(inp["L0_attn_out_proj_w"][3]),
                 fin_w=f32(inp["L0_attn_fn_w"]), fin_b=f32(inp["L0_attn_fn_b"]),
                 w1t=f32(inp["L0_ffn_w1"].T), w2t=f32(inp["L0_ffn_w2"].T),
                 cw_cols=np.ascontiguousarray(cwc),
                 cb_cols32=_cols(inp["L0_ffn_conv_b"], 32),
                 wdT=f32(inp["L1_attn_W_deltaBC"].T),
                 wdtT=f32(inp["L1_attn_W_dt"].T),
                 bdt_cols=_cols(inp["L1_attn_b_dt"], 8),
                 l1_alog_sl=f32(inp["L1_attn_A_log"][sl1]),
                 l1_dp_cols=_cols(inp["L1_attn_D_param"][sl1], 2),
                 WvT=f32(inp["L1_ffn_Wv"].T),
                 bv_cols=_cols(inp["L1_ffn_bv"], 8),
                 WgT=f32(inp["L1_ffn_Wg"].T),
                 bg8=f32(inp["L1_ffn_bg"]).reshape(8, 1),
                 WpT=np.ascontiguousarray(f32(inp["L1_ffn_Wp"]).reshape(64, D).T),
                 W1T=np.ascontiguousarray(f32(inp["L1_ffn_W1"]).reshape(256, D).T),
                 b1_cols=_cols(inp["L1_ffn_b1"].reshape(-1), 2),
                 W2n=np.ascontiguousarray(W2n),
                 b2stk=f32(inp["L1_ffn_b2"]),
                 g_r8=f32(inp["L1_ffn_g_r"]).reshape(8, 1),
                 b_r8=f32(inp["L1_ffn_b_r"]).reshape(8, 1),
                 g_c8=f32(inp["L1_ffn_g_c"]).reshape(8, 1),
                 b_c8=f32(inp["L1_ffn_b_c"]).reshape(8, 1),
                 alpha1=f32(inp["L1_ffn_alpha"]).reshape(1, 1),
                 blk8c=blk8c, rep32c=rep32c)
        for i in range(4):
            d[f"b{i}_norm_w"] = f32(inp["L0_attn_norm_w"][i])
            d[f"b{i}_norm_b"] = f32(inp["L0_attn_norm_b"][i])
            d[f"b{i}_in_w"] = f32(inp["L0_attn_in_proj_w"][i])
            if i > 0:
                d[f"b{i}_ow"] = f32(inp["L0_attn_out_proj_w"][i - 1])
            d[f"b{i}_cw_sl"] = f32(inp["L0_attn_conv_w"][i][sl])
            d[f"b{i}_cb_cols"] = _cols(inp["L0_attn_conv_b"][i][sl], 4)
            d[f"b{i}_xw_sl"] = f32(inp["L0_attn_xproj_w"][i][sl])
            d[f"b{i}_dtw_sl"] = f32(inp["L0_attn_dtproj_w"][i][:, sl])
            d[f"b{i}_dtb_cols"] = _cols(inp["L0_attn_dtproj_b"][i][sl], 4)
            d[f"b{i}_alog_sl"] = f32(inp["L0_attn_A_log"][i][sl])
            d[f"b{i}_dp_cols"] = _cols(inp["L0_attn_skip_D"][i][sl], 4)
        im.append(d)
    res = _run(nc, im)
    OUT = np.concatenate([r["out_rows"] for r in res], axis=0)
    return OUT.reshape(B, S, D).astype(np.float32)


_cache = {}


def _get(name, builder):
    if name not in _cache:
        _cache[name] = builder()
    return _cache[name]


_DBG = {}
IDF_ = np.eye(128, dtype=np.float32)
IDB_ = np.eye(128, dtype=np.float32).astype(BF)


def _run(nc, in_maps):
    return run_bass_kernel_spmd(nc, in_maps, core_ids=list(range(NCORES))).results


def _chsl(c, nch):
    q, g = divmod(c, 4)
    return q, slice(g * nch, (g + 1) * nch)


def _cols(vec, nct):
    return np.ascontiguousarray(np.asarray(vec, np.float32).reshape(nct, 128).T)


def kernel_unfused(**inputs):
    inp = {k: np.asarray(v) for k, v in inputs.items()}
    x = f32(inp["L0_attn_x"]).reshape(NT, D)

    nc = _get("ln1", build_ln1)
    res = _run(nc, [dict(x_rows=np.ascontiguousarray(x[c * TPC:(c + 1) * TPC]),
                         ident_f32=IDF_, ident_bf16=IDB_) for c in range(NCORES)])
    XATTN = np.concatenate([r["xattn_fm"] for r in res], axis=1)
    XNQ = np.concatenate([r["xnq_fm"] for r in res], axis=1)
    QINV = np.concatenate([r["qinv"][0] for r in res])

    nch = C2 // 4
    M = None
    for i in range(4):
        key = "proj0" if i == 0 else "projN"
        nc = _get(key, lambda: build_proj(has_outproj=(i > 0)))
        im = []
        for c in range(NCORES):
            d = dict(xattn=np.ascontiguousarray(XATTN[:, c * TPC:(c + 1) * TPC]),
                     norm_w=f32(inp["L0_attn_norm_w"][i]),
                     norm_b=f32(inp["L0_attn_norm_b"][i]),
                     in_w=f32(inp["L0_attn_in_proj_w"][i]))
            if i > 0:
                d["m_prev"] = np.ascontiguousarray(M[:, c * TPC:(c + 1) * TPC])
                d["ow"] = f32(inp["L0_attn_out_proj_w"][i - 1])
            im.append(d)
        res = _run(nc, im)
        if i > 0:
            XATTN = np.concatenate([r["xattn_new"] for r in res], axis=1)
        U = np.concatenate([r["u_cols"] for r in res], axis=1)
        SZ = np.concatenate([r["siluz_cols"] for r in res], axis=1)

        nc = _get("convx", build_convx)
        im = []
        for c in range(NCORES):
            q, sl = _chsl(c, nch)
            im.append(dict(u_sl=np.ascontiguousarray(U[sl, q * S:(q + 1) * S]),
                           cw_sl=f32(inp["L0_attn_conv_w"][i][sl]),
                           cb_cols=_cols(inp["L0_attn_conv_b"][i][sl], 4),
                           xw_sl=f32(inp["L0_attn_xproj_w"][i][sl])))
        res = _run(nc, im)
        UC = [r["uc_sl"] for r in res]
        DBLP = [r["dblp"] for r in res]

        nc = _get("scan_l0", build_scan_l0)
        im = []
        for c in range(NCORES):
            q, sl = _chsl(c, nch)
            im.append(dict(dblp0=DBLP[q * 4 + 0], dblp1=DBLP[q * 4 + 1],
                           dblp2=DBLP[q * 4 + 2], dblp3=DBLP[q * 4 + 3],
                           uc_sl=UC[c],
                           sz_sl=np.ascontiguousarray(SZ[sl, q * S:(q + 1) * S]),
                           dtw_sl=f32(inp["L0_attn_dtproj_w"][i][:, sl]),
                           dtb_cols=_cols(inp["L0_attn_dtproj_b"][i][sl], 4),
                           alog_sl=f32(inp["L0_attn_A_log"][i][sl]),
                           dp_cols=_cols(inp["L0_attn_skip_D"][i][sl], 4)))
        res = _run(nc, im)
        M = np.empty((C2, NT), dtype=BF)
        for c in range(NCORES):
            q, sl = _chsl(c, nch)
            M[sl, q * S:(q + 1) * S] = res[c]["m_sl"]

    nc = _get("attnfin", lambda: build_proj(has_outproj=True, has_inproj=False, final_ln=True))
    im = [dict(xattn=np.ascontiguousarray(XATTN[:, c * TPC:(c + 1) * TPC]),
               m_prev=np.ascontiguousarray(M[:, c * TPC:(c + 1) * TPC]),
               ow=f32(inp["L0_attn_out_proj_w"][3]),
               norm_w=f32(inp["L0_attn_fn_w"]), norm_b=f32(inp["L0_attn_fn_b"]),
               ident_f32=IDF_) for c in range(NCORES)]
    res = _run(nc, im)
    ATTN = np.concatenate([r["attn_tm"] for r in res], axis=0)
    _DBG["ATTN"] = ATTN; _DBG["XATTN4"] = XATTN; _DBG["M3"] = M

    XNQp = np.pad(XNQ.reshape(D, B, S), ((0, 0), (0, 0), (1, 1)))
    QIp = np.pad(QINV.reshape(B, S), ((0, 0), (1, 1))).astype(np.float32)
    nc = _get("ffn0", build_ffn0)
    im = []
    cwc = f32(inp["L0_ffn_conv_w"][:, 0, :]).reshape(32, 128, 3).transpose(1, 0, 2).reshape(128, 96)
    for c in range(NCORES):
        q, g = divmod(c, 4)
        sl = slice(g * TPC, g * TPC + TPC + 2)
        im.append(dict(xnq_cols=np.ascontiguousarray(XNQp[:, q, sl]),
                       qinv_cols=np.ascontiguousarray(QIp[q, sl][None, :]),
                       w1t=f32(inp["L0_ffn_w1"].T), w2t=f32(inp["L0_ffn_w2"].T),
                       cw_cols=np.ascontiguousarray(cwc),
                       cb_cols=_cols(inp["L0_ffn_conv_b"], 32),
                       ident_f32=IDF_))
    res = _run(nc, im)
    FFN = np.concatenate([r["ffn_tm"] for r in res], axis=0)
    _DBG["FFN"] = FFN

    nc = _get("mid", build_mid)
    im = [dict(x_rows=np.ascontiguousarray(x[c * TPC:(c + 1) * TPC]),
               attn_rows=np.ascontiguousarray(ATTN[c * TPC:(c + 1) * TPC]),
               ffn_rows=np.ascontiguousarray(FFN[c * TPC:(c + 1) * TPC]),
               wdT=f32(inp["L1_attn_W_deltaBC"].T),
               wdtT=f32(inp["L1_attn_W_dt"].T),
               bdt_cols=_cols(inp["L1_attn_b_dt"], 8),
               ident_bf16=IDB_) for c in range(NCORES)]
    res = _run(nc, im)
    X1 = np.concatenate([r["x1_rows"] for r in res], axis=0)
    _DBG["X1"] = X1
    XN2 = np.concatenate([r["xn2_fm"] for r in res], axis=1)
    DBL2 = np.concatenate([r["dbl2"] for r in res], axis=1)
    DT2 = np.concatenate([r["dt2"] for r in res], axis=1)

    nch1 = D // 4
    nc = _get("scan_l1", build_scan_l1)
    im = []
    for c in range(NCORES):
        q, sl = _chsl(c, nch1)
        im.append(dict(dt_sl=np.ascontiguousarray(DT2[sl, q * S:(q + 1) * S]),
                       x_sl=np.ascontiguousarray(XN2[sl, q * S:(q + 1) * S]),
                       bc2=np.ascontiguousarray(DBL2[RK:RK + 32, q * S:(q + 1) * S]),
                       alog_sl=f32(inp["L1_attn_A_log"][sl]),
                       dp_cols=_cols(inp["L1_attn_D_param"][sl], 2),
                       ident_f32=IDF_))
    res = _run(nc, im)
    _DBG["DT2"] = DT2; _DBG["DBL2"] = DBL2
    Y2 = np.concatenate(
        [np.concatenate([res[q * 4 + g]["y_tm"] for g in range(4)], axis=1) for q in range(B)],
        axis=0)

    nc = _get("l1ffn", build_l1ffn)
    W2n = f32(inp["L1_ffn_W2"]).transpose(0, 2, 1).reshape(256, D)
    blk8c = np.zeros((64, 8), np.float32)
    for e in range(8):
        blk8c[e * 8:(e + 1) * 8, e] = 1.0
    rep32c = np.zeros((8, 256), np.float32)
    for e in range(8):
        rep32c[e, e * 32:(e + 1) * 32] = 1.0
    im = [dict(xn2_cols=np.ascontiguousarray(XN2[:, c * TPC:(c + 1) * TPC]),
               x1_rows=np.ascontiguousarray(X1[c * TPC:(c + 1) * TPC]),
               y2_rows=np.ascontiguousarray(Y2[c * TPC:(c + 1) * TPC]),
               WvT=f32(inp["L1_ffn_Wv"].T),
               bv_cols=_cols(inp["L1_ffn_bv"], 8),
               WgT=f32(inp["L1_ffn_Wg"].T),
               bg8=f32(inp["L1_ffn_bg"]).reshape(8, 1),
               WpT=f32(inp["L1_ffn_Wp"]).reshape(64, D).T.copy(),
               W1T=f32(inp["L1_ffn_W1"]).reshape(256, D).T.copy(),
               b1_cols=_cols(inp["L1_ffn_b1"].reshape(-1), 2),
               W2n=np.ascontiguousarray(W2n),
               b2stk=f32(inp["L1_ffn_b2"]),
               g_r8=f32(inp["L1_ffn_g_r"]).reshape(8, 1),
               b_r8=f32(inp["L1_ffn_b_r"]).reshape(8, 1),
               g_c8=f32(inp["L1_ffn_g_c"]).reshape(8, 1),
               b_c8=f32(inp["L1_ffn_b_c"]).reshape(8, 1),
               alpha1=f32(inp["L1_ffn_alpha"]).reshape(1, 1),
               blk8c=blk8c, rep32c=rep32c,
               ident_f32=IDF_) for c in range(NCORES)]
    res = _run(nc, im)
    OUT = np.concatenate([r["out_rows"] for r in res], axis=0)
    _DBG["Y2"] = Y2; _DBG["OUT"] = OUT
    return OUT.reshape(B, S, D).astype(np.float32)


# version 17
# speedup vs baseline: 1.1040x; 1.1040x over previous
import numpy as np
import ml_dtypes
from contextlib import ExitStack

import concourse.bass as bass
import concourse.mybir as mybir
from concourse.bass_utils import run_bass_kernel_spmd

F32 = mybir.dt.float32
BF16 = mybir.dt.bfloat16
I32 = mybir.dt.int32
AF = mybir.ActivationFunctionType
ALU = mybir.AluOpType
AX = mybir.AxisListType
BF = ml_dtypes.bfloat16

NCORES = 8
D = 1024
C2 = 2048
H = 4096
NST = 16
RK = 64
S = 8192
B = 2
NT = B * S
TPC = NT // NCORES
EPS = 1e-5

ENGS = ("sync", "scalar", "vector", "tensor", "gpsimd")


def f32(x):
    return np.ascontiguousarray(np.asarray(x, dtype=np.float32))


def _key(x):
    if hasattr(x, "name"):
        return x.name
    if hasattr(x, "tensor"):
        return x.tensor.name
    return str(x)


class LB:

    def __init__(self):
        self.nc = bass.Bass("TRN2", target_bir_lowering=False)
        self.es = ExitStack()
        self.stacks = [self.es]
        self.ops = {e: [] for e in ENGS}
        self.cnt = {e: 0 for e in ENGS}
        self.last_writer = {}
        self.readers = {}
        self.dma_cnt = {}
        self.stage_idx = 0
        self.active_streams = set()
        self.stream_slots = {}
        self.free_slots = []
        self.max_slot = 0
        self.fence = None
        self._gpos = {}
        self._slot_sem = {}
        self.slot_cnt = {}

    def inp(self, name, shape, dtype=F32):
        return self.nc.dram_tensor(name, list(shape), dtype, kind="ExternalInput").ap()

    def out(self, name, shape, dtype=F32):
        return self.nc.dram_tensor(name, list(shape), dtype, kind="ExternalOutput").ap()

    def dram(self, name, shape, dtype=F32):
        return self.nc.dram_tensor(f"{name}_u{uid()}", list(shape), dtype,
                                   kind="Internal").ap()

    def sb(self, name, shape, dtype=F32):
        return self.stacks[-1].enter_context(
            self.nc.sbuf_tensor(f"{name}_u{uid()}", list(shape), dtype))

    def ps(self, name, shape, dtype=F32):
        return self.stacks[-1].enter_context(
            self.nc.psum_tensor(f"{name}_u{uid()}", list(shape), dtype))

    def gpos(self, E, div=4):
        k = (str(getattr(E, "engine", E)), div)
        if k not in self._gpos:
            self._gpos[k] = E.partition_id() % div
        return self._gpos[k]

    def dynmul(self, E, mul):
        k = ("dm", str(getattr(E, "engine", E)), mul)
        if k not in self._gpos:
            self._gpos[k] = self.gpos(E) * mul
        return self._gpos[k]

    def _stream(self, key):
        if key not in self.dma_cnt:
            self.active_streams.add(key)
            slot = self.free_slots.pop() if self.free_slots else self.max_slot
            self.max_slot = max(self.max_slot, slot + 1)
            self.stream_slots[key] = slot
            self.dma_cnt[key] = self.slot_cnt.get(slot, 0)
        return key

    def op(self, eng, fn, r=(), w=(), dma=False, dma_stream=None, small=False):
        if dma:
            inc_amt = 1 if dma == "coll" else 16
            skey = (f"dma{self.stage_idx}:"
                    + (dma_stream if dma_stream is not None else _key(list(w)[0])))
            self._stream(skey)
            token = (skey, self.dma_cnt[skey] + inc_amt)
        else:
            inc_amt = 1
            token = (eng, self.cnt[eng] + 1)
        waits = {}

        def need(tok, same_ok=False):
            if tok is None:
                return
            s, v = tok
            if same_ok and s == eng:
                return
            if v > 0:
                waits[s] = max(waits.get(s, 0), v)

        def _small(x):
            try:
                sh = x.shape
                return (len(sh) < 2) or (int(np.prod(sh[1:])) < 64)
            except Exception:
                return True

        if self.fence is not None:
            need(self.fence)

        for x in r:
            k = _key(x)
            need(self.last_writer.get(k), same_ok=(not dma) and not (small or _small(x)))
        for x in w:
            k = _key(x)
            lw = self.last_writer.get(k)
            if lw is not None:
                if not (dma and lw[0] == token[0]):
                    need(lw, same_ok=(not dma) and not (small or _small(x)))
            for tok in self.readers.get(k, {}).values():
                need(tok, same_ok=(not dma) and not (small or _small(x)))
        for x in r:
            self.readers.setdefault(_key(x), {})[token[0]] = token
        for x in w:
            k = _key(x)
            self.last_writer[k] = token
            self.readers[k] = {}
        self.ops[eng].append((tuple(waits.items()), fn, token, inc_amt))
        if dma:
            self.dma_cnt[token[0]] += inc_amt
        else:
            self.cnt[eng] += 1

    def dma(self, eng, dst, src, dma_stream=None, **kw):
        self.op(eng, lambda E: E.dma_start(out=dst, in_=src, **kw),
                r=[src], w=[dst], dma=True, dma_stream=dma_stream)

    def cvec(self, val, dtype=F32):
        key = (float(val), dtype)
        if not hasattr(self, "_consts"):
            self._consts = {}
        if key not in self._consts:
            t = self.sb(f"cvec{len(self._consts)}_{uid()}", [128, 1], dtype)
            self.op("vector", lambda E: E.memset(t[:], val), w=[t])
            self._consts[key] = t
        return self._consts[key]

    def barrier_op(self, eng, fn):
        waits = [(e, self.cnt[e]) for e in ENGS if e != eng and self.cnt[e] > 0]
        waits += [(s, self.dma_cnt[s]) for s in self.active_streams
                  if self.dma_cnt[s] > 0]
        self.ops[eng].append((tuple(waits), fn, (eng, self.cnt[eng] + 1), 1))
        self.cnt[eng] += 1

    def stage_open(self):
        self.stacks.append(ExitStack())

    def stage_close(self):
        for e in ENGS:
            self.barrier_op(e, lambda E: E.nop())
        for skey in sorted(self.active_streams):
            slot = self.stream_slots[skey]
            self.slot_cnt[slot] = self.dma_cnt[skey]
            self.free_slots.append(slot)
        self.active_streams.clear()
        self.fence = ("sync", self.cnt["sync"])
        for k in list(self.last_writer):
            self.last_writer[k] = self.fence
        self.readers.clear()
        self._consts = {}
        st = self.stacks.pop()
        st.close()
        self.stage_idx += 1
        self.stage_open()

    def finalize(self):
        nc = self.nc
        sems = {}
        for e in ENGS:
            if self.ops[e] or any(s == e for ops in self.ops.values()
                                  for rec in ops for (s, _v) in rec[0]):
                sems[e] = self.es.enter_context(nc.semaphore(f"sem_{e}"))
        for slot in range(self.max_slot):
            self._slot_sem[slot] = self.es.enter_context(nc.semaphore(f"sd{slot}"))
        for skey, slot in self.stream_slots.items():
            sems[skey] = self._slot_sem[slot]
        crosswaits = {e: set() for e in ENGS}
        for eng in ENGS:
            for (ws, fn, tok, inc_) in self.ops[eng]:
                for (s, v) in ws:
                    if s in ENGS and s != eng:
                        crosswaits[s].add(v)
        block = self.es.enter_context(nc.Block())
        with self.es:
            for eng in ENGS:
                if not self.ops[eng]:
                    continue

                def make(eng=eng, opslist=self.ops[eng]):
                    def body(E):
                        watermark = {}
                        cw = crosswaits[eng]
                        drainable = eng in ("vector", "scalar", "gpsimd")
                        for (ws, fn, tok, inc) in opslist:
                            for (s, v) in ws:
                                if watermark.get(s, -1) >= v:
                                    continue
                                E.wait_ge(sems[s], v)
                                watermark[s] = v
                            sem_s, val = tok
                            inst = fn(E)
                            inst.then_inc(sems[sem_s], inc)
                    return body

                getattr(block, eng)(make())
        return nc




class IO:

    def __init__(self, lb, provided=None):
        self.lb = lb
        self.p = provided
        self._auto = {}

    def inp(self, name, shape, dtype=F32):
        if self.p is None:
            ap = self.lb.inp(name, shape, dtype)
            self._auto[name] = ap
            return ap
        return self.p[name]

    def out(self, name, shape, dtype=F32):
        if self.p is None:
            ap = self.lb.out(name, shape, dtype)
            self._auto[name] = ap
            return ap
        return self.p[name]

    def load(self, name, dst, r0, r1, t0, t1, eng="sync", src=None, **kw):
        lb = self.lb
        if src is None:
            src = self._auto[name] if self.p is None else self.p[name]
        if not isinstance(src, tuple):
            lb.dma(eng, dst, src[r0:r1, t0:t1], **kw)
            return
        kind = src[0]
        if kind == "gath":
            if len(src) == 5:
                _, ap, RR, gmul, CT = src
                roff = 0
            else:
                _, ap, RR, gmul, CT, roff = src
            r0, r1 = r0 + roff, r1 + roff
            r = t0 // CT
            while r * CT < t1:
                a, b = max(t0, r * CT), min(t1, (r + 1) * CT)
                dsl = dst if (a, b) == (t0, t1) else dst[:, (a - t0):(b - t0)]

                def mk(E, ap=ap, dsl=dsl, a=a, b=b, r=r, kw=kw, RR=RR,
                       gmul=gmul, r0=r0, r1=r1, CT=CT):
                    if gmul:
                        base = ap[r * RR + r0:r * RR + r0 + 3 * gmul + (r1 - r0)]
                        s = base[bass.DynSlice(self.lb.dynmul(E, gmul), r1 - r0),
                                 a - r * CT:b - r * CT]
                    else:
                        s = ap[r * RR + r0:r * RR + r1, a - r * CT:b - r * CT]
                    return E.dma_start(out=dsl, in_=s, **kw)
                lb.op(eng, mk, r=[ap], w=[dst], dma=True, dma_stream=_key(dst))
                r += 1
        elif kind == "gcols":
            _, ap, CT = src
            assert t0 == 0 and t1 == CT

            def mk(E, ap=ap, r0=r0, r1=r1, CT=CT, kw=kw):
                return E.dma_start(
                    out=dst,
                    in_=ap[r0:r1, bass.DynSlice(self.lb.dynmul(E, CT), CT)], **kw)
            lb.op(eng, mk, r=[ap], w=[dst], dma=True, dma_stream=_key(dst))
        elif kind == "ycols":
            _, ap, SEQ, OWN, NBLK, BC = src
            for blk in range(NBLK):
                def mk(E, ap=ap, blk=blk, r0=r0, r1=r1, kw=kw,
                       SEQ=SEQ, OWN=OWN, BC=BC):
                    base = ap[blk * SEQ + r0:blk * SEQ + r0 + 3 * OWN + (r1 - r0)]
                    s = base[bass.DynSlice(self.lb.dynmul(E, OWN), r1 - r0), :]
                    return E.dma_start(out=dst[:, blk * BC:(blk + 1) * BC],
                                       in_=s, **kw)
                lb.op(eng, mk, r=[ap], w=[dst], dma=True, dma_stream=_key(dst))
        else:
            raise KeyError(kind)


def chunks(total, size):
    out = []
    o = 0
    while o < total:
        c = min(size, total - o)
        out.append((o, c))
        o += c
    return out



_UID = [0]


def uid():
    _UID[0] += 1
    return _UID[0]


def emit_ln_tm(lb, xt, ot, sq, nf=D):
    u = uid()
    m = lb.sb(f"lnm{u}", [128, 1], F32)
    v = lb.sb(f"lnv{u}", [128, 1], F32)
    rr = lb.sb(f"lnr{u}", [128, 1], F32)
    lb.op("vector", lambda E: E.tensor_reduce(m[:], xt[:], AX.X, ALU.add), r=[xt], w=[m])
    lb.op("vector", lambda E: E.tensor_scalar(m[:], m[:], 1.0 / nf, None, ALU.mult), r=[m], w=[m])
    lb.op("scalar", lambda E: E.activation(sq[:], xt[:], AF.Square), r=[xt], w=[sq])
    lb.op("vector", lambda E: E.tensor_reduce(v[:], sq[:], AX.X, ALU.add), r=[sq], w=[v])
    lb.op("vector", lambda E: E.tensor_scalar(v[:], v[:], 1.0 / nf, None, ALU.mult), r=[v], w=[v])
    lb.op("scalar", lambda E: E.activation(rr[:], m[:], AF.Square), r=[m], w=[rr])
    lb.op("vector", lambda E: E.tensor_sub(v[:], v[:], rr[:]), r=[v, rr], w=[v])
    epsv = lb.cvec(EPS)
    lb.op("scalar", lambda E: E.activation(rr[:], v[:], AF.Sqrt, bias=epsv[:]), r=[v, epsv], w=[rr])
    lb.op("vector", lambda E: E.reciprocal(rr[:], rr[:]), r=[rr], w=[rr])
    lb.op("vector", lambda E: E.tensor_scalar(ot[:], xt[:], m[:], None, ALU.subtract),
          r=[xt, m], w=[ot])
    lb.op("vector", lambda E: E.tensor_scalar(ot[:], ot[:], rr[:], None, ALU.mult),
          r=[ot, rr], w=[ot])
    return m, rr



def emit_ln1(lb, io, window=False):
    ntt = 17 if window else 16
    NR = ntt * 128
    x = io.inp("x_rows", [NR, D], F32)
    idf = io.inp("ident_f32", [128, 128], F32)
    idb = io.inp("ident_bf16", [128, 128], BF16)
    o_xattn = io.out("xattn_fm", [D, NR if window else TPC], F32)
    o_xnq = io.out("xnq_fm", [D, NR], BF16)
    o_qinv = io.out("qinv", [1, NR], F32)
    sid_f = lb.sb("sid_f", [128, 128], F32)
    sid_b = lb.sb("sid_b", [128, 128], BF16)
    lb.dma("sync", sid_f[:], idf)
    lb.dma("sync", sid_b[:], idb)

    xt = [lb.sb(f"xt{i}", [128, D], F32) for i in range(ntt)]
    xq = [lb.sb(f"xq{i}", [128, D], BF16) for i in range(ntt)]
    sq = [lb.sb(f"sq{i}", [128, D], F32) for i in range(2)]
    tmp_i = [lb.sb(f"tmpi{i}", [128, D], I32) for i in range(2)]
    qinv_t = lb.sb("qinv_t", [128, ntt], F32)

    for i in range(ntt):
        lb.dma("sync", xt[i][:], x[i * 128:(i + 1) * 128, :])

    for i in range(ntt):
        s = sq[i % 2]
        emit_ln_tm(lb, xt[i], xt[i], s)
        am = lb.sb(f"am{i}", [128, 1], F32)
        sc = lb.sb(f"sc{i}", [128, 1], F32)
        ti = tmp_i[i % 2]
        lb.op("vector", lambda E, i=i, am=am: E.tensor_reduce(am[:], xt[i][:], AX.X, ALU.max,
                                                              apply_absolute_value=True),
              r=[xt[i]], w=[am])
        lb.op("vector", lambda E, am=am: E.tensor_scalar(am[:], am[:], 1e-5, None, ALU.max),
              r=[am], w=[am])
        lb.op("vector", lambda E, i=i, am=am: E.tensor_scalar(qinv_t[:, i:i + 1], am[:],
                                                              1.0 / 127.0, None, ALU.mult),
              r=[am], w=[qinv_t])
        lb.op("vector", lambda E, am=am, sc=sc: E.reciprocal(sc[:], am[:]), r=[am], w=[sc])
        lb.op("vector", lambda E, sc=sc: E.tensor_scalar(sc[:], sc[:], 127.0, None, ALU.mult),
              r=[sc], w=[sc])
        lb.op("vector", lambda E, i=i, sc=sc, s=s: E.tensor_scalar(s[:], xt[i][:], sc[:], None, ALU.mult),
              r=[xt[i], sc], w=[s])
        lb.op("vector", lambda E, s=s, ti=ti: E.tensor_copy(ti[:], s[:]), r=[s], w=[ti])
        lb.op("vector", lambda E, i=i, ti=ti: E.tensor_copy(xq[i][:], ti[:]), r=[ti], w=[xq[i]])

    ps_f = [lb.ps(f"ps_f{k}", [128, 128], F32) for k in range(2)]
    ps_b = [lb.ps(f"ps_b{k}", [128, 128], BF16) for k in range(2)]
    st_f = [lb.sb(f"st_f{k}", [128, 128], F32) for k in range(4)]
    st_b = [lb.sb(f"st_b{k}", [128, 128], BF16) for k in range(4)]
    k = 0
    for i in range(ntt):
        for j in range(D // 128):
            pf, sf = ps_f[k % 2], st_f[k % 4]
            pb, sb_ = ps_b[k % 2], st_b[k % 4]
            lb.op("tensor", lambda E, i=i, j=j, pf=pf: E.transpose(pf[:], xt[i][:, j * 128:(j + 1) * 128], sid_f[:]),
                  r=[xt[i], sid_f], w=[pf])
            lb.op("vector", lambda E, pf=pf, sf=sf: E.tensor_copy(sf[:], pf[:]), r=[pf], w=[sf])
            lb.dma("gpsimd", o_xattn[j * 128:(j + 1) * 128, i * 128:(i + 1) * 128], sf[:])
            lb.op("tensor", lambda E, i=i, j=j, pb=pb: E.transpose(pb[:], xq[i][:, j * 128:(j + 1) * 128], sid_b[:]),
                  r=[xq[i], sid_b], w=[pb])
            lb.op("scalar", lambda E, pb=pb, sb_=sb_: E.copy(sb_[:], pb[:]), r=[pb], w=[sb_])
            lb.dma("gpsimd", o_xnq[j * 128:(j + 1) * 128, i * 128:(i + 1) * 128], sb_[:])
            k += 1

    lb.dma("gpsimd", o_qinv.rearrange("o (i p) -> o p i", p=128).squeeze(0), qinv_t[:],
           allow_slow_non_contiguous=True)


def build_ln1():
    lb = LB()
    io = IO(lb)
    emit_ln1(lb, io, window=False)
    lb.barrier_op("sync", lambda E: E.nop())
    return lb.finalize()



def emit_fm_ln(lb, xat, out_tiles, out_dtype, nw=None, nb=None, tlen=TPC, gb_tile=None):
    u = uid()
    nc = lb.nc
    nft = len(xat)
    nf = nft * 128
    ones_b = lb.cvec(1.0, BF16)
    ps_s = lb.ps(f"lnps_s{u}", [1, 512], F32)
    ps_q = lb.ps(f"lnps_q{u}", [1, 512], F32)
    xb = lb.sb(f"lnxb{u}", [128, 512], BF16)
    sqb = lb.sb(f"lnsq{u}", [128, 512], BF16)
    st1 = lb.sb(f"lnst1{u}", [1, 512], F32)
    st2 = lb.sb(f"lnst2{u}", [1, 512], F32)
    scr = lb.dram(f"lnscr{u}", [2, tlen], F32)
    mbc = lb.sb(f"lnmbc{u}", [128, 512], F32)
    rbc = lb.sb(f"lnrbc{u}", [128, 512], F32)
    tmps = [lb.sb(f"lnap{u}_{k}", [128, 512], F32) for k in range(2)]
    epsv1 = lb.cvec(EPS)
    for (o, w) in chunks(tlen, 512):
        sl = slice(o, o + w)
        for j in range(nft):
            lb.op("vector", lambda E, j=j, sl=sl, w=w: E.tensor_copy(xb[:, :w], xat[j][:, sl]),
                  r=[xat[j]], w=[xb])
            lb.op("scalar", lambda E, j=j, sl=sl, w=w: E.activation(sqb[:, :w], xat[j][:, sl], AF.Square),
                  r=[xat[j]], w=[sqb])
            lb.op("tensor", lambda E, j=j, sl=sl, w=w: E.matmul(ps_s[:, :w], ones_b[:], xb[:, :w],
                                                    start=(j == 0), stop=(j == nft - 1)),
                  r=[ones_b, xb] + ([ps_s] if j else []), w=[ps_s])
            lb.op("tensor", lambda E, j=j, sl=sl, w=w: E.matmul(ps_q[:, :w], ones_b[:], sqb[:, :w],
                                                    start=(j == 0), stop=(j == nft - 1)),
                  r=[ones_b, sqb] + ([ps_q] if j else []), w=[ps_q])
        lb.op("vector", lambda E, sl=sl, w=w: E.tensor_scalar(st1[:, :w], ps_s[:, :w], 1.0 / nf, None, ALU.mult),
              r=[ps_s], w=[st1])
        lb.op("vector", lambda E, sl=sl, w=w: E.tensor_scalar(st2[:, :w], ps_q[:, :w], 1.0 / nf, None, ALU.mult),
              r=[ps_q], w=[st2])
        lb.op("scalar", lambda E, sl=sl, w=w: E.activation(sqb[:1, :w], st1[:, :w], AF.Square),
              r=[st1], w=[sqb])
        lb.op("vector", lambda E, sl=sl, w=w: E.tensor_sub(st2[:, :w], st2[:, :w], sqb[:1, :w]),
              r=[st2, sqb], w=[st2])
        lb.op("scalar", lambda E, sl=sl, w=w: E.activation(st2[:, :w], st2[:, :w], AF.Sqrt, bias=epsv1[:1]),
              r=[st2, epsv1], w=[st2])
        lb.op("vector", lambda E, sl=sl, w=w: E.reciprocal(st2[:, :w], st2[:, :w]), r=[st2], w=[st2])
        lb.dma("scalar", scr[0:1, sl], st1[:, :w])
        lb.dma("scalar", scr[1:2, sl], st2[:, :w])
        lb.dma("scalar", mbc[:, :w], scr[0:1, sl].broadcast_to([128, w]))
        lb.dma("scalar", rbc[:, :w], scr[1:2, sl].broadcast_to([128, w]))
        for j in range(nft):
            tmpa = tmps[j % 2]
            lb.op("vector", lambda E, j=j, tmpa=tmpa, sl=sl, w=w: E.tensor_tensor(tmpa[:, :w], xat[j][:, sl], mbc[:, :w], ALU.subtract),
                  r=[xat[j], mbc], w=[tmpa])
            lb.op("vector", lambda E, j=j, tmpa=tmpa, sl=sl, w=w: E.tensor_tensor(tmpa[:, :w], tmpa[:, :w], rbc[:, :w], ALU.mult),
                  r=[tmpa, rbc], w=[tmpa])
            if gb_tile is not None:
                lb.op("vector", lambda E, j=j, tmpa=tmpa, sl=sl, w=w:
                      E.tensor_scalar(out_tiles[j][:, sl], tmpa[:, :w],
                                      gb_tile[:, j:j + 1], gb_tile[:, nft + j:nft + j + 1],
                                      ALU.mult, ALU.add),
                      r=[tmpa, gb_tile], w=[out_tiles[j]])
            else:
                lb.op("vector", lambda E, j=j, tmpa=tmpa, sl=sl, w=w: E.tensor_copy(out_tiles[j][:, sl], tmpa[:, :w]),
                      r=[tmpa], w=[out_tiles[j]])


def load_vec_to_cols(lb, dst_tile, src_vec, n, ncols):
    lb.dma("sync", dst_tile[:, 0:ncols],
           src_vec.rearrange("(j p) -> p j", p=128), allow_slow_non_contiguous=True)



def emit_proj(lb, io, has_outproj, has_inproj=True, final_ln=False):
    xattn_in = io.inp("xattn", [D, TPC], F32)
    nft = D // 128
    xat = [lb.sb(f"xat{j}", [128, TPC], F32) for j in range(nft)]
    for j in range(nft):
        lb.dma("sync", xat[j][:], xattn_in[j * 128:(j + 1) * 128, :])

    if has_outproj:
        io.inp("y_prev", [C2, TPC], BF16)
        io.inp("sz_prev", [C2, TPC], BF16)
        ow_in = io.inp("ow", [C2, D], F32)
        o_xattn = io.out("xattn_new", [D, TPC], F32)
        nkt = C2 // 128
        mt = [lb.sb(f"mt{k}", [128, TPC], BF16) for k in range(nkt)]
        szt = lb.sb("szt", [128, TPC], BF16)
        for k in range(nkt):
            io.load("y_prev", mt[k][:], k * 128, (k + 1) * 128, 0, TPC)
            io.load("sz_prev", szt[:], k * 128, (k + 1) * 128, 0, TPC)
            lb.op("vector", lambda E, k=k: E.tensor_mul(mt[k][:], mt[k][:], szt[:]),
                  r=[mt[k], szt], w=[mt[k]])
        ow_f = lb.sb("ow_f", [128, nkt * 128], F32)
        ow_b = lb.sb("ow_b", [128, nkt * 128], BF16)
        psB = [lb.ps(f"psB{t}", [128, 512], F32) for t in range(4)]
        for j in range(nft):
            lb.dma("sync", ow_f[:], ow_in[:, j * 128:(j + 1) * 128].rearrange("(k p) c -> p k c", p=128))
            lb.op("vector", lambda E: E.tensor_copy(ow_b[:], ow_f[:]), r=[ow_f], w=[ow_b])
            for t in range(4):
                for k in range(nkt):
                    lb.op("tensor", lambda E, j=j, t=t, k=k:
                          E.matmul(psB[t][:], ow_b[:, k * 128:(k + 1) * 128],
                                   mt[k][:, t * 512:(t + 1) * 512],
                                   start=(k == 0), stop=(k == nkt - 1)),
                          r=[ow_b, mt[k]] + ([psB[t]] if k else []), w=[psB[t]])
            for t in range(4):
                lb.op("vector", lambda E, j=j, t=t:
                      E.tensor_tensor(xat[j][:, t * 512:(t + 1) * 512],
                                      xat[j][:, t * 512:(t + 1) * 512], psB[t][:], ALU.add),
                      r=[xat[j], psB[t]], w=[xat[j]])
            lb.dma("gpsimd", o_xattn[j * 128:(j + 1) * 128, :], xat[j][:])

    nw_in = io.inp("norm_w", [D], F32)
    nb_in = io.inp("norm_b", [D], F32)
    gb = lb.sb("gb", [128, 2 * nft], F32)
    load_vec_to_cols(lb, gb, nw_in, D, nft)
    lb.dma("sync", gb[:, nft:2 * nft], nb_in.rearrange("(j p) -> p j", p=128),
           allow_slow_non_contiguous=True)

    if final_ln:
        idf = io.inp("ident_f32", [128, 128], F32)
        sid_f = lb.sb("sid_f", [128, 128], F32)
        lb.dma("sync", sid_f[:], idf)
        o_attn = io.out("attn_tm", [TPC, D], F32)
        at = xat
        emit_fm_ln(lb, xat, at, F32, gb_tile=gb)
        ps_t = [lb.ps(f"pstr{k}", [128, 128], F32) for k in range(2)]
        st_t = [lb.sb(f"sttr{k}", [128, 128], F32) for k in range(4)]
        kk = 0
        for j in range(nft):
            for i in range(TPC // 128):
                p, s = ps_t[kk % 2], st_t[kk % 4]
                lb.op("tensor", lambda E, j=j, i=i, p=p:
                      E.transpose(p[:], at[j][:, i * 128:(i + 1) * 128], sid_f[:]),
                      r=[at[j], sid_f], w=[p])
                lb.op("vector", lambda E, p=p, s=s: E.tensor_copy(s[:], p[:]), r=[p], w=[s])
                lb.dma("gpsimd", o_attn[i * 128:(i + 1) * 128, j * 128:(j + 1) * 128], s[:])
                kk += 1
        return

    xn_blk = [lb.sb(f"xnb{j}", [128, TPC], BF16) for j in range(nft)]
    emit_fm_ln(lb, xat, xn_blk, BF16, gb_tile=gb)

    if has_inproj:
        iw_in = io.inp("in_w", [D, 2 * C2], F32)
        o_u = io.out("u_cols", [C2, TPC], BF16)
        o_z = io.out("siluz_cols", [C2, TPC], BF16)
        w_f = [lb.sb(f"ipwf{k}", [128, nft * 128], F32) for k in range(2)]
        w_b = [lb.sb(f"ipwb{k}", [128, nft * 128], BF16) for k in range(2)]
        psA = psB if has_outproj else [lb.ps(f"psA{t}", [128, 512], F32) for t in range(4)]
        stg = [lb.sb(f"ipst{t}", [128, 512], BF16) for t in range(4)]
        nct = 2 * C2 // 128
        for c in range(nct):
            wf, wb = w_f[c % 2], w_b[c % 2]
            lb.dma("sync", wf[:], iw_in[:, c * 128:(c + 1) * 128].rearrange("(j p) c -> p j c", p=128))
            lb.op("vector", lambda E, wf=wf, wb=wb: E.tensor_copy(wb[:], wf[:]), r=[wf], w=[wb])
            for k in range(nft):
                for t in range(4):
                    lb.op("tensor", lambda E, c=c, k=k, t=t, wb=wb:
                          E.matmul(psA[t][:], wb[:, k * 128:(k + 1) * 128],
                                   xn_blk[k][:, t * 512:(t + 1) * 512],
                                   start=(k == 0), stop=(k == nft - 1)),
                          r=[wb, xn_blk[k]] + ([psA[t]] if k else []), w=[psA[t]])
            for t in range(4):
                s = stg[t]
                if c < nct // 2:
                    lb.op("vector", lambda E, t=t, s=s: E.tensor_copy(s[:], psA[t][:]),
                          r=[psA[t]], w=[s])
                    lb.dma("gpsimd", o_u[c * 128:(c + 1) * 128, t * 512:(t + 1) * 512], s[:])
                else:
                    lb.op("scalar", lambda E, t=t, s=s: E.activation(s[:], psA[t][:], AF.Silu),
                          r=[psA[t]], w=[s])
                    lb.dma("gpsimd", o_z[(c - nct // 2) * 128:(c - nct // 2 + 1) * 128,
                                         t * 512:(t + 1) * 512], s[:])



def build_proj(has_outproj, has_inproj=True, final_ln=False):
    lb = LB()
    io = IO(lb)
    emit_proj(lb, io, has_outproj, has_inproj, final_ln)
    lb.barrier_op("sync", lambda E: E.nop())
    return lb.finalize()



def emit_convx(lb, io):
    nch = C2 // 4
    nct = nch // 128
    io.inp("u_sl", [nch, S], BF16)
    cw_in = io.inp("cw_sl", [nch, 4], F32)
    cb_in = io.inp("cb_cols", [128, nct], F32)
    xw_in = io.inp("xw_sl", [nch, 96], F32)
    o_uc = io.out("uc_sl", [nch, S], BF16)
    o_dblp = io.out("dblp", [96, S], F32)

    ut = [lb.sb(f"ut{j}", [128, 3 + S], BF16) for j in range(nct)]
    uct = [lb.sb(f"uct{j}", [128, S], BF16) for j in range(nct)]
    cw = lb.sb("cw", [128, 4 * nct], F32)
    cb = lb.sb("cb", [128, nct], F32)
    xw_f = lb.sb("xw_f", [128, 96 * nct], F32)
    xw_b = lb.sb("xw_b", [128, 96 * nct], BF16)
    cvt = [lb.sb(f"cvt{k}", [128, 2048], F32) for k in range(2)]
    ps_x = [lb.ps(f"ps_x{k}", [96, 512], F32) for k in range(2)]
    stx = [lb.sb(f"stx{k}", [96, 512], F32) for k in range(2)]

    lb.dma("sync", cb[:], cb_in)
    for j in range(nct):
        lb.op("vector", lambda E, j=j: E.memset(ut[j][:, 0:3], 0.0), w=[ut[j]])
        io.load("u_sl", ut[j][:, 3:], j * 128, (j + 1) * 128, 0, S)
        lb.dma("sync", cw[:, j * 4:(j + 1) * 4], cw_in[j * 128:(j + 1) * 128, :])
        lb.dma("sync", xw_f[:, j * 96:(j + 1) * 96], xw_in[j * 128:(j + 1) * 128, :])
    lb.op("vector", lambda E: E.tensor_copy(xw_b[:], xw_f[:]), r=[xw_f], w=[xw_b])

    for j in range(nct):
        for (o, w) in chunks(S, 2048):
            cv = cvt[(j * 4 + o // 2048) % 2]
            lb.op("vector", lambda E, j=j, o=o, w=w, cv=cv:
                  E.tensor_scalar(cv[:, :w], ut[j][:, o + 3:o + 3 + w],
                                  cw[:, j * 4 + 3:j * 4 + 4], cb[:, j:j + 1],
                                  ALU.mult, ALU.add),
                  r=[ut[j], cw, cb], w=[cv])
            for k in (2, 1, 0):
                lb.op("vector", lambda E, j=j, o=o, w=w, k=k, cv=cv:
                      E.scalar_tensor_tensor(cv[:, :w], ut[j][:, o + k:o + k + w],
                                             cw[:, j * 4 + k:j * 4 + k + 1], cv[:, :w],
                                             ALU.mult, ALU.add),
                      r=[ut[j], cw, cv], w=[cv])
            lb.op("scalar", lambda E, j=j, o=o, w=w, cv=cv:
                  E.activation(uct[j][:, o:o + w], cv[:, :w], AF.Silu),
                  r=[cv], w=[uct[j]])
        lb.dma("gpsimd", o_uc[j * 128:(j + 1) * 128, :], uct[j][:])

    for ci, (o, w) in enumerate(chunks(S, 512)):
        ps = ps_x[ci % 2]
        st = stx[ci % 2]
        for j in range(nct):
            lb.op("tensor", lambda E, j=j, o=o, w=w, ps=ps:
                  E.matmul(ps[:, :w], xw_b[:, j * 96:(j + 1) * 96], uct[j][:, o:o + w],
                           start=(j == 0), stop=(j == nct - 1)),
                  r=[xw_b, uct[j]] + ([ps] if j else []), w=[ps])
        lb.op("vector", lambda E, ps=ps, st=st, w=w: E.tensor_copy(st[:, :w], ps[:, :w]),
              r=[ps], w=[st])
        lb.dma("gpsimd", o_dblp[:, o:o + w], st[:, :w])



def build_convx():
    lb = LB()
    io = IO(lb)
    emit_convx(lb, io)
    lb.barrier_op("sync", lambda E: E.nop())
    return lb.finalize()



TCH = 1024


def emit_scan_core(lb, io, nch, dt_name, u_name, bc_scr, s_A, post,
                   dt_src=None, dt_key=None):
    nct = nch // 128
    u = uid()
    Bbc_n = [lb.sb(f"Bbc{u}_{n}", [128, TCH], BF16) for n in range(16)]
    Cbc_n = [lb.sb(f"Cbc{u}_{n}", [128, TCH], BF16) for n in range(16)]
    hC = lb.sb(f"hC{u}", [128, 16 * TCH], BF16)
    dAt_r = [lb.sb(f"dAt{u}_{k}", [128, TCH], F32) for k in range(3)]
    dBu_r = [lb.sb(f"dBu{u}_{k}", [128, TCH], BF16) for k in range(3)]
    ht_r = [lb.sb(f"ht{u}_{k}", [128, TCH], BF16) for k in range(3)]
    st_state = lb.sb(f"stst{u}", [128, nct * 16], F32)
    dt_j = [lb.sb(f"dtj{u}_{k}", [128, TCH], BF16) for k in range(2)]
    u_j = [lb.sb(f"uj{u}_{k}", [128, TCH], BF16) for k in range(2)]
    w_j = lb.sb(f"wj{u}", [128, TCH], BF16)
    y_j = lb.sb(f"yj{u}", [128, TCH], F32)
    lb.op("vector", lambda E: E.memset(st_state[:], 0.0), w=[st_state])

    cl = chunks(S, TCH)

    def bload(n, ci2):
        o2, w2 = cl[ci2]
        lb.dma("sync", Bbc_n[n][:, :w2],
               bc_scr[n:n + 1, o2:o2 + w2].broadcast_to([128, w2]))

    def cload(n, ci2):
        o2, w2 = cl[ci2]
        lb.dma("sync", Cbc_n[n][:, :w2],
               bc_scr[16 + n:17 + n, o2:o2 + w2].broadcast_to([128, w2]))

    for ci, (o, w) in enumerate(cl):
        if ci == 0:
            for n in range(16):
                bload(n, 0)
                cload(n, 0)
        for j in range(nct):
            dtj, uj = dt_j[j % 2], u_j[j % 2]
            if dt_key is not None:
                lb.op("sync", lambda E, dtj=dtj, j=j, o=o, w=w:
                      E.dma_start(out=dtj[:, :w],
                                  in_=dt_src[j * 128:(j + 1) * 128, o:o + w]),
                      r=[dt_key(j, ci)], w=[dtj], dma=True,
                      dma_stream=f"dtld{u}_{j % 2}")
            else:
                io.load(dt_name, dtj[:, :w], j * 128, (j + 1) * 128, o, o + w,
                        src=dt_src)
            io.load(u_name, uj[:, :w], j * 128, (j + 1) * 128, o, o + w)
            lb.op("vector", lambda E, dtj=dtj, uj=uj, w=w:
                  E.tensor_mul(w_j[:, :w], dtj[:, :w], uj[:, :w]),
                  r=[dtj, uj], w=[w_j])
            for n in range(16):
                col = j * 16 + n
                dAt, dBu, ht = dAt_r[n % 3], dBu_r[n % 3], ht_r[n % 3]
                lb.op("scalar", lambda E, dtj=dtj, w=w, col=col, dAt=dAt:
                      E.activation(dAt[:, :w], dtj[:, :w], AF.Exp,
                                   scale=s_A[:, col:col + 1]),
                      r=[dtj, s_A], w=[dAt])
                lb.op("vector", lambda E, n=n, w=w, dBu=dBu:
                      E.tensor_mul(dBu[:, :w], w_j[:, :w], Bbc_n[n][:, :w]),
                      r=[w_j, Bbc_n[n]], w=[dBu])
                if j == nct - 1 and ci + 1 < len(cl):
                    bload(n, ci + 1)
                lb.op("vector", lambda E, w=w, col=col, dAt=dAt, dBu=dBu, ht=ht:
                      E.tensor_tensor_scan(ht[:, :w], dAt[:, :w], dBu[:, :w],
                                           st_state[:, col:col + 1], ALU.mult, ALU.add),
                      r=[dAt, dBu, st_state], w=[ht])
                lb.op("vector", lambda E, w=w, col=col, ht=ht:
                      E.tensor_copy(st_state[:, col:col + 1], ht[:, w - 1:w]),
                      r=[ht], w=[st_state])
                lb.op("vector", lambda E, n=n, w=w, ht=ht:
                      E.tensor_mul(hC[:, n * TCH:n * TCH + w], ht[:, :w],
                                   Cbc_n[n][:, :w]),
                      r=[ht, Cbc_n[n]], w=[hC])
                if j == nct - 1 and ci + 1 < len(cl):
                    cload(n, ci + 1)
            lb.op("vector", lambda E:
                  E.tensor_add(hC[:, :8 * TCH], hC[:, :8 * TCH], hC[:, 8 * TCH:]),
                  r=[hC], w=[hC])
            lb.op("vector", lambda E:
                  E.tensor_add(hC[:, :4 * TCH], hC[:, :4 * TCH],
                               hC[:, 4 * TCH:8 * TCH]),
                  r=[hC], w=[hC])
            lb.op("vector", lambda E:
                  E.tensor_add(hC[:, :2 * TCH], hC[:, :2 * TCH],
                               hC[:, 2 * TCH:4 * TCH]),
                  r=[hC], w=[hC])
            lb.op("vector", lambda E, w=w:
                  E.tensor_reduce(y_j[:, :w],
                                  hC.rearrange("p (n t) -> p t n", n=16)[:, :w, 0:2],
                                  AX.X, ALU.add),
                  r=[hC], w=[y_j])
            post(j, ci, o, w, y_j, uj)
    return st_state



def emit_scan_l0(lb, io):
    nch = C2 // 4
    nct = nch // 128
    dblp_in = [io.inp(f"dblp{q}", [96, S], F32) for q in range(4)]
    io.inp("uc_sl", [nch, S], BF16)
    dtw_in = io.inp("dtw_sl", [RK, nch], F32)
    dtb_in = io.inp("dtb_cols", [128, nct], F32)
    alog_in = io.inp("alog_sl", [nch, NST], F32)
    dp_in = io.inp("dp_cols", [128, nct], F32)
    o_m = io.out("y_sl", [nch, S], BF16)

    acc = lb.sb("acc", [96, S], F32)
    ldb = lb.sb("ldb", [96, 2048], F32)
    lb.dma("sync", acc[:], dblp_in[0])
    for q in range(1, 4):
        for (o, w) in chunks(S, 2048):
            lb.dma("sync", ldb[:, :w], dblp_in[q][:, o:o + w])
            lb.op("vector", lambda E, o=o, w=w: E.tensor_add(acc[:, o:o + w], acc[:, o:o + w], ldb[:, :w]),
                  r=[acc, ldb], w=[acc])
    dbl_b = lb.sb("dbl_b", [96, S], BF16)
    lb.op("vector", lambda E: E.tensor_copy(dbl_b[:], acc[:]), r=[acc], w=[dbl_b])
    bc_scr = lb.dram("bc_scr", [32, S], BF16)
    lb.dma("gpsimd", bc_scr, dbl_b[64:96, :])

    dt_scr = lb.dram("dt_scr", [nch, S], BF16)
    dtw_f = lb.sb("dtw_f", [RK, nch], F32)
    dtw_b = lb.sb("dtw_b", [RK, nch], BF16)
    lb.dma("sync", dtw_f[:], dtw_in)
    lb.op("vector", lambda E: E.tensor_copy(dtw_b[:], dtw_f[:]), r=[dtw_f], w=[dtw_b])
    dtb = lb.sb("dtb", [128, nct], F32)
    lb.dma("sync", dtb[:], dtb_in)
    ps_dt = [lb.ps(f"ps_dt{k}", [128, 512], F32) for k in range(2)]
    sp_e = [lb.sb(f"sp_e{k}", [128, 512], F32) for k in range(2)]
    dt_st = [lb.sb(f"dt_st{k}", [128, 512], BF16) for k in range(2)]
    onev = lb.cvec(1.0)
    su = uid()
    it = 0
    for cg in range(S // TCH):
        for j in range(nct):
            for sub in range(2):
                o = (cg * 2 + sub) * 512
                w = 512
                ps, se, dst = ps_dt[it % 2], sp_e[it % 2], dt_st[it % 2]
                it += 1
                lb.op("tensor", lambda E, j=j, o=o, w=w, ps=ps:
                      E.matmul(ps[:, :w], dtw_b[:, j * 128:(j + 1) * 128],
                               dbl_b[0:RK, o:o + w], start=True, stop=True),
                      r=[dtw_b, dbl_b], w=[ps])
                lb.op("scalar", lambda E, j=j, w=w, ps=ps, se=se:
                      E.activation(se[:, :w], ps[:, :w], AF.Exp,
                                   bias=dtb[:, j:j + 1]),
                      r=[ps, dtb], w=[se])
                lb.op("scalar", lambda E, w=w, se=se, dst=dst:
                      E.activation(dst[:, :w], se[:, :w], AF.Ln, bias=onev[:]),
                      r=[se, onev], w=[dst])
                lb.op("gpsimd", lambda E, j=j, o=o, w=w, dst=dst:
                      E.dma_start(out=dt_scr[j * 128:(j + 1) * 128, o:o + w],
                                  in_=dst[:, :w]),
                      r=[dst], w=[f"dtk{su}_{j}_{cg}"], dma=True,
                      dma_stream=f"dtw{su}_{j}")

    s_A = lb.sb("s_A", [128, nct * 16], F32)
    for j in range(nct):
        lb.dma("sync", s_A[:, j * 16:(j + 1) * 16], alog_in[j * 128:(j + 1) * 128, :])
    lb.op("scalar", lambda E: E.activation(s_A[:], s_A[:], AF.Exp), r=[s_A], w=[s_A])
    lb.op("vector", lambda E: E.tensor_scalar(s_A[:], s_A[:], -1.0, None, ALU.mult),
          r=[s_A], w=[s_A])
    dp = lb.sb("dp", [128, nct], F32)
    lb.dma("sync", dp[:], dp_in)

    m_st = lb.sb("m_st", [128, TCH], BF16)

    def post(j, ci, o, w, y_j, uj):
        lb.op("vector", lambda E, j=j, w=w, uj=uj:
              E.scalar_tensor_tensor(m_st[:, :w], uj[:, :w], dp[:, j:j + 1], y_j[:, :w],
                                     ALU.mult, ALU.add),
              r=[uj, dp, y_j], w=[m_st])
        lb.dma("gpsimd", o_m[j * 128:(j + 1) * 128, o:o + w], m_st[:, :w])

    emit_scan_core(lb, io, nch, None, "uc_sl", bc_scr, s_A, post, dt_src=dt_scr,
                   dt_key=lambda j, ci: f"dtk{su}_{j}_{ci}")


def build_scan_l0():
    lb = LB()
    io = IO(lb)
    emit_scan_l0(lb, io)
    lb.barrier_op("sync", lambda E: E.nop())
    return lb.finalize()



def emit_scan_l1(lb, io):
    nch = D // 4
    nct = nch // 128
    io.inp("dt_sl", [nch, S], BF16)
    io.inp("x_sl", [nch, S], BF16)
    io.inp("bc2", [32, S], F32)
    alog_in = io.inp("alog_sl", [nch, NST], F32)
    dp_in = io.inp("dp_cols", [128, nct], F32)
    idf = io.inp("ident_f32", [128, 128], F32)
    o_y = io.out("y_tm", [S, nch], F32)

    bcf = lb.sb("bcf", [32, S], F32)
    bcb = lb.sb("bcb", [32, S], BF16)
    io.load("bc2", bcf[:], 0, 32, 0, S)
    lb.op("vector", lambda E: E.tensor_copy(bcb[:], bcf[:]), r=[bcf], w=[bcb])
    bc_scr = lb.dram("bc_scr1", [32, S], BF16)
    lb.dma("gpsimd", bc_scr, bcb[:])

    s_A = lb.sb("s_A", [128, nct * 16], F32)
    for j in range(nct):
        lb.dma("sync", s_A[:, j * 16:(j + 1) * 16], alog_in[j * 128:(j + 1) * 128, :])
    lb.op("scalar", lambda E: E.activation(s_A[:], s_A[:], AF.Exp), r=[s_A], w=[s_A])
    lb.op("vector", lambda E: E.tensor_scalar(s_A[:], s_A[:], -1.0, None, ALU.mult),
          r=[s_A], w=[s_A])
    dp = lb.sb("dp", [128, nct], F32)
    lb.dma("sync", dp[:], dp_in)
    sid_f = lb.sb("sid_f", [128, 128], F32)
    lb.dma("sync", sid_f[:], idf)

    ps_t = [lb.ps(f"pstr{k}", [128, 128], F32) for k in range(2)]
    st_t = [lb.sb(f"sttr{k}", [128, 128], F32) for k in range(4)]
    kcnt = [0]

    def post(j, ci, o, w, y_j, uj):
        lb.op("vector", lambda E, j=j, w=w, uj=uj:
              E.scalar_tensor_tensor(y_j[:, :w], uj[:, :w], dp[:, j:j + 1], y_j[:, :w],
                                     ALU.mult, ALU.add),
              r=[uj, dp, y_j], w=[y_j])
        for i in range(w // 128):
            p, s = ps_t[kcnt[0] % 2], st_t[kcnt[0] % 4]
            lb.op("tensor", lambda E, i=i, w=w, p=p:
                  E.transpose(p[:], y_j[:, i * 128:(i + 1) * 128], sid_f[:]),
                  r=[y_j, sid_f], w=[p])
            lb.op("vector", lambda E, p=p, s=s: E.tensor_copy(s[:], p[:]), r=[p], w=[s])
            lb.dma("gpsimd", o_y[o + i * 128:o + (i + 1) * 128, j * 128:(j + 1) * 128], s[:])
            kcnt[0] += 1

    emit_scan_core(lb, io, nch, "dt_sl", "x_sl", bc_scr, s_A, post)


def build_scan_l1():
    lb = LB()
    io = IO(lb)
    emit_scan_l1(lb, io)
    lb.barrier_op("sync", lambda E: E.nop())
    return lb.finalize()



def emit_wmean(lb, wt_in, ntiles, free, slab, psm):
    u = uid()
    acc = lb.sb(f"wm_acc{u}", [128, 1], F32)
    prt = lb.sb(f"wm_p{u}", [128, 1], F32)
    lb.op("vector", lambda E: E.memset(acc[:], 0.0), w=[acc])
    for i in range(ntiles):
        for (o, w) in chunks(free, 1024):
            lb.dma("sync", slab[:, :w], wt_in[i * 128:(i + 1) * 128, o:o + w])
            lb.op("vector", lambda E, w=w: E.tensor_reduce(prt[:], slab[:, :w], AX.X, ALU.add,
                                                           apply_absolute_value=True),
                  r=[slab], w=[prt])
            lb.op("vector", lambda E: E.tensor_add(acc[:], acc[:], prt[:]), r=[acc, prt], w=[acc])
    onesf = lb.cvec(1.0, F32)
    mn = lb.sb(f"wm_mn{u}", [1, 2], F32)
    lb.op("tensor", lambda E: E.matmul(psm[:], acc[:], onesf[:], start=True, stop=True),
          r=[acc, onesf], w=[psm])
    lb.op("vector", lambda E: E.tensor_scalar(mn[:, 0:1], psm[:], 1.0 / (ntiles * 128 * free),
                                              1e-5, ALU.mult, ALU.max), r=[psm], w=[mn])
    lb.op("vector", lambda E: E.reciprocal(mn[:, 1:2], mn[:, 0:1]), r=[mn], w=[mn])
    scr = lb.dram(f"wm_scr{u}", [1, 2], F32)
    out = lb.sb(f"wm_bc{u}", [128, 2], F32)
    lb.dma("scalar", scr, mn[:])
    lb.dma("scalar", out[:], scr.broadcast_to([128, 2]))
    return out


def emit_ffn0(lb, io):
    TW = TPC + 2
    nft = D // 128
    nht = H // 128
    xnq_in = io.inp("xnq_cols", [D, TW], BF16)
    qinv_in = io.inp("qinv_cols", [1, TW], F32)
    w1t_in = io.inp("w1t", [D, H], F32)
    w2t_in = io.inp("w2t", [H, D], F32)
    cw_in = io.inp("cw_cols", [128, 3 * nht], F32)
    cb_in = io.inp("cb_cols", [128, nht], F32)
    idf = io.inp("ident_f32", [128, 128], F32)
    o_ffn = io.out("ffn_tm", [TPC, D], F32)

    xnq = [lb.sb(f"xnq{k}", [128, TW], BF16) for k in range(nft)]
    for k in range(nft):
        lb.dma("sync", xnq[k][:], xnq_in[k * 128:(k + 1) * 128, :])
    qbc = lb.sb("qbc", [128, TW], F32)
    lb.dma("sync", qbc[:], qinv_in.broadcast_to([128, TW]))
    cw = lb.sb("cw", [128, 3 * nht], F32)
    cb = lb.sb("cb", [128, nht], F32)
    lb.dma("sync", cw[:], cw_in)
    lb.dma("sync", cb[:], cb_in)
    sid_f = lb.sb("sid_f", [128, 128], F32)
    lb.dma("sync", sid_f[:], idf)

    slab_f = lb.sb("slab_f", [128, D], F32)
    wm_ps = lb.ps("wm_ps", [1, 1], F32)
    mn1 = emit_wmean(lb, w1t_in, nft, H, slab_f, wm_ps)
    slab2_f = slab_f
    mn2 = emit_wmean(lb, w2t_in, nht, D, slab2_f, wm_ps)

    w2q = [lb.sb(f"w2q{g}", [128, D], BF16) for g in range(nht)]
    tmp_i = lb.sb("tmp_i", [128, 1024], I32)
    tmp_f = lb.sb("tmp_f", [128, 1024], F32)
    for g in range(nht):
        lb.dma("sync", slab2_f[:], w2t_in[g * 128:(g + 1) * 128, :])
        lb.op("vector", lambda E: E.tensor_scalar(tmp_f[:, :D], slab2_f[:], mn2[:, 1:2], None, ALU.mult),
              r=[slab2_f, mn2], w=[tmp_f])
        lb.op("vector", lambda E: E.tensor_copy(tmp_i[:, :D], tmp_f[:, :D]), r=[tmp_f], w=[tmp_i])
        lb.op("vector", lambda E, g=g: E.tensor_copy(w2q[g][:], tmp_i[:, :D]), r=[tmp_i], w=[w2q[g]])
        lb.op("vector", lambda E, g=g: E.tensor_scalar(w2q[g][:], w2q[g][:], 1.0, -1.0, ALU.min, ALU.max),
              r=[w2q[g]], w=[w2q[g]])

    hs_scr = lb.dram("hs_scr", [H, TPC], F32)
    w1slab_b = lb.sb("w1slab_b", [128, nft * 128], BF16)
    h1 = lb.sb("h1", [128, TW], F32)
    cv = lb.sb("cvt", [128, TPC], F32)
    amax = lb.sb("amax", [128, TPC], F32)
    lb.op("vector", lambda E: E.memset(amax[:], 0.0), w=[amax])
    psA = [lb.ps(f"psA{t}", [128, 512], F32) for t in range(2)]
    for g in range(nht):
        lb.dma("sync", slab_f[:, :D], w1t_in[:, g * 128:(g + 1) * 128].rearrange("(j p) c -> p j c", p=128))
        lb.op("vector", lambda E: E.tensor_scalar(tmp_f[:, :D], slab_f[:, :D], mn1[:, 1:2], None, ALU.mult),
              r=[slab_f, mn1], w=[tmp_f])
        lb.op("vector", lambda E: E.tensor_copy(tmp_i[:, :D], tmp_f[:, :D]), r=[tmp_f], w=[tmp_i])
        lb.op("vector", lambda E: E.tensor_copy(w1slab_b[:], tmp_i[:, :D]), r=[tmp_i], w=[w1slab_b])
        lb.op("vector", lambda E: E.tensor_scalar(w1slab_b[:], w1slab_b[:], 1.0, -1.0, ALU.min, ALU.max),
              r=[w1slab_b], w=[w1slab_b])
        for ci, (o, w) in enumerate(chunks(TW, 410)):
            ps = psA[ci % 2]
            for k in range(nft):
                lb.op("tensor", lambda E, k=k, o=o, w=w, ps=ps:
                      E.matmul(ps[:, :w], w1slab_b[:, k * 128:(k + 1) * 128], xnq[k][:, o:o + w],
                               start=(k == 0), stop=(k == nft - 1)),
                      r=[w1slab_b, xnq[k]] + ([ps] if k else []), w=[ps])
            lb.op("vector", lambda E, o=o, w=w, ps=ps:
                  E.scalar_tensor_tensor(h1[:, o:o + w], ps[:, :w], mn1[:, 0:1], qbc[:, o:o + w],
                                         ALU.mult, ALU.mult),
                  r=[ps, mn1, qbc], w=[h1])
        lb.op("vector", lambda E, g=g:
              E.tensor_scalar(cv[:], h1[:, 2:2 + TPC], cw[:, g * 3 + 2:g * 3 + 3],
                              cb[:, g:g + 1], ALU.mult, ALU.add),
              r=[h1, cw, cb], w=[cv])
        for k in (1, 0):
            lb.op("vector", lambda E, g=g, k=k:
                  E.scalar_tensor_tensor(cv[:], h1[:, k:k + TPC], cw[:, g * 3 + k:g * 3 + k + 1],
                                         cv[:], ALU.mult, ALU.add),
                  r=[h1, cw, cv], w=[cv])
        lb.op("scalar", lambda E: E.activation(cv[:], cv[:], AF.Silu), r=[cv], w=[cv])
        lb.dma("gpsimd", hs_scr[g * 128:(g + 1) * 128, :], cv[:])
        lb.op("vector", lambda E: E.scalar_tensor_tensor(h1[:, 0:TPC], cv[:], -1.0, cv[:], ALU.mult, ALU.max),
              r=[cv], w=[h1])
        lb.op("vector", lambda E: E.tensor_max(amax[:], amax[:], h1[:, 0:TPC]), r=[amax, h1], w=[amax])

    ps_t = [psA[0][:, 0:128], psA[1][:, 0:128]]
    am_t = lb.sb("am_t", [128, 128], F32)
    am_tm = lb.sb("am_tm", [128, TPC // 128], F32)
    for i in range(TPC // 128):
        p = ps_t[i % 2]
        lb.op("tensor", lambda E, i=i, p=p: E.transpose(p[:], amax[:, i * 128:(i + 1) * 128], sid_f[:]),
              r=[amax, sid_f], w=[p])
        lb.op("vector", lambda E, p=p: E.tensor_copy(am_t[:], p[:]), r=[p], w=[am_t])
        lb.op("vector", lambda E, i=i: E.tensor_reduce(am_tm[:, i:i + 1], am_t[:], AX.X, ALU.max),
              r=[am_t], w=[am_tm])
    sc_tm = lb.sb("sc_tm", [128, TPC // 128], F32)
    rc_tm = lb.sb("rc_tm", [128, TPC // 128], F32)
    lb.op("vector", lambda E: E.tensor_scalar(am_tm[:], am_tm[:], 1e-5, None, ALU.max),
          r=[am_tm], w=[am_tm])
    lb.op("vector", lambda E: E.reciprocal(sc_tm[:], am_tm[:]), r=[am_tm], w=[sc_tm])
    lb.op("vector", lambda E: E.tensor_scalar(sc_tm[:], sc_tm[:], 127.0, None, ALU.mult),
          r=[sc_tm], w=[sc_tm])
    lb.op("vector", lambda E: E.tensor_scalar(rc_tm[:], am_tm[:], 1.0 / 127.0, None, ALU.mult),
          r=[am_tm], w=[rc_tm])
    scr_s = lb.dram("scr_s", [2, TPC], F32)
    lb.dma("scalar", scr_s[0:1, :].rearrange("o (i p) -> o p i", p=128).squeeze(0), sc_tm[:],
           allow_slow_non_contiguous=True)
    lb.dma("scalar", scr_s[1:2, :].rearrange("o (i p) -> o p i", p=128).squeeze(0), rc_tm[:],
           allow_slow_non_contiguous=True)
    sc_bc = lb.sb("sc_bc", [128, TPC], F32)
    rc_bc = lb.sb("rc_bc", [128, TPC], F32)
    lb.dma("scalar", sc_bc[:], scr_s[0:1, :].broadcast_to([128, TPC]))
    lb.dma("scalar", rc_bc[:], scr_s[1:2, :].broadcast_to([128, TPC]))

    ffn_scr = lb.dram("ffn_scr", [D, TPC], F32)
    qg = [lb.sb(f"qg{g}", [128, 512], BF16) for g in range(nht)]
    hs_ld = lb.sb("hs_ld", [128, 512], F32)
    psF = [lb.ps(f"psF{f}", [128, 512], F32) for f in range(4)]
    stF = [lb.sb(f"stF{f}", [128, 512], F32) for f in range(4)]
    for (o, w) in chunks(TPC, 512):
        for g in range(nht):
            lb.dma("sync", hs_ld[:, :w], hs_scr[g * 128:(g + 1) * 128, o:o + w])
            lb.op("vector", lambda E, o=o, w=w: E.tensor_mul(tmp_f[:, :w], hs_ld[:, :w], sc_bc[:, o:o + w]),
                  r=[hs_ld, sc_bc], w=[tmp_f])
            lb.op("vector", lambda E, w=w: E.tensor_copy(tmp_i[:, :w], tmp_f[:, :w]),
                  r=[tmp_f], w=[tmp_i])
            lb.op("vector", lambda E, g=g, w=w: E.tensor_copy(qg[g][:, :w], tmp_i[:, :w]),
                  r=[tmp_i], w=[qg[g]])
        for half in range(2):
            for f in range(4):
                ff = half * 4 + f
                for g in range(nht):
                    lb.op("tensor", lambda E, ff=ff, f=f, g=g, w=w:
                          E.matmul(psF[f][:, :w], w2q[g][:, ff * 128:(ff + 1) * 128], qg[g][:, :w],
                                   start=(g == 0), stop=(g == nht - 1)),
                          r=[w2q[g], qg[g]] + ([psF[f]] if g else []), w=[psF[f]])
            for f in range(4):
                ff = half * 4 + f
                lb.op("vector", lambda E, ff=ff, f=f, o=o, w=w:
                      E.scalar_tensor_tensor(stF[f][:, :w], psF[f][:, :w], mn2[:, 0:1],
                                             rc_bc[:, o:o + w], ALU.mult, ALU.mult),
                      r=[psF[f], mn2, rc_bc], w=[stF[f]])
                lb.dma("gpsimd", ffn_scr[ff * 128:(ff + 1) * 128, o:o + w], stF[f][:, :w])

    ld_f = cv
    st_o = [stF[k][:, 0:128] for k in range(4)]
    kk = 0
    for j in range(nft):
        lb.dma("sync", ld_f[:], ffn_scr[j * 128:(j + 1) * 128, :])
        for i in range(TPC // 128):
            p, s = ps_t[kk % 2], st_o[kk % 4]
            lb.op("tensor", lambda E, i=i, p=p: E.transpose(p, ld_f[:, i * 128:(i + 1) * 128], sid_f[:]),
                  r=[ld_f, sid_f], w=[p])
            lb.op("vector", lambda E, p=p, s=s: E.tensor_copy(s, p), r=[p], w=[s])
            lb.dma("gpsimd", o_ffn[i * 128:(i + 1) * 128, j * 128:(j + 1) * 128], s)
            kk += 1



def build_ffn0():
    lb = LB()
    io = IO(lb)
    emit_ffn0(lb, io)
    lb.barrier_op("sync", lambda E: E.nop())
    return lb.finalize()



def emit_mid(lb, io):
    x_in = io.inp("x_rows", [TPC, D], F32)
    a_in = io.inp("attn_rows", [TPC, D], F32)
    f_in = io.inp("ffn_rows", [TPC, D], F32)
    wd_in = io.inp("wdT", [D, 96], F32)
    wdt_in = io.inp("wdtT", [RK, D], F32)
    bdt_in = io.inp("bdt_cols", [128, D // 128], F32)
    idb = io.inp("ident_bf16", [128, 128], BF16)
    o_x1 = io.out("x1_rows", [TPC, D], F32)
    o_xn2 = io.out("xn2_fm", [D, TPC], BF16)
    o_dbl2 = io.out("dbl2", [96, TPC], F32)
    o_dt2 = io.out("dt2", [D, TPC], BF16)

    nft = D // 128
    ntt = TPC // 128
    sid_b = lb.sb("sid_b", [128, 128], BF16)
    lb.dma("sync", sid_b[:], idb)
    xt = [lb.sb(f"xt{k}", [128, D], F32) for k in range(2)]
    at = [lb.sb(f"at{k}", [128, D], F32) for k in range(2)]
    ft = [lb.sb(f"ft{k}", [128, D], F32) for k in range(2)]
    sq = lb.sb("sq", [128, D], F32)
    xnb = [lb.sb(f"xnb{k}", [128, D], BF16) for k in range(2)]
    xn2 = [lb.sb(f"xn2_{j}", [128, TPC], BF16) for j in range(nft)]
    ps_b = [lb.ps(f"ps_b{k}", [128, 128], BF16) for k in range(2)]

    for i in range(ntt):
        xi, ai, fi, xb = xt[i % 2], at[i % 2], ft[i % 2], xnb[i % 2]
        lb.dma("sync", xi[:], x_in[i * 128:(i + 1) * 128, :])
        lb.dma("sync", ai[:], a_in[i * 128:(i + 1) * 128, :])
        lb.dma("sync", fi[:], f_in[i * 128:(i + 1) * 128, :])
        lb.op("vector", lambda E, xi=xi, ai=ai: E.tensor_add(xi[:], xi[:], ai[:]),
              r=[xi, ai], w=[xi])
        lb.op("vector", lambda E, xi=xi, fi=fi: E.tensor_add(xi[:], xi[:], fi[:]),
              r=[xi, fi], w=[xi])
        lb.dma("gpsimd", o_x1[i * 128:(i + 1) * 128, :], xi[:])
        emit_ln_tm(lb, xi, ai, sq)
        lb.op("vector", lambda E, ai=ai, xb=xb: E.tensor_copy(xb[:], ai[:]), r=[ai], w=[xb])
        for j in range(nft):
            p = ps_b[(i * nft + j) % 2]
            lb.op("tensor", lambda E, i=i, j=j, p=p, xb=xb:
                  E.transpose(p[:], xb[:, j * 128:(j + 1) * 128], sid_b[:]),
                  r=[xb, sid_b], w=[p])
            lb.op("scalar", lambda E, i=i, j=j, p=p:
                  E.copy(xn2[j][:, i * 128:(i + 1) * 128], p[:]), r=[p], w=[xn2[j]])
    for j in range(nft):
        lb.dma("gpsimd", o_xn2[j * 128:(j + 1) * 128, :], xn2[j][:])

    wd_f = lb.sb("wd_f", [128, nft * 96], F32)
    wd_b = lb.sb("wd_b", [128, nft * 96], BF16)
    lb.dma("sync", wd_f[:], wd_in.rearrange("(j p) c -> p j c", p=128))
    lb.op("vector", lambda E: E.tensor_copy(wd_b[:], wd_f[:]), r=[wd_f], w=[wd_b])
    ps_d = [lb.ps(f"ps_d{k}", [128, 512], F32) for k in range(2)]
    dbl2 = lb.sb("dbl2_sb", [96, TPC], F32)
    for ci, (o, w) in enumerate(chunks(TPC, 512)):
        ps = ps_d[ci % 2]
        for j in range(nft):
            lb.op("tensor", lambda E, j=j, o=o, w=w, ps=ps:
                  E.matmul(ps[0:96, :w], wd_b[:, j * 96:(j + 1) * 96], xn2[j][:, o:o + w],
                           start=(j == 0), stop=(j == nft - 1)),
                  r=[wd_b, xn2[j]] + ([ps] if j else []), w=[ps])
        lb.op("vector", lambda E, o=o, w=w, ps=ps: E.tensor_copy(dbl2[:, o:o + w], ps[0:96, :w]),
              r=[ps], w=[dbl2])
    lb.dma("gpsimd", o_dbl2, dbl2[:])
    dbl2_b = lb.sb("dbl2_b", [RK, TPC], BF16)
    lb.op("vector", lambda E: E.tensor_copy(dbl2_b[:], dbl2[0:RK, :]), r=[dbl2], w=[dbl2_b])

    wdt_f = lb.sb("wdt_f", [RK, D], F32)
    wdt_b = lb.sb("wdt_b", [RK, D], BF16)
    lb.dma("sync", wdt_f[:], wdt_in)
    lb.op("vector", lambda E: E.tensor_copy(wdt_b[:], wdt_f[:]), r=[wdt_f], w=[wdt_b])
    bdt = lb.sb("bdt", [128, nft], F32)
    lb.dma("sync", bdt[:], bdt_in)
    sp_t1 = lb.sb("sp_t1", [128, 512], F32)
    sp_e = lb.sb("sp_e", [128, 512], F32)
    sp_l = lb.sb("sp_l", [128, 512], F32)
    dt_st = [lb.sb(f"dt_st{k}", [128, 512], BF16) for k in range(2)]
    onev = lb.cvec(1.0)
    for j in range(nft):
        for ci, (o, w) in enumerate(chunks(TPC, 512)):
            ps = ps_d[ci % 2]
            lb.op("tensor", lambda E, j=j, o=o, w=w, ps=ps:
                  E.matmul(ps[0:128, :w] if False else ps[:, :w], wdt_b[:, j * 128:(j + 1) * 128],
                           dbl2_b[:, o:o + w], start=True, stop=True),
                  r=[wdt_b, dbl2_b], w=[ps])
            lb.op("vector", lambda E, j=j, w=w, ps=ps:
                  E.tensor_scalar(sp_t1[:, :w], ps[:, :w], bdt[:, j:j + 1], 30.0, ALU.add, ALU.min),
                  r=[ps, bdt], w=[sp_t1])
            lb.op("scalar", lambda E, w=w: E.activation(sp_e[:, :w], sp_t1[:, :w], AF.Exp),
                  r=[sp_t1], w=[sp_e])
            lb.op("scalar", lambda E, w=w: E.activation(sp_l[:, :w], sp_e[:, :w], AF.Ln, bias=onev[:]),
                  r=[sp_e, onev], w=[sp_l])
            dst = dt_st[ci % 2]
            lb.op("vector", lambda E, j=j, w=w, ps=ps, dst=dst:
                  E.scalar_tensor_tensor(dst[:, :w], ps[:, :w], bdt[:, j:j + 1], sp_l[:, :w],
                                         ALU.add, ALU.max),
                  r=[ps, bdt, sp_l], w=[dst])
            lb.dma("gpsimd", o_dt2[j * 128:(j + 1) * 128, o:o + w], dst[:, :w])



def build_mid():
    lb = LB()
    io = IO(lb)
    emit_mid(lb, io)
    lb.barrier_op("sync", lambda E: E.nop())
    return lb.finalize()



def emit_l1ffn(lb, io):
    nft = D // 128
    ntt = TPC // 128
    xn2_in = io.inp("xn2_cols", [D, TPC], BF16)
    x1_in = io.inp("x1_rows", [TPC, D], F32)
    io.inp("y2_rows", [TPC, D], F32)
    wv_in = io.inp("WvT", [D, D], F32)
    bv_in = io.inp("bv_cols", [128, nft], F32)
    wg_in = io.inp("WgT", [D, 8], F32)
    bg_in = io.inp("bg8", [8, 1], F32)
    wp_in = io.inp("WpT", [D, 64], F32)
    w1_in = io.inp("W1T", [D, 256], F32)
    b1_in = io.inp("b1_cols", [128, 2], F32)
    w2_in = io.inp("W2n", [256, D], F32)
    b2_in = io.inp("b2stk", [8, D], F32)
    gr_in = io.inp("g_r8", [8, 1], F32)
    br_in = io.inp("b_r8", [8, 1], F32)
    gc_in = io.inp("g_c8", [8, 1], F32)
    bc_in = io.inp("b_c8", [8, 1], F32)
    al_in = io.inp("alpha1", [1, 1], F32)
    idf = io.inp("ident_f32", [128, 128], F32)
    o_out = io.out("out_rows", [TPC, D], F32)

    sid_f = lb.sb("sid_f", [128, 128], F32)
    lb.dma("sync", sid_f[:], idf)
    xn2 = [lb.sb(f"xn2_{j}", [128, TPC], BF16) for j in range(nft)]
    for j in range(nft):
        lb.dma("sync", xn2[j][:], xn2_in[j * 128:(j + 1) * 128, :])

    psA = [lb.ps(f"psA{k}", [128, 512], F32) for k in range(2)]
    ps8 = lb.ps("ps8", [128, 512], F32)
    ps1 = lb.ps("ps1", [1, 512], F32)

    Vf = [lb.sb(f"Vf{j}", [128, TPC], F32) for j in range(nft)]
    Vb = [lb.sb(f"Vb{j}", [128, TPC], BF16) for j in range(nft)]
    bv = lb.sb("bv", [128, nft], F32)
    lb.dma("sync", bv[:], bv_in)
    wslab_f = lb.sb("wslab_f", [128, nft * 128], F32)
    wslab_b = lb.sb("wslab_b", [128, nft * 128], BF16)
    for f in range(nft):
        lb.dma("sync", wslab_f[:], wv_in[:, f * 128:(f + 1) * 128].rearrange("(j p) c -> p j c", p=128))
        lb.op("vector", lambda E: E.tensor_copy(wslab_b[:], wslab_f[:]), r=[wslab_f], w=[wslab_b])
        for ci, (o, w) in enumerate(chunks(TPC, 512)):
            ps = psA[ci % 2]
            for k in range(nft):
                lb.op("tensor", lambda E, k=k, o=o, w=w, ps=ps:
                      E.matmul(ps[:, :w], wslab_b[:, k * 128:(k + 1) * 128], xn2[k][:, o:o + w],
                               start=(k == 0), stop=(k == nft - 1)),
                      r=[wslab_b, xn2[k]] + ([ps] if k else []), w=[ps])
            lb.op("vector", lambda E, f=f, o=o, w=w, ps=ps:
                  E.tensor_scalar(Vf[f][:, o:o + w], ps[:, :w], bv[:, f:f + 1], None, ALU.add),
                  r=[ps, bv], w=[Vf[f]])
            lb.op("vector", lambda E, f=f, o=o, w=w: E.tensor_copy(Vb[f][:, o:o + w], Vf[f][:, o:o + w]),
                  r=[Vf[f]], w=[Vb[f]])

    wg_f = lb.sb("wg_f", [128, nft * 8], F32)
    wg_b = lb.sb("wg_b", [128, nft * 8], BF16)
    lb.dma("sync", wg_f[:], wg_in.rearrange("(j p) c -> p j c", p=128))
    lb.op("vector", lambda E: E.tensor_copy(wg_b[:], wg_f[:]), r=[wg_f], w=[wg_b])
    wp_f = lb.sb("wp_f", [128, nft * 64], F32)
    wp_b = lb.sb("wp_b", [128, nft * 64], BF16)
    lb.dma("sync", wp_f[:], wp_in.rearrange("(j p) c -> p j c", p=128))
    lb.op("vector", lambda E: E.tensor_copy(wp_b[:], wp_f[:]), r=[wp_f], w=[wp_b])
    rt = lb.sb("rt", [8, TPC], F32)
    ct = lb.sb("ct", [8, TPC], F32)
    pv = lb.sb("pv", [64, TPC], F32)
    sqv = lb.sb("sqv", [64, 512], F32)
    bg = lb.sb("bg", [8, 1], F32)
    lb.dma("sync", bg[:], bg_in)
    blk8_in = io.inp("blk8c", [64, 8], F32)
    blk8 = lb.sb("blk8", [64, 8], F32)
    lb.dma("sync", blk8[:], blk8_in)
    onesf = lb.cvec(1.0, F32)
    eighth = lb.cvec(0.125, F32)
    for ci, (o, w) in enumerate(chunks(TPC, 512)):
        for k in range(nft):
            lb.op("tensor", lambda E, k=k, o=o, w=w:
                  E.matmul(ps8[0:8, :w], wg_b[:, k * 8:(k + 1) * 8], xn2[k][:, o:o + w] if False else Vb[k][:, o:o + w],
                           start=(k == 0), stop=(k == nft - 1)),
                  r=[wg_b, Vb[k]] + ([ps8] if k else []), w=[ps8])
        lb.op("vector", lambda E, o=o, w=w: E.tensor_scalar(rt[:, o:o + w], ps8[0:8, :w], bg[:], None, ALU.add),
              r=[ps8, bg], w=[rt])
        for k in range(nft):
            lb.op("tensor", lambda E, k=k, o=o, w=w:
                  E.matmul(ps8[0:64, :w], wp_b[:, k * 64:(k + 1) * 64], Vb[k][:, o:o + w],
                           start=(k == 0), stop=(k == nft - 1)),
                  r=[wp_b, Vb[k]] + ([ps8] if k else []), w=[ps8])
        lb.op("vector", lambda E, o=o, w=w: E.tensor_copy(pv[:, o:o + w], ps8[0:64, :w]),
              r=[ps8], w=[pv])
        lb.op("vector", lambda E, o=o, w=w: E.tensor_mul(sqv[:, :w], pv[:, o:o + w], pv[:, o:o + w]),
              r=[pv], w=[sqv])
        lb.op("tensor", lambda E, o=o, w=w: E.matmul(ps8[0:8, :w], blk8[:], sqv[:, :w],
                                                     start=True, stop=True),
              r=[blk8, sqv], w=[ps8])
        lb.op("scalar", lambda E, o=o, w=w: E.activation(ct[:, o:o + w], ps8[0:8, :w], AF.Sqrt,
                                                         scale=eighth[0:8, :]),
              r=[ps8, eighth], w=[ct])

    def ln8(src, g_in, b_in, dst):
        u = uid()
        g8 = lb.sb(f"g8_{u}", [8, 1], F32)
        b8 = lb.sb(f"b8_{u}", [8, 1], F32)
        lb.dma("sync", g8[:], g_in)
        lb.dma("sync", b8[:], b_in)
        st1 = lb.sb(f"l8m{u}", [1, 512], F32)
        st2 = lb.sb(f"l8r{u}", [1, 512], F32)
        sqs = lb.sb(f"l8q{u}", [8, 512], F32)
        scr = lb.dram(f"l8scr{u}", [2, TPC], F32)
        mbc = lb.sb(f"l8mb{u}", [8, 512], F32)
        rbc = lb.sb(f"l8rb{u}", [8, 512], F32)
        epsv = lb.cvec(EPS)
        for (o, w) in chunks(TPC, 512):
            lb.op("tensor", lambda E, o=o, w=w: E.matmul(ps1[:, :w], onesf[0:8, :], src[:, o:o + w],
                                                         start=True, stop=True),
                  r=[onesf, src], w=[ps1])
            lb.op("vector", lambda E, o=o, w=w: E.tensor_scalar(st1[:, :w], ps1[:, :w], 0.125, None, ALU.mult),
                  r=[ps1], w=[st1])
            lb.op("scalar", lambda E, o=o, w=w: E.activation(sqs[:, :w], src[:, o:o + w], AF.Square),
                  r=[src], w=[sqs])
            lb.op("tensor", lambda E, o=o, w=w: E.matmul(ps1[:, :w], onesf[0:8, :], sqs[:, :w],
                                                         start=True, stop=True),
                  r=[onesf, sqs], w=[ps1])
            lb.op("vector", lambda E, o=o, w=w: E.tensor_scalar(st2[:, :w], ps1[:, :w], 0.125, None, ALU.mult),
                  r=[ps1], w=[st2])
            lb.op("scalar", lambda E, w=w: E.activation(sqs[0:1, :w], st1[:, :w], AF.Square),
                  r=[st1], w=[sqs])
            lb.op("vector", lambda E, w=w: E.tensor_sub(st2[:, :w], st2[:, :w], sqs[0:1, :w]),
                  r=[st2, sqs], w=[st2])
            lb.op("scalar", lambda E, w=w: E.activation(st2[:, :w], st2[:, :w], AF.Sqrt, bias=epsv[0:1, :]),
                  r=[st2, epsv], w=[st2])
            lb.op("vector", lambda E, w=w: E.reciprocal(st2[:, :w], st2[:, :w]), r=[st2], w=[st2])
            lb.dma("scalar", scr[0:1, o:o + w], st1[:, :w])
            lb.dma("scalar", scr[1:2, o:o + w], st2[:, :w])
            lb.dma("scalar", mbc[:, :w], scr[0:1, o:o + w].broadcast_to([8, w]))
            lb.dma("scalar", rbc[:, :w], scr[1:2, o:o + w].broadcast_to([8, w]))
            lb.op("vector", lambda E, o=o, w=w: E.tensor_sub(dst[:, o:o + w], src[:, o:o + w], mbc[:, :w]),
                  r=[src, mbc], w=[dst])
            lb.op("vector", lambda E, o=o, w=w: E.tensor_tensor(dst[:, o:o + w], dst[:, o:o + w], rbc[:, :w], ALU.mult),
                  r=[dst, rbc], w=[dst])
            lb.op("vector", lambda E, o=o, w=w: E.tensor_scalar(dst[:, o:o + w], dst[:, o:o + w], g8[:], b8[:],
                                                                ALU.mult, ALU.add),
                  r=[dst, g8, b8], w=[dst])

    lnr = rt
    lnc = ct
    ln8(rt, gr_in, br_in, lnr)
    ln8(ct, gc_in, bc_in, lnc)

    alv = lb.sb("alv", [1, 1], F32)
    lb.dma("sync", alv[:], al_in)
    lb.op("scalar", lambda E: E.activation(alv[:], alv[:], AF.Sigmoid), r=[alv], w=[alv])
    scr_a = lb.dram("scr_a", [1, 1], F32)
    sgv = lb.sb("sgv", [8, 1], F32)
    lb.dma("scalar", scr_a, alv[:])
    lb.dma("scalar", sgv[:], scr_a.broadcast_to([8, 1]))
    sfm = lnc
    lb.op("vector", lambda E: E.scalar_tensor_tensor(sfm[:], lnc[:], sgv[:], lnr[:], ALU.mult, ALU.add),
          r=[lnc, sgv, lnr], w=[sfm])

    we_fm = lb.sb("we_fm", [8, TPC], F32)
    we_b = lb.sb("we_b", [8, TPC], BF16)
    s_tm = lb.sb("s_tm", [128, 8], F32)
    we_tm = lb.sb("we_tm", [128, 8], F32)
    msk1 = lb.sb("msk1", [128, 8], F32)
    msk2 = lb.sb("msk2", [128, 8], F32)
    m1 = lb.sb("m1", [128, 1], F32)
    m2 = lb.sb("m2", [128, 1], F32)
    w2w = lb.sb("w2w", [128, 1], F32)
    w1w = lb.sb("w1w", [128, 1], F32)
    for i in range(ntt):
        lb.op("tensor", lambda E, i=i: E.transpose(psA[0][:, 0:8], sfm[:, i * 128:(i + 1) * 128], sid_f[0:8, 0:8]),
              r=[sfm, sid_f], w=[psA[0]])
        lb.op("vector", lambda E: E.tensor_copy(s_tm[:], psA[0][:, 0:8]), r=[psA[0]], w=[s_tm])
        lb.op("vector", lambda E: E.tensor_reduce(m1[:], s_tm[:], AX.X, ALU.max), r=[s_tm], w=[m1])
        lb.op("vector", lambda E: E.tensor_scalar(msk1[:], s_tm[:], m1[:], None, ALU.is_equal),
              r=[s_tm, m1], w=[msk1])
        lb.op("vector", lambda E: E.scalar_tensor_tensor(msk2[:], msk1[:], -1e30, s_tm[:], ALU.mult, ALU.add),
              r=[msk1, s_tm], w=[msk2])
        lb.op("vector", lambda E: E.tensor_reduce(m2[:], msk2[:], AX.X, ALU.max), r=[msk2], w=[m2])
        lb.op("vector", lambda E: E.tensor_scalar(msk2[:], s_tm[:], m2[:], None, ALU.is_equal),
              r=[s_tm, m2], w=[msk2])
        lb.op("vector", lambda E: E.tensor_sub(w2w[:], m2[:], m1[:]), r=[m2, m1], w=[w2w])
        lb.op("scalar", lambda E: E.activation(w2w[:], w2w[:], AF.Sigmoid), r=[w2w], w=[w2w])
        lb.op("vector", lambda E: E.tensor_scalar(w1w[:], w2w[:], -1.0, 1.0, ALU.mult, ALU.add),
              r=[w2w], w=[w1w])
        lb.op("vector", lambda E: E.tensor_scalar(we_tm[:], msk1[:], w1w[:], None, ALU.mult),
              r=[msk1, w1w], w=[we_tm])
        lb.op("vector", lambda E: E.scalar_tensor_tensor(we_tm[:], msk2[:], w2w[:], we_tm[:], ALU.mult, ALU.add),
              r=[msk2, w2w, we_tm], w=[we_tm])
        lb.op("tensor", lambda E, i=i: E.transpose(psA[1][0:8, 0:128], we_tm[:], sid_f[:]),
              r=[we_tm, sid_f], w=[psA[1]])
        lb.op("vector", lambda E, i=i: E.tensor_copy(we_fm[:, i * 128:(i + 1) * 128], psA[1][0:8, 0:128]),
              r=[psA[1]], w=[we_fm])
    lb.op("vector", lambda E: E.tensor_copy(we_b[:], we_fm[:]), r=[we_fm], w=[we_b])

    w1_b = lb.sb("w1_b", [128, nft * 256], BF16)
    w1r = w1_in.rearrange("(j p) c -> p j c", p=128)
    for half in range(2):
        lb.dma("sync", wslab_f[:], w1r[:, half * 4:(half + 1) * 4, :])
        lb.op("vector", lambda E, half=half:
              E.tensor_copy(w1_b[:, half * 1024:(half + 1) * 1024], wslab_f[:]),
              r=[wslab_f], w=[w1_b])
    b1c = lb.sb("b1c", [128, 2], F32)
    lb.dma("sync", b1c[:], b1_in)
    rep32_in = io.inp("rep32c", [8, 256], F32)
    rep32 = lb.sb("rep32", [8, 256], F32)
    lb.dma("sync", rep32[:], rep32_in)
    Hw = [xn2[0], xn2[1]]
    for c in range(2):
        for ci, (o, w) in enumerate(chunks(TPC, 512)):
            ps = psA[ci % 2]
            for k in range(nft):
                lb.op("tensor", lambda E, c=c, k=k, o=o, w=w, ps=ps:
                      E.matmul(ps[:, :w], w1_b[:, k * 256 + c * 128:k * 256 + (c + 1) * 128],
                               Vb[k][:, o:o + w], start=(k == 0), stop=(k == nft - 1)),
                      r=[w1_b, Vb[k]] + ([ps] if k else []), w=[ps])
            lb.op("vector", lambda E, c=c, w=w, ps=ps:
                  E.tensor_scalar(ps[:, :w], ps[:, :w], b1c[:, c:c + 1], None, ALU.add),
                  r=[ps, b1c], w=[ps])
            lb.op("scalar", lambda E, c=c, o=o, w=w, ps=ps:
                  E.activation(Hw[c][:, o:o + w], ps[:, :w], AF.Silu), r=[ps], w=[Hw[c]])
            lb.op("tensor", lambda E, c=c, o=o, w=w:
                  E.matmul(ps8[0:128, :w], rep32[:, c * 128:(c + 1) * 128], we_fm[:, o:o + w],
                           start=True, stop=True),
                  r=[rep32, we_fm], w=[ps8])
            lb.op("vector", lambda E, c=c, o=o, w=w:
                  E.tensor_tensor(Hw[c][:, o:o + w], Hw[c][:, o:o + w], ps8[0:128, :w], ALU.mult),
                  r=[Hw[c], ps8], w=[Hw[c]])

    w2_f = wslab_f
    w2_b = lb.sb("w2_b", [128, 2 * 128], BF16)
    b2_b = lb.sb("b2_b", [8, D], BF16)
    lb.dma("sync", wslab_f[0:8, 0:D], b2_in)
    lb.op("vector", lambda E: E.tensor_copy(b2_b[:], wslab_f[0:8, 0:D]), r=[wslab_f], w=[b2_b])
    for f in range(nft):
        lb.dma("sync", w2_f[:, 0:256], w2_in[:, f * 128:(f + 1) * 128].rearrange("(j p) c -> p j c", p=128))
        lb.op("vector", lambda E: E.tensor_copy(w2_b[:], w2_f[:, 0:256]), r=[w2_f], w=[w2_b])
        for ci, (o, w) in enumerate(chunks(TPC, 512)):
            ps = psA[ci % 2]
            for c in range(2):
                lb.op("tensor", lambda E, f=f, c=c, o=o, w=w, ps=ps:
                      E.matmul(ps[:, :w], w2_b[:, c * 128:(c + 1) * 128], Hw[c][:, o:o + w],
                               start=(c == 0), stop=False),
                      r=[w2_b, Hw[c]] + ([ps] if c else []), w=[ps])
            lb.op("tensor", lambda E, f=f, o=o, w=w, ps=ps:
                  E.matmul(ps[:, :w], b2_b[:, f * 128:(f + 1) * 128], we_b[:, o:o + w],
                           start=False, stop=True),
                  r=[b2_b, we_b, ps], w=[ps])
            lb.op("vector", lambda E, f=f, o=o, w=w, ps=ps:
                  E.tensor_add(Vf[f][:, o:o + w], Vf[f][:, o:o + w], ps[:, :w]),
                  r=[Vf[f], ps], w=[Vf[f]])

    x1t = [Vb[0][:].bitcast(F32), Vb[1][:].bitcast(F32)]
    y2t = [Vb[2][:].bitcast(F32), Vb[3][:].bitcast(F32)]
    o2t = [Vb[4][:].bitcast(F32), Vb[5][:].bitcast(F32)]
    for i in range(ntt):
        xi, yi, oi = x1t[i % 2], y2t[i % 2], o2t[i % 2]
        lb.dma("sync", xi, x1_in[i * 128:(i + 1) * 128, :])
        io.load("y2_rows", yi, i * 128, (i + 1) * 128, 0, D)
        for f in range(nft):
            p = psA[f % 2]
            lb.op("tensor", lambda E, i=i, f=f, p=p:
                  E.transpose(p[:, 0:128], Vf[f][:, i * 128:(i + 1) * 128], sid_f[:]),
                  r=[Vf[f], sid_f], w=[p])
            lb.op("vector", lambda E, f=f, p=p, oi=oi: E.tensor_copy(oi[:, f * 128:(f + 1) * 128], p[:, 0:128]),
                  r=[p], w=[oi])
        lb.op("vector", lambda E, xi=xi, yi=yi: E.tensor_add(xi, xi, yi), r=[xi, yi], w=[xi])
        lb.op("vector", lambda E, xi=xi, oi=oi: E.tensor_add(oi, oi, xi), r=[oi, xi], w=[oi])
        lb.dma("gpsimd", o_out[i * 128:(i + 1) * 128, :], oi)



def build_l1ffn():
    lb = LB()
    io = IO(lb)
    emit_l1ffn(lb, io)
    lb.barrier_op("sync", lambda E: E.nop())
    return lb.finalize()




GRPS = [[0, 1, 2, 3], [4, 5, 6, 7]]


def build_fused():
    lb = LB()
    nc = lb.nc
    NR = 17 * 128
    nch = C2 // 4
    nch1 = D // 4

    E = {}
    E["x_rows"] = lb.inp("x_rows", [NR, D], F32)
    E["idf"] = lb.inp("ident_f32", [128, 128], F32)
    E["idb"] = lb.inp("ident_bf16", [128, 128], BF16)
    for i in range(4):
        E[f"b{i}_norm_w"] = lb.inp(f"b{i}_norm_w", [D], F32)
        E[f"b{i}_norm_b"] = lb.inp(f"b{i}_norm_b", [D], F32)
        E[f"b{i}_in_w"] = lb.inp(f"b{i}_in_w", [D, 2 * C2], F32)
        if i > 0:
            E[f"b{i}_ow"] = lb.inp(f"b{i}_ow", [C2, D], F32)
        E[f"b{i}_cw_sl"] = lb.inp(f"b{i}_cw_sl", [nch, 4], F32)
        E[f"b{i}_cb_cols"] = lb.inp(f"b{i}_cb_cols", [128, 4], F32)
        E[f"b{i}_xw_sl"] = lb.inp(f"b{i}_xw_sl", [nch, 96], F32)
        E[f"b{i}_dtw_sl"] = lb.inp(f"b{i}_dtw_sl", [RK, nch], F32)
        E[f"b{i}_dtb_cols"] = lb.inp(f"b{i}_dtb_cols", [128, 4], F32)
        E[f"b{i}_alog_sl"] = lb.inp(f"b{i}_alog_sl", [nch, NST], F32)
        E[f"b{i}_dp_cols"] = lb.inp(f"b{i}_dp_cols", [128, 4], F32)
    E["fin_ow"] = lb.inp("fin_ow", [C2, D], F32)
    E["fin_w"] = lb.inp("fin_w", [D], F32)
    E["fin_b"] = lb.inp("fin_b", [D], F32)
    E["w1t"] = lb.inp("w1t", [D, H], F32)
    E["w2t"] = lb.inp("w2t", [H, D], F32)
    E["cw_cols"] = lb.inp("cw_cols", [128, 96], F32)
    E["cb_cols32"] = lb.inp("cb_cols32", [128, 32], F32)
    E["wdT"] = lb.inp("wdT", [D, 96], F32)
    E["wdtT"] = lb.inp("wdtT", [RK, D], F32)
    E["bdt_cols"] = lb.inp("bdt_cols", [128, 8], F32)
    E["l1_alog_sl"] = lb.inp("l1_alog_sl", [nch1, NST], F32)
    E["l1_dp_cols"] = lb.inp("l1_dp_cols", [128, 2], F32)
    for nm, shp in [("WvT", [D, D]), ("bv_cols", [128, 8]), ("WgT", [D, 8]),
                    ("bg8", [8, 1]), ("WpT", [D, 64]), ("W1T", [D, 256]),
                    ("b1_cols", [128, 2]), ("W2n", [256, D]), ("b2stk", [8, D]),
                    ("g_r8", [8, 1]), ("b_r8", [8, 1]), ("g_c8", [8, 1]),
                    ("b_c8", [8, 1]), ("alpha1", [1, 1]), ("blk8c", [64, 8]),
                    ("rep32c", [8, 256])]:
        E[nm] = lb.inp(nm, shp, F32)
    out_rows = lb.out("out_rows", [TPC, D], F32)

    d_xattnW = lb.dram("d_xattnW", [D, 17 * 128], F32)
    d_xattn = d_xattnW[:, 1:TPC + 1]
    d_xnq = lb.dram("d_xnq", [D, NR], BF16)
    d_qinv = lb.dram("d_qinv", [1, NR], F32)
    d_u = lb.dram("d_u", [C2, TPC], BF16)
    d_sz = lb.dram("d_sz", [C2, TPC], BF16)
    g_u = lb.dram("g_u", [4 * C2, TPC], BF16)
    d_uc = lb.dram("d_uc", [nch, S], BF16)
    d_dblp = lb.dram("d_dblp", [96, S], F32)
    g_dblp = lb.dram("g_dblp", [4 * 96, S], F32)
    d_y = lb.dram("d_y", [nch, S], BF16)
    g_y = lb.dram("g_y", [C2, S], BF16)
    d_attn = lb.dram("d_attn", [TPC, D], F32)
    d_ffn = lb.dram("d_ffn", [TPC, D], F32)
    d_x1 = lb.dram("d_x1", [TPC, D], F32)
    d_xn2 = lb.dram("d_xn2", [D, TPC], BF16)
    d_dbl2 = lb.dram("d_dbl2", [96, TPC], F32)
    d_dt2 = lb.dram("d_dt2", [D, TPC], BF16)
    g_dt2 = lb.dram("g_dt2", [4 * D, TPC], BF16)
    g_xn2 = lb.dram("g_xn2", [4 * D, TPC], BF16)
    g_dbl2 = lb.dram("g_dbl2", [4 * 96, TPC], F32)
    d_y2 = lb.dram("d_y2", [S, nch1], F32)
    g_y2 = lb.dram("g_y2", [4 * S, nch1], F32)
    l_u = lb.dram("l_u", [nch, S], BF16)
    l_y = lb.dram("l_y", [C2, TPC], BF16)
    l_dt2 = lb.dram("l_dt2", [nch1, S], BF16)
    l_xn2 = lb.dram("l_xn2", [nch1, S], BF16)
    l_bc2 = lb.dram("l_bc2", [32, S], F32)
    l_y2 = lb.dram("l_y2", [TPC, D], F32)

    def gather(ins, outs):
        for i_ap, o_ap in zip(ins, outs):
            def fn(Eg, i_ap=i_ap, o_ap=o_ap):
                return Eg.collective_compute(
                    "AllGather", ALU.bypass, replica_groups=GRPS,
                    ins=[i_ap.opt()], outs=[o_ap.opt()])
            lb.op("gpsimd", fn, r=[i_ap], w=[o_ap], dma="coll", dma_stream="cc")

    def dyncopy_rows(dst, src, RR, gmul, roff=0):
        nrows = dst.shape[0]
        CT = src.shape[1]
        nr = src.shape[0] // RR
        s3 = src.rearrange("(r c) t -> c r t", c=RR)
        d3 = dst.rearrange("c (r t) -> c r t", r=nr)

        def fn(Ed, s3=s3, d3=d3, gmul=gmul, roff=roff, nrows=nrows):
            if gmul:
                sl = s3[bass.DynSlice(lb.dynmul(Ed, gmul) + roff if roff else
                                      lb.dynmul(Ed, gmul), nrows)]
            else:
                sl = s3[roff:roff + nrows]
            return Ed.dma_start(out=d3, in_=sl)
        lb.op("sync", fn, r=[src], w=[dst], dma=True, dma_stream=_key(dst))

    def dyncopy_cols(dst, src):
        def fn(Ed, src=src, dst=dst):
            return Ed.dma_start(
                out=dst, in_=src[:, bass.DynSlice(lb.dynmul(Ed, TPC), TPC)])
        lb.op("sync", fn, r=[src], w=[dst], dma=True, dma_stream=_key(dst))

    def dyncopy_y2(dst, src):
        BC = src.shape[1]
        s3 = src.rearrange("(b s) c -> s b c", s=S)
        d3 = dst.rearrange("t (b c) -> t b c", c=BC)

        def fn(Ed, s3=s3, d3=d3):
            return Ed.dma_start(
                out=d3, in_=s3[bass.DynSlice(lb.dynmul(Ed, TPC), TPC)])
        lb.op("sync", fn, r=[src], w=[dst], dma=True, dma_stream=_key(dst))

    lb.stage_open()
    emit_ln1(lb, IO(lb, dict(x_rows=E["x_rows"], ident_f32=E["idf"],
                             ident_bf16=E["idb"], xattn_fm=d_xattnW,
                             xnq_fm=d_xnq, qinv=d_qinv)), window=True)
    lb.stage_close()

    for i in range(4):
        p = dict(xattn=d_xattn, norm_w=E[f"b{i}_norm_w"], norm_b=E[f"b{i}_norm_b"],
                 in_w=E[f"b{i}_in_w"], u_cols=d_u, siluz_cols=d_sz)
        if i > 0:
            p["y_prev"] = l_y
            p["sz_prev"] = d_sz
            p["ow"] = E[f"b{i}_ow"]
            p["xattn_new"] = d_xattn
        emit_proj(lb, IO(lb, p), has_outproj=(i > 0))
        lb.stage_close()
        gather([d_u], [g_u])
        dyncopy_rows(l_u, g_u, C2, nch)
        emit_convx(lb, IO(lb, dict(
            u_sl=l_u,
            cw_sl=E[f"b{i}_cw_sl"], cb_cols=E[f"b{i}_cb_cols"],
            xw_sl=E[f"b{i}_xw_sl"], uc_sl=d_uc, dblp=d_dblp)))
        lb.stage_close()
        gather([d_dblp], [g_dblp])
        emit_scan_l0(lb, IO(lb, dict(
            dblp0=g_dblp[0:96], dblp1=g_dblp[96:192],
            dblp2=g_dblp[192:288], dblp3=g_dblp[288:384],
            uc_sl=d_uc,
            dtw_sl=E[f"b{i}_dtw_sl"], dtb_cols=E[f"b{i}_dtb_cols"],
            alog_sl=E[f"b{i}_alog_sl"], dp_cols=E[f"b{i}_dp_cols"],
            y_sl=d_y)))
        lb.stage_close()
        gather([d_y], [g_y])
        dyncopy_cols(l_y, g_y)

    emit_proj(lb, IO(lb, dict(
        xattn=d_xattn, y_prev=l_y, sz_prev=d_sz, ow=E["fin_ow"],
        xattn_new=d_xattn, norm_w=E["fin_w"], norm_b=E["fin_b"],
        ident_f32=E["idf"], attn_tm=d_attn)),
        has_outproj=True, has_inproj=False, final_ln=True)
    lb.stage_close()

    emit_ffn0(lb, IO(lb, dict(
        xnq_cols=d_xnq[:, 0:TPC + 2], qinv_cols=d_qinv[:, 0:TPC + 2],
        w1t=E["w1t"], w2t=E["w2t"], cw_cols=E["cw_cols"],
        cb_cols=E["cb_cols32"], ident_f32=E["idf"], ffn_tm=d_ffn)))
    lb.stage_close()

    emit_mid(lb, IO(lb, dict(
        x_rows=E["x_rows"][1:TPC + 1, :], attn_rows=d_attn, ffn_rows=d_ffn,
        wdT=E["wdT"], wdtT=E["wdtT"], bdt_cols=E["bdt_cols"],
        ident_bf16=E["idb"], x1_rows=d_x1, xn2_fm=d_xn2, dbl2=d_dbl2,
        dt2=d_dt2)))
    lb.stage_close()
    gather([d_dt2, d_xn2], [g_dt2, g_xn2])
    gather([d_dbl2], [g_dbl2])
    dyncopy_rows(l_dt2, g_dt2, D, nch1)
    dyncopy_rows(l_xn2, g_xn2, D, nch1)
    dyncopy_rows(l_bc2, g_dbl2, 96, 0, roff=RK)

    emit_scan_l1(lb, IO(lb, dict(
        dt_sl=l_dt2,
        x_sl=l_xn2,
        bc2=l_bc2,
        alog_sl=E["l1_alog_sl"], dp_cols=E["l1_dp_cols"],
        ident_f32=E["idf"], y_tm=d_y2)))
    lb.stage_close()
    gather([d_y2], [g_y2])
    dyncopy_y2(l_y2, g_y2)

    emit_l1ffn(lb, IO(lb, dict(
        xn2_cols=d_xn2, x1_rows=d_x1,
        y2_rows=l_y2,
        WvT=E["WvT"], bv_cols=E["bv_cols"], WgT=E["WgT"], bg8=E["bg8"],
        WpT=E["WpT"], W1T=E["W1T"], b1_cols=E["b1_cols"], W2n=E["W2n"],
        b2stk=E["b2stk"], g_r8=E["g_r8"], b_r8=E["b_r8"], g_c8=E["g_c8"],
        b_c8=E["b_c8"], alpha1=E["alpha1"], blk8c=E["blk8c"],
        rep32c=E["rep32c"], ident_f32=E["idf"], out_rows=out_rows)))

    lb.barrier_op("sync", lambda Eb: Eb.nop())
    return lb.finalize()


def kernel(**inputs):
    inp = {k: np.asarray(v) for k, v in inputs.items()}
    x = f32(inp["L0_attn_x"]).reshape(B, S, D)
    NR = 17 * 128
    nch = C2 // 4
    nch1 = D // 4
    nc = _get("fused", build_fused)
    blk8c = np.zeros((64, 8), np.float32)
    for e in range(8):
        blk8c[e * 8:(e + 1) * 8, e] = 1.0
    rep32c = np.zeros((8, 256), np.float32)
    for e in range(8):
        rep32c[e, e * 32:(e + 1) * 32] = 1.0
    W2n = f32(inp["L1_ffn_W2"]).transpose(0, 2, 1).reshape(256, D)
    cwc = f32(inp["L0_ffn_conv_w"][:, 0, :]).reshape(32, 128, 3).transpose(1, 0, 2).reshape(128, 96)
    im = []
    for c in range(NCORES):
        q, g = divmod(c, 4)
        xw = np.zeros((NR, D), np.float32)
        xw[1:TPC + 1] = x[q, g * TPC:(g + 1) * TPC]
        if g > 0:
            xw[0] = x[q, g * TPC - 1]
        if g < 3:
            xw[TPC + 1] = x[q, (g + 1) * TPC]
        sl = slice(g * nch, (g + 1) * nch)
        sl1 = slice(g * nch1, (g + 1) * nch1)
        d = dict(x_rows=xw, ident_f32=IDF_, ident_bf16=IDB_,
                 fin_ow=f32(inp["L0_attn_out_proj_w"][3]),
                 fin_w=f32(inp["L0_attn_fn_w"]), fin_b=f32(inp["L0_attn_fn_b"]),
                 w1t=f32(inp["L0_ffn_w1"].T), w2t=f32(inp["L0_ffn_w2"].T),
                 cw_cols=np.ascontiguousarray(cwc),
                 cb_cols32=_cols(inp["L0_ffn_conv_b"], 32),
                 wdT=f32(inp["L1_attn_W_deltaBC"].T),
                 wdtT=f32(inp["L1_attn_W_dt"].T),
                 bdt_cols=_cols(inp["L1_attn_b_dt"], 8),
                 l1_alog_sl=f32(inp["L1_attn_A_log"][sl1]),
                 l1_dp_cols=_cols(inp["L1_attn_D_param"][sl1], 2),
                 WvT=f32(inp["L1_ffn_Wv"].T),
                 bv_cols=_cols(inp["L1_ffn_bv"], 8),
                 WgT=f32(inp["L1_ffn_Wg"].T),
                 bg8=f32(inp["L1_ffn_bg"]).reshape(8, 1),
                 WpT=np.ascontiguousarray(f32(inp["L1_ffn_Wp"]).reshape(64, D).T),
                 W1T=np.ascontiguousarray(f32(inp["L1_ffn_W1"]).reshape(256, D).T),
                 b1_cols=_cols(inp["L1_ffn_b1"].reshape(-1), 2),
                 W2n=np.ascontiguousarray(W2n),
                 b2stk=f32(inp["L1_ffn_b2"]),
                 g_r8=f32(inp["L1_ffn_g_r"]).reshape(8, 1),
                 b_r8=f32(inp["L1_ffn_b_r"]).reshape(8, 1),
                 g_c8=f32(inp["L1_ffn_g_c"]).reshape(8, 1),
                 b_c8=f32(inp["L1_ffn_b_c"]).reshape(8, 1),
                 alpha1=f32(inp["L1_ffn_alpha"]).reshape(1, 1),
                 blk8c=blk8c, rep32c=rep32c)
        for i in range(4):
            d[f"b{i}_norm_w"] = f32(inp["L0_attn_norm_w"][i])
            d[f"b{i}_norm_b"] = f32(inp["L0_attn_norm_b"][i])
            d[f"b{i}_in_w"] = f32(inp["L0_attn_in_proj_w"][i])
            if i > 0:
                d[f"b{i}_ow"] = f32(inp["L0_attn_out_proj_w"][i - 1])
            d[f"b{i}_cw_sl"] = f32(inp["L0_attn_conv_w"][i][sl])
            d[f"b{i}_cb_cols"] = _cols(inp["L0_attn_conv_b"][i][sl], 4)
            d[f"b{i}_xw_sl"] = f32(inp["L0_attn_xproj_w"][i][sl])
            d[f"b{i}_dtw_sl"] = f32(inp["L0_attn_dtproj_w"][i][:, sl])
            d[f"b{i}_dtb_cols"] = _cols(inp["L0_attn_dtproj_b"][i][sl], 4)
            d[f"b{i}_alog_sl"] = f32(inp["L0_attn_A_log"][i][sl])
            d[f"b{i}_dp_cols"] = _cols(inp["L0_attn_skip_D"][i][sl], 4)
        im.append(d)
    res = _run(nc, im)
    OUT = np.concatenate([r["out_rows"] for r in res], axis=0)
    return OUT.reshape(B, S, D).astype(np.float32)


_cache = {}


def _get(name, builder):
    if name not in _cache:
        _cache[name] = builder()
    return _cache[name]


_DBG = {}
IDF_ = np.eye(128, dtype=np.float32)
IDB_ = np.eye(128, dtype=np.float32).astype(BF)


def _run(nc, in_maps):
    return run_bass_kernel_spmd(nc, in_maps, core_ids=list(range(NCORES))).results


def _chsl(c, nch):
    q, g = divmod(c, 4)
    return q, slice(g * nch, (g + 1) * nch)


def _cols(vec, nct):
    return np.ascontiguousarray(np.asarray(vec, np.float32).reshape(nct, 128).T)


def kernel_unfused(**inputs):
    inp = {k: np.asarray(v) for k, v in inputs.items()}
    x = f32(inp["L0_attn_x"]).reshape(NT, D)

    nc = _get("ln1", build_ln1)
    res = _run(nc, [dict(x_rows=np.ascontiguousarray(x[c * TPC:(c + 1) * TPC]),
                         ident_f32=IDF_, ident_bf16=IDB_) for c in range(NCORES)])
    XATTN = np.concatenate([r["xattn_fm"] for r in res], axis=1)
    XNQ = np.concatenate([r["xnq_fm"] for r in res], axis=1)
    QINV = np.concatenate([r["qinv"][0] for r in res])

    nch = C2 // 4
    M = None
    for i in range(4):
        key = "proj0" if i == 0 else "projN"
        nc = _get(key, lambda: build_proj(has_outproj=(i > 0)))
        im = []
        for c in range(NCORES):
            d = dict(xattn=np.ascontiguousarray(XATTN[:, c * TPC:(c + 1) * TPC]),
                     norm_w=f32(inp["L0_attn_norm_w"][i]),
                     norm_b=f32(inp["L0_attn_norm_b"][i]),
                     in_w=f32(inp["L0_attn_in_proj_w"][i]))
            if i > 0:
                d["m_prev"] = np.ascontiguousarray(M[:, c * TPC:(c + 1) * TPC])
                d["ow"] = f32(inp["L0_attn_out_proj_w"][i - 1])
            im.append(d)
        res = _run(nc, im)
        if i > 0:
            XATTN = np.concatenate([r["xattn_new"] for r in res], axis=1)
        U = np.concatenate([r["u_cols"] for r in res], axis=1)
        SZ = np.concatenate([r["siluz_cols"] for r in res], axis=1)

        nc = _get("convx", build_convx)
        im = []
        for c in range(NCORES):
            q, sl = _chsl(c, nch)
            im.append(dict(u_sl=np.ascontiguousarray(U[sl, q * S:(q + 1) * S]),
                           cw_sl=f32(inp["L0_attn_conv_w"][i][sl]),
                           cb_cols=_cols(inp["L0_attn_conv_b"][i][sl], 4),
                           xw_sl=f32(inp["L0_attn_xproj_w"][i][sl])))
        res = _run(nc, im)
        UC = [r["uc_sl"] for r in res]
        DBLP = [r["dblp"] for r in res]

        nc = _get("scan_l0", build_scan_l0)
        im = []
        for c in range(NCORES):
            q, sl = _chsl(c, nch)
            im.append(dict(dblp0=DBLP[q * 4 + 0], dblp1=DBLP[q * 4 + 1],
                           dblp2=DBLP[q * 4 + 2], dblp3=DBLP[q * 4 + 3],
                           uc_sl=UC[c],
                           sz_sl=np.ascontiguousarray(SZ[sl, q * S:(q + 1) * S]),
                           dtw_sl=f32(inp["L0_attn_dtproj_w"][i][:, sl]),
                           dtb_cols=_cols(inp["L0_attn_dtproj_b"][i][sl], 4),
                           alog_sl=f32(inp["L0_attn_A_log"][i][sl]),
                           dp_cols=_cols(inp["L0_attn_skip_D"][i][sl], 4)))
        res = _run(nc, im)
        M = np.empty((C2, NT), dtype=BF)
        for c in range(NCORES):
            q, sl = _chsl(c, nch)
            M[sl, q * S:(q + 1) * S] = res[c]["m_sl"]

    nc = _get("attnfin", lambda: build_proj(has_outproj=True, has_inproj=False, final_ln=True))
    im = [dict(xattn=np.ascontiguousarray(XATTN[:, c * TPC:(c + 1) * TPC]),
               m_prev=np.ascontiguousarray(M[:, c * TPC:(c + 1) * TPC]),
               ow=f32(inp["L0_attn_out_proj_w"][3]),
               norm_w=f32(inp["L0_attn_fn_w"]), norm_b=f32(inp["L0_attn_fn_b"]),
               ident_f32=IDF_) for c in range(NCORES)]
    res = _run(nc, im)
    ATTN = np.concatenate([r["attn_tm"] for r in res], axis=0)
    _DBG["ATTN"] = ATTN; _DBG["XATTN4"] = XATTN; _DBG["M3"] = M

    XNQp = np.pad(XNQ.reshape(D, B, S), ((0, 0), (0, 0), (1, 1)))
    QIp = np.pad(QINV.reshape(B, S), ((0, 0), (1, 1))).astype(np.float32)
    nc = _get("ffn0", build_ffn0)
    im = []
    cwc = f32(inp["L0_ffn_conv_w"][:, 0, :]).reshape(32, 128, 3).transpose(1, 0, 2).reshape(128, 96)
    for c in range(NCORES):
        q, g = divmod(c, 4)
        sl = slice(g * TPC, g * TPC + TPC + 2)
        im.append(dict(xnq_cols=np.ascontiguousarray(XNQp[:, q, sl]),
                       qinv_cols=np.ascontiguousarray(QIp[q, sl][None, :]),
                       w1t=f32(inp["L0_ffn_w1"].T), w2t=f32(inp["L0_ffn_w2"].T),
                       cw_cols=np.ascontiguousarray(cwc),
                       cb_cols=_cols(inp["L0_ffn_conv_b"], 32),
                       ident_f32=IDF_))
    res = _run(nc, im)
    FFN = np.concatenate([r["ffn_tm"] for r in res], axis=0)
    _DBG["FFN"] = FFN

    nc = _get("mid", build_mid)
    im = [dict(x_rows=np.ascontiguousarray(x[c * TPC:(c + 1) * TPC]),
               attn_rows=np.ascontiguousarray(ATTN[c * TPC:(c + 1) * TPC]),
               ffn_rows=np.ascontiguousarray(FFN[c * TPC:(c + 1) * TPC]),
               wdT=f32(inp["L1_attn_W_deltaBC"].T),
               wdtT=f32(inp["L1_attn_W_dt"].T),
               bdt_cols=_cols(inp["L1_attn_b_dt"], 8),
               ident_bf16=IDB_) for c in range(NCORES)]
    res = _run(nc, im)
    X1 = np.concatenate([r["x1_rows"] for r in res], axis=0)
    _DBG["X1"] = X1
    XN2 = np.concatenate([r["xn2_fm"] for r in res], axis=1)
    DBL2 = np.concatenate([r["dbl2"] for r in res], axis=1)
    DT2 = np.concatenate([r["dt2"] for r in res], axis=1)

    nch1 = D // 4
    nc = _get("scan_l1", build_scan_l1)
    im = []
    for c in range(NCORES):
        q, sl = _chsl(c, nch1)
        im.append(dict(dt_sl=np.ascontiguousarray(DT2[sl, q * S:(q + 1) * S]),
                       x_sl=np.ascontiguousarray(XN2[sl, q * S:(q + 1) * S]),
                       bc2=np.ascontiguousarray(DBL2[RK:RK + 32, q * S:(q + 1) * S]),
                       alog_sl=f32(inp["L1_attn_A_log"][sl]),
                       dp_cols=_cols(inp["L1_attn_D_param"][sl], 2),
                       ident_f32=IDF_))
    res = _run(nc, im)
    _DBG["DT2"] = DT2; _DBG["DBL2"] = DBL2
    Y2 = np.concatenate(
        [np.concatenate([res[q * 4 + g]["y_tm"] for g in range(4)], axis=1) for q in range(B)],
        axis=0)

    nc = _get("l1ffn", build_l1ffn)
    W2n = f32(inp["L1_ffn_W2"]).transpose(0, 2, 1).reshape(256, D)
    blk8c = np.zeros((64, 8), np.float32)
    for e in range(8):
        blk8c[e * 8:(e + 1) * 8, e] = 1.0
    rep32c = np.zeros((8, 256), np.float32)
    for e in range(8):
        rep32c[e, e * 32:(e + 1) * 32] = 1.0
    im = [dict(xn2_cols=np.ascontiguousarray(XN2[:, c * TPC:(c + 1) * TPC]),
               x1_rows=np.ascontiguousarray(X1[c * TPC:(c + 1) * TPC]),
               y2_rows=np.ascontiguousarray(Y2[c * TPC:(c + 1) * TPC]),
               WvT=f32(inp["L1_ffn_Wv"].T),
               bv_cols=_cols(inp["L1_ffn_bv"], 8),
               WgT=f32(inp["L1_ffn_Wg"].T),
               bg8=f32(inp["L1_ffn_bg"]).reshape(8, 1),
               WpT=f32(inp["L1_ffn_Wp"]).reshape(64, D).T.copy(),
               W1T=f32(inp["L1_ffn_W1"]).reshape(256, D).T.copy(),
               b1_cols=_cols(inp["L1_ffn_b1"].reshape(-1), 2),
               W2n=np.ascontiguousarray(W2n),
               b2stk=f32(inp["L1_ffn_b2"]),
               g_r8=f32(inp["L1_ffn_g_r"]).reshape(8, 1),
               b_r8=f32(inp["L1_ffn_b_r"]).reshape(8, 1),
               g_c8=f32(inp["L1_ffn_g_c"]).reshape(8, 1),
               b_c8=f32(inp["L1_ffn_b_c"]).reshape(8, 1),
               alpha1=f32(inp["L1_ffn_alpha"]).reshape(1, 1),
               blk8c=blk8c, rep32c=rep32c,
               ident_f32=IDF_) for c in range(NCORES)]
    res = _run(nc, im)
    OUT = np.concatenate([r["out_rows"] for r in res], axis=0)
    _DBG["Y2"] = Y2; _DBG["OUT"] = OUT
    return OUT.reshape(B, S, D).astype(np.float32)
